# version 03
# baseline (speedup 1.0000x reference)
import contextlib
import math

import numpy as np

import concourse.bass as bass
import concourse.mybir as mybir
from concourse.bass_utils import run_bass_kernel_spmd

F32 = mybir.dt.float32
BF16 = mybir.dt.bfloat16
I8 = mybir.dt.int8
AF = mybir.ActivationFunctionType
AL = mybir.AluOpType
AX = mybir.AxisListType.X

LN_EPS = 1e-5


class Cfg:
    def __init__(self, D=4096, DC=512, DC1=1536, DR=64, H=32, DH=128,
                 SK=8192, NQ=2048, FH=11008, CH=512, QC=512, FQC=512,
                 n_phases=99, debug=False):
        self.D, self.DC, self.DC1, self.DR = D, DC, DC1, DR
        self.H, self.DH, self.SK, self.NQ, self.FH = H, DH, SK, NQ, FH
        self.CH, self.QC, self.FQC = CH, QC, FQC
        self.n_phases = n_phases
        self.debug = debug
        assert D % 128 == 0 and DC % 128 == 0 and DC1 % 128 == 0
        assert DR == 64 and DH == 128 and H * DH == D
        assert SK % CH == 0 and NQ % CH == 0 and CH % 128 == 0
        assert NQ % QC == 0 and QC % 128 == 0 and QC <= 512
        assert FH % 128 == 0 and NQ % FQC == 0 and FQC <= 512
        self.FB = D // 128
        self.CB = DC // 128
        self.C1B = DC1 // 128
        self.RB = (H * DR) // 128
        self.FHB = FH // 128
        self.NCH = SK // CH
        self.NQCH = NQ // CH
        self.TPC = CH // 128
        self.SKB = SK // 128
        self.NQC = NQ // QC
        self.NQB = NQ // 128
        self.SCALER = 1.0 / math.sqrt(DH + DR)


class Sem:
    def __init__(self, nc, es, name):
        self.h = es.enter_context(nc.semaphore(name))
        self.n = 0

    def inc(self, ins, k=1):
        ins.then_inc(self.h, k)
        self.n += k
        return ins


class St:
    pass


def _mk_sems(st, names):
    for nm in names:
        if nm not in st.sems:
            st.sems[nm] = Sem(st.nc, st.es, nm)
    return [st.sems[nm] for nm in names]


def phase_barrier(st):
    blk, bar = st.blk, st.sems["bar"]
    dS, dA, dG = st.sems["dS"], st.sems["dA"], st.sems["dG"]
    target = bar.n + 5
    flushes = {"sync": [(dS.h, dS.n)] + [(s.h, s.n) for s in st.spool],
               "scalar": [(dA.h, dA.n)] + [(s.h, s.n) for s in st.apool],
               "gpsimd": [(dG.h, dG.n)] + [(s.h, s.n) for s in st.gpool]}

    @blk.sync
    def _(e):
        for h, n in flushes["sync"]:
            e.wait_ge(h, n)
        bar.inc(e.nop())
        e.wait_ge(bar.h, target)

    @blk.scalar
    def _(e):
        for h, n in flushes["scalar"]:
            e.wait_ge(h, n)
        bar.inc(e.memzero(st.bjunk[0:1, 1:2]))
        e.wait_ge(bar.h, target)

    @blk.gpsimd
    def _(e):
        for h, n in flushes["gpsimd"]:
            e.wait_ge(h, n)
        bar.inc(e.nop())
        e.wait_ge(bar.h, target)

    @blk.vector
    def _(e):
        bar.inc(e.memset(st.bjunk[0:1, 0:1], 0.0))
        e.wait_ge(bar.h, target)

    @blk.tensor
    def _(e):
        bar.inc(e.nop())
        e.wait_ge(bar.h, target)


def phase_p0(st):
    c, nc, blk = st.cfg, st.nc, st.blk
    dS, dG, tQ, sQ = _mk_sems(st, ["dS", "dG", "p0t", "p0s"])

    with contextlib.ExitStack() as ph:
        bstg = ph.enter_context(nc.sbuf_tensor("bstg", [max(c.FB, c.C1B, c.RB), 128], F32))
        pT0 = ph.enter_context(nc.psum_tensor("pT0", [128, 128], F32))

        @blk.sync
        def _(e):
            dS.inc(e.dma_start(out=st.identf_sb[:, :], in_=st.ident_f[:, :]), 16)
            dS.inc(e.dma_start(out=st.identb[:, :], in_=st.ident_b[:, :]), 16)
            dS.inc(e.dma_start(out=st.ones1f[:, :], in_=st.ones_col_f[:, :]), 16)
            dS.inc(e.dma_start(out=st.ones32f[:, :], in_=st.ones32_f[:, :]), 16)
            dS.inc(e.dma_start(out=st.ones128f[:, :], in_=st.ones128_f[:, :]), 16)
            dS.inc(e.dma_start(out=st.ones1b[:, :], in_=st.ones_col_b[:, :]), 16)
            dS.inc(e.dma_start(out=st.ones128b[:, :], in_=st.ones128_b[:, :]), 16)
            dS.inc(e.dma_start(out=st.ones33b[:, :], in_=st.ones33_b[:, :]), 16)
            dS.inc(e.dma_start(out=st.e34b[:, :], in_=st.e34_t[:, :]), 16)

        @blk.gpsimd
        def _(e):
            dG.inc(e.dma_start(out=st.brow[0:1, :], in_=st.io["b_uv"][None, :]), 16)
            dG.inc(e.dma_start(out=st.brow[32:33, :], in_=st.io["b_out"][None, :]), 16)

        @blk.vector
        def _(e):
            e.memset(st.epsb[:, :], LN_EPS)

        bias_jobs = [("b_dq", c.C1B, st.bT_dq), ("b_dkv", c.CB, st.bT_dkv),
                     ("b_uk", c.FB, st.bT_uk), ("b_uq", c.FB, st.bT_uq),
                     ("b_rq", c.RB, st.bT_rq)]
        for name, nb, bdst in bias_jobs:
            s_before = sQ.n

            @blk.sync
            def _(e, name=name, nb=nb, s_before=s_before):
                e.wait_ge(sQ.h, s_before)
                dS.inc(e.dma_start(
                    out=bstg[0:nb, :],
                    in_=st.io[name].rearrange("(nb p) -> nb p", p=128)), 16)

            d_need, s_need = dS.n, sQ.n

            @blk.tensor
            def _(e, nb=nb, d_need=d_need, s_need=s_need):
                e.wait_ge(dS.h, d_need)
                e.wait_ge(sQ.h, s_need)
                tQ.inc(e.matmul(pT0[:, 0:nb], bstg[0:nb, :],
                                st.identf_sb[0:nb, 0:nb], start=True, stop=True))

            t_need = tQ.n

            @blk.scalar
            def _(e, nb=nb, bdst=bdst, t_need=t_need):
                e.wait_ge(tQ.h, t_need)
                sQ.inc(e.copy(out=bdst[:, 0:nb], in_=pT0[:, 0:nb]))

        s_before = sQ.n

        @blk.sync
        def _(e, s_before=s_before):
            e.wait_ge(sQ.h, s_before)
            dS.inc(e.dma_start(out=bstg[0:1, 0:64], in_=st.io["b_rk"][None, :]), 16)

        d_need, s_need = dS.n, sQ.n

        @blk.tensor
        def _(e, d_need=d_need, s_need=s_need):
            e.wait_ge(dS.h, d_need)
            e.wait_ge(sQ.h, s_need)
            tQ.inc(e.matmul(pT0[0:64, 0:1], bstg[0:1, 0:64],
                            st.identf_sb[0:1, 0:1], start=True, stop=True))

        t_need = tQ.n

        @blk.scalar
        def _(e, t_need=t_need):
            e.wait_ge(tQ.h, t_need)
            sQ.inc(e.copy(out=st.bT_rk[:, :], in_=pT0[0:64, 0:1]))

        phase_barrier(st)


def phase_a1(st):
    c, nc, blk = st.cfg, st.nc, st.blk
    dS, dG = st.sems["dS"], st.sems["dG"]
    (aST, aSQ, aAP, aTP, aTE, aMM, aCE, aKE, vCH) = _mk_sems(
        st, ["a1stat", "a1sqrt", "a1apply", "a1tpose", "a1tpev",
             "a1mm", "a1cev", "a1krev", "a1vch"])
    NG = c.D // 512
    NT = c.NCH * c.TPC
    GPT = c.FB // 4

    with contextlib.ExitStack() as ph:
        w_dkv_b = ph.enter_context(nc.sbuf_tensor("w_dkv_b", [128, c.FB, c.DC], BF16))
        w_rk_b = ph.enter_context(nc.sbuf_tensor("w_rk_b", [128, c.FB, c.DR], BF16))
        xs = ph.enter_context(nc.sbuf_tensor("xs", [128, 2, c.D], F32))
        xlb = ph.enter_context(nc.sbuf_tensor("xlb", [128, 2, c.D], BF16))
        xlT = ph.enter_context(nc.sbuf_tensor("xlT", [128, c.FB, c.CH], BF16))
        stat = ph.enter_context(nc.sbuf_tensor("stat", [128, 8], F32))
        mv6 = ph.enter_context(nc.sbuf_tensor("mv6", [128, NG, 6], F32))
        pT1 = ph.enter_context(nc.psum_tensor("pT1", [128, 2, 512], F32))
        pM1 = ph.enter_context(nc.psum_tensor("pM1", [128, 2, 512], F32))
        pK1 = ph.enter_context(nc.psum_tensor("pK1", [64, 512], F32))

        d_w0 = dG.n

        @blk.gpsimd
        def _(e):
            dG.inc(e.dma_start(out=w_dkv_b[:, :, :],
                               in_=st.io["w_dkv"].rearrange("(fb p) k -> p fb k", p=128)), 16)
            dG.inc(e.dma_start(out=w_rk_b[:, :, :],
                               in_=st.io["w_rk"].rearrange("(fb p) k -> p fb k", p=128)), 16)
            if c.debug:
                pass

        d_wready = dG.n
        d_x0 = dS.n

        def spills_done(ci):
            return d_wready + 16 * (min(ci + 1, c.NQCH))

        xb0, xb1 = st.spool[0].n, st.spool[1].n

        @blk.sync
        def _(e):
            for g in range(NT):
                if g >= 2:
                    e.wait_ge(aAP.h, g - 1)
                st.spool[g % 2].inc(e.dma_start(
                    out=xs[:, g % 2, :],
                    in_=st.io["x"][g * 128:(g + 1) * 128, :]), 16)

        def xload_done(g):
            return (xb0 if g % 2 == 0 else xb1) + 16 * (g // 2 + 1)

        @blk.vector
        def _(e):
            for ci in range(c.NCH):
                for t in range(c.TPC):
                    g = ci * c.TPC + t
                    e.wait_ge(st.spool[g % 2].h, xload_done(g))
                    for ng in range(NG):
                        ins = e.bn_stats(out=mv6[:, ng, :],
                                         in_=xs[:, g % 2, ng * 512:(ng + 1) * 512])
                    vCH.inc(ins)
                    e.wait_ge(vCH.h, vCH.n)
                    aST.inc(e.bn_aggr(out=stat[:, 0:2], in_=mv6[:, :, :]))
                    e.wait_ge(aSQ.h, g + 1)
                    vCH.inc(e.reciprocal(out=stat[:, 3:4], in_=stat[:, 2:3]))
                    e.wait_ge(vCH.h, vCH.n)
                    vCH.inc(e.tensor_scalar(out=stat[:, 4:5], in0=stat[:, 0:1],
                                    scalar1=stat[:, 3:4], scalar2=-1.0,
                                    op0=AL.mult, op1=AL.mult))
                    e.wait_ge(vCH.h, vCH.n)
                    if g >= 2:
                        e.wait_ge(aTP.h, (g - 1) * GPT)
                    aAP.inc(e.tensor_scalar(out=xlb[:, g % 2, :], in0=xs[:, g % 2, :],
                                            scalar1=stat[:, 4:5], scalar2=stat[:, 3:4],
                                            op0=AL.add, op1=AL.mult))
                for cb in range(c.CB):
                    e.wait_ge(aMM.h, ci * (c.CB + 1) + cb + 1)
                    aCE.inc(e.tensor_scalar(
                        out=st.CKVT[:, cb, ci * c.CH:(ci + 1) * c.CH],
                        in0=pM1[:, cb % 2, :], scalar1=st.bT_dkv[:, cb:cb + 1],
                        scalar2=None, op0=AL.add))
                e.wait_ge(aMM.h, (ci + 1) * (c.CB + 1))
                aKE.inc(e.tensor_scalar(
                    out=st.KRR[0:64, ci * c.CH:(ci + 1) * c.CH],
                    in0=pK1[:, :], scalar1=st.bT_rk[:, 0:1],
                    scalar2=None, op0=AL.add))
                if ci == c.NCH - 1:
                    aKE.inc(e.memset(st.KRR[64:65, :], 1.0))

        @blk.scalar
        def _(e):
            for ci in range(c.NCH):
                for t in range(c.TPC):
                    g = ci * c.TPC + t
                    e.wait_ge(aST.h, g + 1)
                    aSQ.inc(e.activation(out=stat[:, 2:3], in_=stat[:, 1:2],
                                         func=AF.Sqrt, bias=st.epsb[:, 0:1], scale=1.0))
                    for j in range(GPT):
                        gidx = g * GPT + j
                        e.wait_ge(aTP.h, gidx + 1)
                        if t == 0 and j == 0 and ci >= 1:
                            e.wait_ge(aMM.h, ci * (c.CB + 1))
                            if ci - 1 < c.NQCH:
                                e.wait_ge(dG.h, spills_done(ci - 1))
                        aTE.inc(e.copy(
                            out=xlT[:, 4 * j:4 * j + 4, t * 128:(t + 1) * 128],
                            in_=pT1[:, gidx % 2, :].rearrange("p (b t) -> p b t", b=4)))

        @blk.tensor
        def _(e):
            e.wait_ge(dG.h, d_wready)
            for ci in range(c.NCH):
                for t in range(c.TPC):
                    g = ci * c.TPC + t
                    e.wait_ge(aAP.h, g + 1)
                    for j in range(GPT):
                        gidx = g * GPT + j
                        if gidx >= 2:
                            e.wait_ge(aTE.h, gidx - 1)
                        for q in range(4):
                            fb = 4 * j + q
                            tQ_ins = e.matmul(
                                pT1[:, gidx % 2, q * 128:(q + 1) * 128],
                                xlb[:, g % 2, fb * 128:(fb + 1) * 128],
                                st.identb[:, :], start=True, stop=True)
                            if q == 3:
                                aTP.inc(tQ_ins)
                e.wait_ge(aTE.h, (ci + 1) * c.TPC * GPT)
                for cb in range(c.CB):
                    mmi = ci * (c.CB + 1) + cb
                    if mmi >= 2:
                        e.wait_ge(aCE.h, ci * c.CB + cb - 1)
                    for fb in range(c.FB):
                        ins = e.matmul(pM1[:, cb % 2, :],
                                       w_dkv_b[:, fb, cb * 128:(cb + 1) * 128],
                                       xlT[:, fb, :],
                                       start=(fb == 0), stop=(fb == c.FB - 1))
                    aMM.inc(ins)
                if ci >= 1:
                    e.wait_ge(aKE.h, ci)
                for fb in range(c.FB):
                    ins = e.matmul(pK1[:, :], w_rk_b[:, fb, :], xlT[:, fb, :],
                                   start=(fb == 0), stop=(fb == c.FB - 1))
                aMM.inc(ins)

        import os as _osa

        @blk.gpsimd
        def _(e):
            for ci in range(c.NQCH):
                e.wait_ge(aTE.h, (ci + 1) * c.TPC * GPT)
                if not _osa.environ.get("K_A1_NOSPILL", ""):
                    dG.inc(e.dma_start(
                        out=st.XLT[:, ci * c.CH:(ci + 1) * c.CH].rearrange(
                            "(fb p) t -> p fb t", p=128),
                        in_=xlT[:, :, :]), 16)

        phase_barrier(st)

        if c.debug:
            _dump_debug(st, [("dbg_ckvt", st.CKVT, [128, c.CB, c.SK], BF16),
                             ("dbg_krr", st.KRR, [65, c.SK], BF16)])


def _dump_debug(st, items):
    c, nc, blk = st.cfg, st.nc, st.blk
    dG = st.sems["dG"]
    phase_barrier(st)

    @blk.gpsimd
    def _(e):
        for name, sbten, shape, dt in items:
            dten = nc.dram_tensor(name, shape, dt, kind="ExternalOutput").ap()
            dG.inc(e.dma_start(out=dten[tuple(slice(None) for _ in shape)],
                               in_=sbten[tuple(slice(None) for _ in shape)]), 16)

    phase_barrier(st)


def phase_a2(st):
    c, nc, blk = st.cfg, st.nc, st.blk
    dS, dG = st.sems["dS"], st.sems["dG"]
    (kMM, kEV, kSQ, rSQ, nMM, vMM, vEV, mxA, nK) = _mk_sems(
        st, ["a2kmm", "a2kev", "a2ksq", "a2krsq", "a2nmm",
             "a2vmm", "a2vev", "a2mx", "a2nk"])
    NDC = c.D // 512

    with contextlib.ExitStack() as ph:
        w_uk_b = ph.enter_context(nc.sbuf_tensor("w_uk_b", [128, c.CB, c.D], BF16))
        w_uv_b = ph.enter_context(nc.sbuf_tensor("w_uv_b", [128, c.CB, c.D], BF16))
        kbuf = ph.enter_context(nc.sbuf_tensor("kbuf", [128, 8, 512], BF16))
        ksq = ph.enter_context(nc.sbuf_tensor("ksq", [128, 4, 512], BF16))
        krsq = ph.enter_context(nc.sbuf_tensor("krsq", [64, 512], BF16))
        vbuf = ph.enter_context(nc.sbuf_tensor("vbuf", [128, 2, c.D], BF16))
        mxacc = ph.enter_context(nc.sbuf_tensor("mxacc", [1, 512], F32))
        mxs = ph.enter_context(nc.sbuf_tensor("mxs", [1, 2], F32))
        pK = ph.enter_context(nc.psum_tensor("pK", [128, 2, 512], F32))
        pV = ph.enter_context(nc.psum_tensor("pV", [128, 2, 512], F32))
        pN = ph.enter_context(nc.psum_tensor("pN", [1, 512], F32))

        d_w0 = dG.n

        @blk.gpsimd
        def _(e):
            dG.inc(e.dma_start(out=w_uk_b[:, :, :],
                               in_=st.io["w_uk"].rearrange("(cb p) k -> p cb k", p=128)), 16)
            dG.inc(e.dma_start(out=w_uv_b[:, :, :],
                               in_=st.io["w_uv"].rearrange("(cb p) k -> p cb k", p=128)), 16)

        d_wready = dG.n
        KGR = c.FB // 4
        DPC = KGR + c.TPC

        kb = [st.gpool[0].n, st.gpool[1].n]
        vb = [st.gpool[2].n, st.gpool[3].n]

        def kgidx(i_fb):
            ci_, gr_ = i_fb // c.FB, (i_fb % c.FB) // 4
            return ci_ * KGR + gr_

        def kdma_sem(i_fb):
            g = kgidx(i_fb)
            return st.gpool[g % 2], kb[g % 2] + 16 * (g // 2 + 1)

        def vdma_sem(jj):
            return st.gpool[2 + jj % 2], vb[jj % 2] + 16 * (jj // 2 + 1)

        @blk.vector
        def _(e):
            e.memset(mxacc[:, :], 0.0)
            for ci in range(c.NCH):
                sl = slice(ci * c.CH, (ci + 1) * c.CH)
                for fb in range(c.FB):
                    i = ci * c.FB + fb
                    e.wait_ge(kMM.h, i + 1)
                    if i >= 8:
                        s_, v_ = kdma_sem(i - 8)
                        e.wait_ge(s_.h, v_)
                    kEV.inc(e.tensor_scalar(out=kbuf[:, i % 8, :],
                                            in0=pK[:, i % 2, :],
                                            scalar1=st.bT_uk[:, fb:fb + 1],
                                            scalar2=None, op0=AL.add))
                    e.wait_ge(kEV.h, kEV.n)
                    if i >= 4:
                        e.wait_ge(nK.h, i - 3)
                    kSQ.inc(e.tensor_tensor(out=ksq[:, i % 4, :],
                                            in0=kbuf[:, i % 8, :],
                                            in1=kbuf[:, i % 8, :], op=AL.mult))
                if ci >= 1:
                    e.wait_ge(nMM.h, ci)
                rSQ.inc(e.tensor_tensor(out=krsq[:, :], in0=st.KRR[0:64, sl],
                                        in1=st.KRR[0:64, sl], op=AL.mult))
                for tt in range(c.TPC):
                    for dc in range(NDC):
                        i = (ci * c.TPC + tt) * NDC + dc
                        jj = ci * c.TPC + tt
                        e.wait_ge(vMM.h, i + 1)
                        if dc == 0 and jj >= 2:
                            s_, v_ = vdma_sem(jj - 2)
                            e.wait_ge(s_.h, v_)
                        vEV.inc(e.tensor_copy(out=vbuf[:, tt % 2, dc * 512:(dc + 1) * 512],
                                              in_=pV[:, i % 2, :]))
                e.wait_ge(nMM.h, ci + 1)
                mxA.inc(e.tensor_tensor(out=mxacc[:, :], in0=mxacc[:, :],
                                        in1=pN[:, :], op=AL.max))
            e.wait_ge(mxA.h, c.NCH)
            mxA.inc(e.reduce_max(out=mxs[:, 0:1], in_=mxacc[:, :], axis=AX))

        @blk.scalar
        def _(e):
            e.wait_ge(mxA.h, c.NCH + 1)
            mxA.inc(e.activation(out=mxs[:, 1:2], in_=mxs[:, 0:1], func=AF.Sqrt,
                                 bias=0.0, scale=1.0))

        @blk.vector
        def _(e):
            e.wait_ge(mxA.h, c.NCH + 2)
            mxA.inc(e.tensor_scalar(out=st.negB[:, :], in0=mxs[:, 1:2],
                                    scalar1=-1.05, scalar2=None, op0=AL.mult))

        @blk.tensor
        def _(e):
            e.wait_ge(dG.h, d_wready)
            for ci in range(c.NCH):
                sl = slice(ci * c.CH, (ci + 1) * c.CH)
                for fb in range(c.FB):
                    i = ci * c.FB + fb
                    if i >= 2:
                        e.wait_ge(kEV.h, i - 1)
                    for cb in range(c.CB):
                        ins = e.matmul(pK[:, i % 2, :],
                                       w_uk_b[:, cb, fb * 128:(fb + 1) * 128],
                                       st.CKVT[:, cb, sl],
                                       start=(cb == 0), stop=(cb == c.CB - 1))
                    kMM.inc(ins)
                    nfb = fb - 4
                    if nfb >= 0:
                        ni = ci * c.FB + nfb
                        e.wait_ge(kSQ.h, ni + 1)
                        if nfb == 0 and ci >= 1:
                            e.wait_ge(mxA.h, ci)
                        nK.inc(e.matmul(pN[:, :], st.ones128b[:, :], ksq[:, ni % 4, :],
                                        start=(nfb == 0), stop=False))
                for nfb in range(max(c.FB - 4, 0), c.FB):
                    ni = ci * c.FB + nfb
                    e.wait_ge(kSQ.h, ni + 1)
                    if nfb == 0 and ci >= 1:
                        e.wait_ge(mxA.h, ci)
                    nK.inc(e.matmul(pN[:, :], st.ones128b[:, :], ksq[:, ni % 4, :],
                                    start=(nfb == 0), stop=False))
                e.wait_ge(rSQ.h, ci + 1)
                nMM.inc(e.matmul(pN[:, :], st.ones128b[0:64, :], krsq[:, :],
                                 start=False, stop=True))
                for tt in range(c.TPC):
                    tsl = slice(ci * c.CH + tt * 128, ci * c.CH + (tt + 1) * 128)
                    for dc in range(NDC):
                        i = (ci * c.TPC + tt) * NDC + dc
                        if i >= 2:
                            e.wait_ge(vEV.h, i - 1)
                        for cb in range(c.CB):
                            e.matmul(pV[:, i % 2, :],
                                     st.CKVT[:, cb, tsl],
                                     w_uv_b[:, cb, dc * 512:(dc + 1) * 512],
                                     start=(cb == 0), stop=False)
                        vMM.inc(e.matmul(pV[:, i % 2, :], st.ones1b[:, :],
                                         st.brow[0:1, dc * 512:(dc + 1) * 512],
                                         start=False, stop=True))

        @blk.gpsimd
        def _(e):
            for ci in range(c.NCH):
                for gr in range(KGR):
                    g = ci * KGR + gr
                    e.wait_ge(kEV.h, ci * c.FB + 4 * gr + 4)
                    st.gpool[g % 2].inc(e.dma_start(
                        out=st.KT[4 * gr * 128:(4 * gr + 4) * 128,
                                  ci * c.CH:(ci + 1) * c.CH].rearrange(
                                      "(fb p) t -> p fb t", p=128),
                        in_=kbuf[:, (4 * g) % 8:(4 * g) % 8 + 4, :]), 16)
                for tt in range(c.TPC):
                    jj = ci * c.TPC + tt
                    e.wait_ge(vEV.h, (jj + 1) * NDC)
                    st.gpool[2 + jj % 2].inc(e.dma_start(
                        out=st.VS[ci * c.CH + tt * 128:ci * c.CH + (tt + 1) * 128, :],
                        in_=vbuf[:, tt % 2, :]), 16)

        phase_barrier(st)


def phase_a2b(st):
    c, nc, blk = st.cfg, st.nc, st.blk
    dS, dG = st.sems["dS"], st.sems["dG"]
    (bMM, bEV) = _mk_sems(st, ["a2bmm", "a2bev"])

    with contextlib.ExitStack() as ph:
        w_dq_b = ph.enter_context(nc.sbuf_tensor("w_dq_b", [128, c.FB, c.DC1], BF16))
        xq = ph.enter_context(nc.sbuf_tensor("xq", [128, 2, c.FB, c.CH], BF16))
        cqb = ph.enter_context(nc.sbuf_tensor("cqb", [128, 1, c.C1B, c.CH], BF16))
        pB = ph.enter_context(nc.psum_tensor("pB2", [128, 2, 512], F32))

        d_w0 = dG.n

        @blk.gpsimd
        def _(e):
            dG.inc(e.dma_start(out=w_dq_b[:, :, :],
                               in_=st.io["w_dq"].rearrange("(fb p) k -> p fb k", p=128)), 16)
            for qch in range(c.NQCH):
                e.wait_ge(bEV.h, (qch + 1) * c.C1B)
                dG.inc(e.dma_start(
                    out=st.CQT[:, qch * c.CH:(qch + 1) * c.CH].rearrange(
                        "(cb p) t -> p cb t", p=128),
                    in_=cqb[:, 0, :, :]), 16)

        d_wready = d_w0 + 16
        d_x0 = dS.n

        qb0, qb1 = st.spool[0].n, st.spool[1].n

        @blk.sync
        def _(e):
            for qch in range(c.NQCH):
                if qch >= 2:
                    e.wait_ge(bMM.h, (qch - 1) * c.C1B)
                st.spool[qch % 2].inc(e.dma_start(
                    out=xq[:, qch % 2, :, :],
                    in_=st.XLT[:, qch * c.CH:(qch + 1) * c.CH].rearrange(
                        "(fb p) t -> p fb t", p=128)), 16)

        @blk.tensor
        def _(e):
            e.wait_ge(dG.h, d_wready)
            for qch in range(c.NQCH):
                e.wait_ge(st.spool[qch % 2].h,
                          (qb0 if qch % 2 == 0 else qb1) + 16 * (qch // 2 + 1))
                for cb in range(c.C1B):
                    i = qch * c.C1B + cb
                    if i >= 2:
                        e.wait_ge(bEV.h, i - 1)
                    for fb in range(c.FB):
                        ins = e.matmul(pB[:, i % 2, 0:c.CH],
                                       w_dq_b[:, fb, cb * 128:(cb + 1) * 128],
                                       xq[:, qch % 2, fb, :],
                                       start=(fb == 0), stop=(fb == c.FB - 1))
                    bMM.inc(ins)

        @blk.vector
        def _(e):
            for qch in range(c.NQCH):
                for cb in range(c.C1B):
                    i = qch * c.C1B + cb
                    e.wait_ge(bMM.h, i + 1)
                    if qch >= 1 and cb == 0:
                        e.wait_ge(dG.h, d_wready + 16 * qch)
                    bEV.inc(e.tensor_scalar(out=cqb[:, 0, cb, :],
                                            in0=pB[:, i % 2, 0:c.CH],
                                            scalar1=st.bT_dq[:, cb:cb + 1],
                                            scalar2=None, op0=AL.add))

        phase_barrier(st)


def _phase_a3_generic(st, tag, wname, wblocks, bT, dst, scale):
    c, nc, blk = st.cfg, st.nc, st.blk
    dS, dG = st.sems["dS"], st.sems["dG"]
    (gMM, gEV) = _mk_sems(st, [tag + "mm", tag + "ev"])
    OB = wblocks

    with contextlib.ExitStack() as ph:
        w_b = ph.enter_context(nc.sbuf_tensor(tag + "w", [128, c.C1B, OB * 128], BF16))
        cq = ph.enter_context(nc.sbuf_tensor(tag + "cq", [128, 2, c.C1B, c.CH], BF16))
        g4pre = 4 if OB % 4 == 0 else (2 if OB % 2 == 0 else 1)
        ob_ring = ph.enter_context(nc.sbuf_tensor(tag + "ob", [128, 2 * g4pre, c.CH], BF16))
        pQ = ph.enter_context(nc.psum_tensor(tag + "pq", [128, 2, 512], F32))

        d_w0 = dG.n
        g4 = 4 if OB % 4 == 0 else (2 if OB % 2 == 0 else 1)
        NSLOT = 2 * g4

        @blk.gpsimd
        def _(e):
            dG.inc(e.dma_start(out=w_b[:, :, :],
                               in_=st.io[wname].rearrange("(cb p) k -> p cb k", p=128)), 16)
            for qch in range(c.NQCH):
                for gr in range(OB // g4):
                    e.wait_ge(gEV.h, qch * OB + g4 * gr + g4)
                    dG.inc(e.dma_start(
                        out=dst[g4 * gr * 128:(g4 * gr + g4) * 128,
                                qch * c.CH:(qch + 1) * c.CH].rearrange(
                                    "(ob p) t -> p ob t", p=128),
                        in_=ob_ring[:, g4 * (gr % 2):g4 * (gr % 2) + g4, :]), 16)

        d_wready = d_w0 + 16
        d_x0 = dS.n
        DPQ = OB // g4

        a3_dg_total = d_wready + 16 * c.NQCH * DPQ

        def odma_done(i_ob):
            q_, g_ = i_ob // OB, (i_ob % OB) // g4
            return min(d_wready + 16 * (q_ * DPQ + g_ + 2), a3_dg_total)

        cb0, cb1 = st.spool[0].n, st.spool[1].n

        @blk.sync
        def _(e):
            for qch in range(c.NQCH):
                if qch >= 2:
                    e.wait_ge(gMM.h, (qch - 1) * OB)
                st.spool[qch % 2].inc(e.dma_start(
                    out=cq[:, qch % 2, :, :],
                    in_=st.CQT[:, qch * c.CH:(qch + 1) * c.CH].rearrange(
                        "(cb p) t -> p cb t", p=128)), 16)

        @blk.tensor
        def _(e):
            e.wait_ge(dG.h, d_wready)
            for qch in range(c.NQCH):
                e.wait_ge(st.spool[qch % 2].h,
                          (cb0 if qch % 2 == 0 else cb1) + 16 * (qch // 2 + 1))
                for ob in range(OB):
                    i = qch * OB + ob
                    if i >= 2:
                        e.wait_ge(gEV.h, i - 1)
                    for cb in range(c.C1B):
                        ins = e.matmul(pQ[:, i % 2, 0:c.CH],
                                       w_b[:, cb, ob * 128:(ob + 1) * 128],
                                       cq[:, qch % 2, cb, :],
                                       start=(cb == 0), stop=(cb == c.C1B - 1))
                    gMM.inc(ins)

        @blk.vector
        def _(e):
            for qch in range(c.NQCH):
                for ob in range(OB):
                    i = qch * OB + ob
                    e.wait_ge(gMM.h, i + 1)
                    if i >= NSLOT:
                        e.wait_ge(dG.h, odma_done(i - NSLOT))
                    if scale is not None:
                        gEV.inc(e.tensor_scalar(out=ob_ring[:, i % NSLOT, :],
                                                in0=pQ[:, i % 2, 0:c.CH],
                                                scalar1=bT[:, ob:ob + 1],
                                                scalar2=scale, op0=AL.add,
                                                op1=AL.mult))
                    else:
                        gEV.inc(e.tensor_scalar(out=ob_ring[:, i % NSLOT, :],
                                                in0=pQ[:, i % 2, 0:c.CH],
                                                scalar1=bT[:, ob:ob + 1],
                                                scalar2=None, op0=AL.add))

        phase_barrier(st)


def phase_att(st):
    c, nc, blk = st.cfg, st.nc, st.blk
    dS, dG = st.sems["dS"], st.sems["dG"]
    (nqM, sqN, rk1, rowQ, sMM, eXP, aP4, rcV, bcM, sBC, vOU,
     hSQ) = _mk_sems(st, ["atnq", "atsq", "atr1", "atrow", "atsmm", "atexp",
                          "atap4", "atrcv", "atbcm",
                          "atsbc", "atvou", "athsq"])
    NQC, SKB, QC = c.NQC, c.SKB, c.QC
    e_base = eXP.n
    a_base = aP4.n
    s_base = sMM.n

    def pair_done(gkt):
        return e_base + gkt // 2 + 1

    with contextlib.ExitStack() as ph:
        ksb = ph.enter_context(nc.sbuf_tensor("ksb", [128, 2, c.SK], BF16))
        vsb = ph.enter_context(nc.sbuf_tensor("vsb", [128, 2, SKB, 128], BF16))
        qsb = ph.enter_context(nc.sbuf_tensor("qsb", [128, 2, c.NQ], BF16))
        qrr = ph.enter_context(nc.sbuf_tensor("qrr", [65, 2, c.NQ], BF16))
        qsq = ph.enter_context(nc.sbuf_tensor("qsq", [128, c.NQ], BF16))
        qrsq = ph.enter_context(nc.sbuf_tensor("qrsq", [64, c.NQ], BF16))
        PR = 16
        Pst = ph.enter_context(nc.sbuf_tensor("Pst", [128, PR, QC], BF16))
        sqn = ph.enter_context(nc.sbuf_tensor("sqn", [1, QC], BF16))
        rinv = ph.enter_context(nc.sbuf_tensor("rinv", [1, QC], F32))
        bcst = ph.enter_context(nc.sbuf_tensor("bcst", [128, QC], F32))
        ao = ph.enter_context(nc.sbuf_tensor("ao", [128, 2, c.NQ], BF16))
        pS = ph.enter_context(nc.psum_tensor("pS", [128, 4, 512], F32))
        pAV = ph.enter_context(nc.psum_tensor("pAV", [128, 512], F32))
        pBC = ph.enter_context(nc.psum_tensor("pBC", [128, 512], F32))
        pSm = ph.enter_context(nc.psum_tensor("pSm", [1, 2, 512], F32))

        hb0, hb1 = st.gpool[0].n, st.gpool[1].n
        ab0, ab1 = st.gpool[2].n, st.gpool[3].n

        def hload_done(h):
            return (hb0 if h % 2 == 0 else hb1) + 64 * (h // 2 + 1)

        def aodma_done(j):
            return (ab0 if j % 2 == 0 else ab1) + 16 * (j // 2 + 1)

        @blk.gpsimd
        def _(e):
            for h in range(c.H):
                if h >= 2:
                    e.wait_ge(vOU.h, (h - 2 + 1) * NQC)
                    st.gpool[2 + h % 2].inc(e.dma_start(
                        out=st.AOT[(h - 2) * 128:(h - 1) * 128, :],
                        in_=ao[:, h % 2, :]), 16)
                if h >= 2:
                    e.wait_ge(aP4.h, a_base + (SKB // 4) * (h - 1) * NQC)
                hp = st.gpool[h % 2]
                hp.inc(e.dma_start(out=ksb[:, h % 2, :],
                                   in_=st.KT[h * 128:(h + 1) * 128, :]), 16)
                hp.inc(e.dma_start(out=vsb[:, h % 2, :, :],
                                   in_=st.VS[:, h * 128:(h + 1) * 128].rearrange(
                                       "(kt p) d -> p kt d", p=128)), 16)
                hp.inc(e.dma_start(out=qsb[:, h % 2, :],
                                   in_=st.QT[h * 128:(h + 1) * 128, :]), 16)
                hp.inc(e.dma_start(out=qrr[0:64, h % 2, :],
                                   in_=st.QRT[h * c.DR:(h + 1) * c.DR, :]), 16)
            for h in range(max(c.H - 2, 0), c.H):
                e.wait_ge(vOU.h, (h + 1) * NQC)
                st.gpool[2 + h % 2].inc(e.dma_start(
                    out=st.AOT[h * 128:(h + 1) * 128, :],
                    in_=ao[:, h % 2, :]), 16)

        @blk.vector
        def _(e):
            for h in range(c.H):
                e.wait_ge(st.gpool[h % 2].h, hload_done(h))
                if h >= 1:
                    e.wait_ge(nqM.h, h * NQC)
                hSQ.inc(e.tensor_tensor(out=qsq[:, :], in0=qsb[:, h % 2, :],
                                        in1=qsb[:, h % 2, :], op=AL.mult))
                hSQ.inc(e.tensor_tensor(out=qrsq[:, :], in0=qrr[0:64, h % 2, :],
                                        in1=qrr[0:64, h % 2, :], op=AL.mult))
                for qc in range(NQC):
                    t = h * NQC + qc
                    qsl = slice(qc * QC, (qc + 1) * QC)
                    if qc == 0 and h >= 2:
                        e.wait_ge(st.gpool[2 + h % 2].h, aodma_done(h - 2))
                    e.wait_ge(aP4.h, a_base + (SKB // 4) * (t + 1))
                    rcV.inc(e.reciprocal(out=rinv[:, :], in_=pSm[:, 1, 0:QC]))
                    e.wait_ge(sBC.h, t + 1)
                    vOU.inc(e.tensor_tensor(out=ao[:, h % 2, qsl], in0=pAV[:, 0:QC],
                                            in1=bcst[:, 0:QC], op=AL.mult))

        @blk.scalar
        def _(e):
            for h in range(c.H):
                for qc in range(NQC):
                    t = h * NQC + qc
                    qsl = slice(qc * QC, (qc + 1) * QC)
                    e.wait_ge(nqM.h, t + 1)
                    sqN.inc(e.activation(out=sqn[:, :], in_=pSm[:, 0, 0:QC],
                                         func=AF.Sqrt, bias=0.0, scale=1.0))
                    e.wait_ge(rk1.h, t + 1)
                    rowQ.inc(e.copy(out=qrr[64:65, h % 2, qsl], in_=pSm[:, 0, 0:QC]))
                    for ep in range(SKB // 2):
                        g0 = t * SKB + 2 * ep
                        e.wait_ge(sMM.h, s_base + g0 + 2)
                        if g0 >= PR:
                            e.wait_ge(aP4.h, a_base + (g0 - 15) // 4 + 1)
                        eXP.inc(e.activation(
                            out=Pst[:, g0 % PR:g0 % PR + 2, :],
                            in_=pS[:, (2 * ep) % 4:(2 * ep) % 4 + 2, 0:QC],
                            func=AF.Exp, bias=0.0, scale=1.0))
                    e.wait_ge(bcM.h, t + 1)
                    sBC.inc(e.copy(out=bcst[:, 0:QC], in_=pBC[:, 0:QC]))

        @blk.tensor
        def _(e):
            for h in range(c.H):
                for qc in range(NQC):
                    t = h * NQC + qc
                    qsl = slice(qc * QC, (qc + 1) * QC)
                    e.wait_ge(hSQ.h, 2 * (h + 1))
                    if t >= 1:
                        e.wait_ge(rowQ.h, t)
                    e.matmul(pSm[:, 0, 0:QC], st.ones128b[:, :], qsq[:, qsl],
                             start=True, stop=False)
                    nqM.inc(e.matmul(pSm[:, 0, 0:QC], st.ones128b[0:64, :],
                                     qrsq[:, qsl], start=False, stop=True))
                    e.wait_ge(sqN.h, t + 1)
                    rk1.inc(e.matmul(pSm[:, 0, 0:QC], st.negB[:, :], sqn[:, :],
                                     start=True, stop=True))
                    e.wait_ge(rowQ.h, t + 1)
                    for kt in range(SKB):
                        gkt = t * SKB + kt
                        if gkt >= 4:
                            e.wait_ge(eXP.h, pair_done(gkt - 4))
                        e.matmul(pS[:, kt % 4, 0:QC],
                                 ksb[:, h % 2, kt * 128:(kt + 1) * 128],
                                 qsb[:, h % 2, qsl], start=True, stop=False)
                        sMM.inc(e.matmul(pS[:, kt % 4, 0:QC],
                                         st.KRR[:, kt * 128:(kt + 1) * 128],
                                         qrr[:, h % 2, qsl], start=False, stop=True))
                        if kt >= 1:
                            akt = kt - 1
                            gat = t * SKB + akt
                            e.wait_ge(eXP.h, pair_done(gat))
                            if akt == 0:
                                if t >= 1:
                                    e.wait_ge(vOU.h, t)
                                    e.wait_ge(rcV.h, t)
                            e.matmul(pAV[:, 0:QC], vsb[:, h % 2, akt, :],
                                     Pst[:, gat % PR, :], start=(akt == 0),
                                     stop=(akt == SKB - 1))
                            ins = e.matmul(pSm[:, 1, 0:QC], st.ones128b[:, :],
                                           Pst[:, gat % PR, :], start=(akt == 0),
                                           stop=(akt == SKB - 1))
                            if akt % 4 == 3:
                                aP4.inc(ins)
                    for akt in range(SKB - 1, SKB):
                        gat = t * SKB + akt
                        e.wait_ge(eXP.h, pair_done(gat))
                        e.matmul(pAV[:, 0:QC], vsb[:, h % 2, akt, :],
                                 Pst[:, gat % PR, :], start=False, stop=True)
                        ins = e.matmul(pSm[:, 1, 0:QC], st.ones128b[:, :],
                                       Pst[:, gat % PR, :], start=False, stop=True)
                        aP4.inc(ins)
                    e.wait_ge(rcV.h, t + 1)
                    if t >= 1:
                        e.wait_ge(sBC.h, t)
                    bcM.inc(e.matmul(pBC[:, 0:QC], st.ones1f[:, :], rinv[:, :],
                                     start=True, stop=True))

        phase_barrier(st)


def phase_wo(st):
    c, nc, blk = st.cfg, st.nc, st.blk
    dS, dA, dG = st.sems["dS"], st.sems["dA"], st.sems["dG"]
    (wMM, wEV) = _mk_sems(st, ["womm", "woev"])
    NDC = c.D // 512
    NQB = c.NQB

    with contextlib.ExitStack() as ph:
        wo_bf = ph.enter_context(nc.sbuf_tensor("wo_bf", [128, 2, c.FB, 512], BF16))
        aot = ph.enter_context(nc.sbuf_tensor("aot", [128, 2, c.FB, 128], BF16))
        xres = ph.enter_context(nc.sbuf_tensor("xres", [128, 2, 512], F32))
        xm = ph.enter_context(nc.sbuf_tensor("xm", [128, 2, 512], F32))
        pW = ph.enter_context(nc.psum_tensor("pW", [128, 2, 512], F32))

        d_g0, d_s0, d_a0 = dG.n, dS.n, dA.n
        atb = [st.gpool[0].n, st.gpool[1].n]
        xrb = [st.spool[0].n, st.spool[1].n]
        xmb = [st.apool[0].n, st.apool[1].n]

        def wo_done(fc):
            return d_g0 + 16 * (fc + 1)

        def aot_sem(i):
            return st.gpool[i % 2], atb[i % 2] + 16 * (i // 2 + 1)

        def xres_sem(i):
            return st.spool[i % 2], xrb[i % 2] + 16 * (i // 2 + 1)

        def xm_sem(i):
            return st.apool[i % 2], xmb[i % 2] + 16 * (i // 2 + 1)

        @blk.gpsimd
        def _(e):
            for fc in range(NDC):
                if fc >= 2:
                    e.wait_ge(wMM.h, (fc - 1) * NQB)
                dG.inc(e.dma_start(
                    out=wo_bf[:, fc % 2, :, :],
                    in_=st.io["w_out"][:, fc * 512:(fc + 1) * 512].rearrange(
                        "(hd p) f -> p hd f", p=128)), 16)
                for qt in range(NQB):
                    i = fc * NQB + qt
                    if i >= 2:
                        e.wait_ge(wMM.h, i - 1)
                    st.gpool[i % 2].inc(e.dma_start(
                        out=aot[:, qt % 2, :, :],
                        in_=st.AOT[:, qt * 128:(qt + 1) * 128].rearrange(
                            "(hd p) t -> p hd t", p=128)), 16)

        @blk.sync
        def _(e):
            for fc in range(NDC):
                for qt in range(NQB):
                    i = fc * NQB + qt
                    if i >= 2:
                        e.wait_ge(wEV.h, i - 1)
                    st.spool[i % 2].inc(e.dma_start(
                        out=xres[:, i % 2, :],
                        in_=st.io["x"][qt * 128:(qt + 1) * 128,
                                       fc * 512:(fc + 1) * 512]), 16)

        @blk.tensor
        def _(e):
            for fc in range(NDC):
                e.wait_ge(dG.h, wo_done(fc))
                for qt in range(NQB):
                    i = fc * NQB + qt
                    s_, v_ = aot_sem(i)
                    e.wait_ge(s_.h, v_)
                    if i >= 2:
                        e.wait_ge(wEV.h, i - 1)
                    for hd in range(c.FB):
                        e.matmul(pW[:, i % 2, :], aot[:, qt % 2, hd, :],
                                 wo_bf[:, fc % 2, hd, :],
                                 start=(hd == 0), stop=False)
                    wMM.inc(e.matmul(pW[:, i % 2, :], st.ones33b[32:33, :],
                                     st.brow[32:33, fc * 512:(fc + 1) * 512],
                                     start=False, stop=True))

        @blk.vector
        def _(e):
            for fc in range(NDC):
                for qt in range(NQB):
                    i = fc * NQB + qt
                    e.wait_ge(wMM.h, i + 1)
                    s_, v_ = xres_sem(i)
                    e.wait_ge(s_.h, v_)
                    if i >= 2:
                        s2_, v2_ = xm_sem(i - 2)
                        e.wait_ge(s2_.h, v2_)
                    wEV.inc(e.tensor_tensor(out=xm[:, i % 2, :], in0=pW[:, i % 2, :],
                                            in1=xres[:, i % 2, :], op=AL.add))

        @blk.scalar
        def _(e):
            for fc in range(NDC):
                for qt in range(NQB):
                    i = fc * NQB + qt
                    e.wait_ge(wEV.h, i + 1)
                    st.apool[i % 2].inc(e.dma_start(
                        out=st.XMID[qt * 128:(qt + 1) * 128,
                                    fc * 512:(fc + 1) * 512],
                        in_=xm[:, i % 2, :]), 16)

        phase_barrier(st)


def phase_ln2(st):
    c, nc, blk = st.cfg, st.nc, st.blk
    dS, dG = st.sems["dS"], st.sems["dG"]
    (lST, lSQ, lAP, lTP, lTE, vCH) = _mk_sems(
        st, ["l2stat", "l2sqrt", "l2apply", "l2tp", "l2tev", "l2vch"])
    NG = c.D // 512
    GPT = c.FB // 4

    with contextlib.ExitStack() as ph:
        xs = ph.enter_context(nc.sbuf_tensor("xs2", [128, 2, c.D], F32))
        xlb = ph.enter_context(nc.sbuf_tensor("xlb2", [128, 2, c.D], BF16))
        xlT = ph.enter_context(nc.sbuf_tensor("xlT2", [128, 2, c.FB, 128], BF16))
        stat = ph.enter_context(nc.sbuf_tensor("stat2", [128, 8], F32))
        mv6 = ph.enter_context(nc.sbuf_tensor("mv62", [128, NG, 6], F32))
        pT1 = ph.enter_context(nc.psum_tensor("pT2", [128, 2, 512], F32))

        d_s0, d_g0 = dS.n, dG.n

        lb = [st.spool[0].n, st.spool[1].n]
        gb = [st.gpool[0].n, st.gpool[1].n]

        @blk.sync
        def _(e):
            for g in range(c.NQB):
                if g >= 2:
                    e.wait_ge(lAP.h, g - 1)
                st.spool[g % 2].inc(e.dma_start(
                    out=xs[:, g % 2, :],
                    in_=st.XMID[g * 128:(g + 1) * 128, :]), 16)

        @blk.vector
        def _(e):
            for g in range(c.NQB):
                e.wait_ge(st.spool[g % 2].h,
                          lb[g % 2] + 16 * (g // 2 + 1))
                for ng in range(NG):
                    ins = e.bn_stats(out=mv6[:, ng, :],
                                     in_=xs[:, g % 2, ng * 512:(ng + 1) * 512])
                vCH.inc(ins)
                e.wait_ge(vCH.h, vCH.n)
                lST.inc(e.bn_aggr(out=stat[:, 0:2], in_=mv6[:, :, :]))
                e.wait_ge(lSQ.h, g + 1)
                vCH.inc(e.reciprocal(out=stat[:, 3:4], in_=stat[:, 2:3]))
                e.wait_ge(vCH.h, vCH.n)
                vCH.inc(e.tensor_scalar(out=stat[:, 4:5], in0=stat[:, 0:1],
                                        scalar1=stat[:, 3:4], scalar2=-1.0,
                                        op0=AL.mult, op1=AL.mult))
                e.wait_ge(vCH.h, vCH.n)
                if g >= 2:
                    e.wait_ge(lTP.h, (g - 1) * GPT)
                lAP.inc(e.tensor_scalar(out=xlb[:, g % 2, :], in0=xs[:, g % 2, :],
                                        scalar1=stat[:, 4:5], scalar2=stat[:, 3:4],
                                        op0=AL.add, op1=AL.mult))

        @blk.scalar
        def _(e):
            for g in range(c.NQB):
                e.wait_ge(lST.h, g + 1)
                lSQ.inc(e.activation(out=stat[:, 2:3], in_=stat[:, 1:2],
                                     func=AF.Sqrt, bias=st.epsb[:, 0:1], scale=1.0))
                for j in range(GPT):
                    gidx = g * GPT + j
                    e.wait_ge(lTP.h, gidx + 1)
                    if j == 0 and g >= 2:
                        e.wait_ge(st.gpool[g % 2].h,
                                  gb[g % 2] + 16 * ((g - 2) // 2 + 1))
                    lTE.inc(e.copy(
                        out=xlT[:, g % 2, 4 * j:4 * j + 4, :],
                        in_=pT1[:, gidx % 2, :].rearrange("p (b t) -> p b t", b=4)))

        @blk.tensor
        def _(e):
            for g in range(c.NQB):
                e.wait_ge(lAP.h, g + 1)
                for j in range(GPT):
                    gidx = g * GPT + j
                    if gidx >= 2:
                        e.wait_ge(lTE.h, gidx - 1)
                    for q in range(4):
                        fb = 4 * j + q
                        ins = e.matmul(pT1[:, gidx % 2, q * 128:(q + 1) * 128],
                                       xlb[:, g % 2, fb * 128:(fb + 1) * 128],
                                       st.identb[:, :], start=True, stop=True)
                        if q == 3:
                            lTP.inc(ins)

        @blk.gpsimd
        def _(e):
            for g in range(c.NQB):
                e.wait_ge(lTE.h, (g + 1) * GPT)
                st.gpool[g % 2].inc(e.dma_start(
                    out=st.XL2T[:, g * 128:(g + 1) * 128].rearrange(
                        "(fb p) t -> p fb t", p=128),
                    in_=xlT[:, g % 2, :, :]), 16)

        phase_barrier(st)


def _phase_deq(st, tag, wp, sr, WT, IBN, ON):
    c, nc, blk = st.cfg, st.nc, st.blk
    dS, dG = st.sems["dS"], st.sems["dG"]
    (dqB, dqW, dqV, dqM) = _mk_sems(st, [tag + "b", tag + "w", tag + "v",
                                          tag + "m"])
    chunks = []
    o = 0
    while o < ON:
        sz = min(512, ON - o)
        chunks.append((o, sz))
        o += sz
    NCHK = len(chunks)

    with contextlib.ExitStack() as ph:
        pk8 = ph.enter_context(nc.sbuf_tensor(tag + "pk", [64, 2, ON], I8))
        srf = ph.enter_context(nc.sbuf_tensor(tag + "sr", [2, 1, ON], F32))
        ssp = ph.enter_context(nc.sbuf_tensor(tag + "ss", [34, ON], BF16))
        v8 = ph.enter_context(nc.sbuf_tensor(tag + "v8", [64, ON], I8))
        tb8 = ph.enter_context(nc.sbuf_tensor(tag + "tb", [64, ON], I8))
        tt8 = ph.enter_context(nc.sbuf_tensor(tag + "tt", [64, ON], I8))
        wrow = ph.enter_context(nc.sbuf_tensor(tag + "wr", [128, 2, ON], BF16))
        pSC = ph.enter_context(nc.psum_tensor(tag + "ps", [128, 2, 512], F32))

        db_ = [st.spool[0].n, st.spool[1].n]
        gb_deq = [st.gpool[0].n, st.gpool[1].n]

        @blk.sync
        def _(e):
            for ib in range(IBN):
                if ib >= 2:
                    e.wait_ge(dqV.h, 4 * (ib - 1))
                if ib >= 1:
                    e.wait_ge(dqV.h, 4 * (ib - 1) + 2)
                st.spool[ib % 2].inc(e.dma_start(out=pk8[:, ib % 2, :],
                                                 in_=wp[ib, :, :]), 16)
                st.spool[ib % 2].inc(e.dma_start(out=srf[:, 0, :],
                                                 in_=sr[ib, :, :]), 16)

        @blk.vector
        def _(e):
            dqM.inc(e.memset(ssp[:, :], 0.0))
            e.wait_ge(dqM.h, dqM.n)
            for ib in range(IBN):
                e.wait_ge(st.spool[ib % 2].h, db_[ib % 2] + 32 * (ib // 2 + 1))
                if ib >= 1:
                    e.wait_ge(dqB.h, ib * NCHK)
                dqV.inc(e.tensor_copy(out=ssp[0:2, :], in_=srf[:, 0, :]))
                e.wait_ge(dqV.h, dqV.n)
                dqV.inc(e.tensor_tensor(out=ssp[32:34, :], in0=srf[:, 0, :],
                                        in1=ssp[0:2, :], op=AL.subtract))
                if ib >= 1:
                    e.wait_ge(dqW.h, ib * 2 * NCHK)
                dqV.inc(e.tensor_scalar(out=v8[:, :], in0=pk8[:, ib % 2, :],
                                        scalar1=15, scalar2=None, op0=AL.bitwise_and))
                e.wait_ge(dqV.h, dqV.n)
                e.tensor_scalar(out=tb8[:, :], in0=v8[:, :],
                                scalar1=8, scalar2=None, op0=AL.bitwise_xor)
                dqV.inc(e.tensor_tensor(out=tt8[:, :], in0=pk8[:, ib % 2, :],
                                        in1=v8[:, :], op=AL.subtract))
                e.wait_ge(dqV.h, dqV.n)
                for k, (o0, sz) in enumerate(chunks):
                    e.wait_ge(dqB.h, ib * NCHK + k + 1)
                    if ib >= 2 and k == 0:
                        e.wait_ge(st.gpool[ib % 2].h,
                                  gb_deq[ib % 2] + 16 * ((ib - 2) // 2 + 1))
                    dqW.inc(e.tensor_tensor(out=wrow[0:64, ib % 2, o0:o0 + sz],
                                            in0=tt8[:, o0:o0 + sz],
                                            in1=pSC[0:64, k % 2, 0:sz], op=AL.mult))
                    dqW.inc(e.scalar_tensor_tensor(
                        out=wrow[64:128, ib % 2, o0:o0 + sz],
                        in0=tb8[:, o0:o0 + sz], scalar=-8.0,
                        in1=pSC[64:128, k % 2, 0:sz],
                        op0=AL.add, op1=AL.mult))

        @blk.tensor
        def _(e):
            for ib in range(IBN):
                for k, (o0, sz) in enumerate(chunks):
                    i = ib * NCHK + k
                    e.wait_ge(dqV.h, 4 * ib + 2)
                    if i >= 2:
                        e.wait_ge(dqW.h, 2 * (i - 1))
                    dqB.inc(e.matmul(pSC[:, k % 2, 0:sz], st.e34b[:, :],
                                     ssp[:, o0:o0 + sz], start=True, stop=True))

        @blk.gpsimd
        def _(e):
            for ib in range(IBN):
                e.wait_ge(dqW.h, (ib + 1) * 2 * NCHK)
                st.gpool[ib % 2].inc(e.dma_start(
                    out=WT[ib * 128:(ib + 1) * 128, :],
                    in_=wrow[:, ib % 2, :]), 16)

        phase_barrier(st)


def phase_ffn1(st, qc):
    c, nc, blk = st.cfg, st.nc, st.blk
    dS = st.sems["dS"]
    (fAM, fBM, fSI, fHV) = _mk_sems(st, ["ffam", "ffbm", "ffsi", "ffhv"])
    FQC = c.FQC
    hsb = st.hsb

    with contextlib.ExitStack() as ph:
        xl2 = ph.enter_context(nc.sbuf_tensor(f"xl2_{qc}", [128, c.FB, FQC], BF16))
        w13 = ph.enter_context(nc.sbuf_tensor(f"w13_{qc}", [128, 2, 2, c.FB, 128], BF16))
        asil = ph.enter_context(nc.sbuf_tensor(f"asil_{qc}", [128, FQC], BF16))
        pA = ph.enter_context(nc.psum_tensor(f"pA_{qc}", [128, 2, 512], F32))
        pB = ph.enter_context(nc.psum_tensor(f"pBf_{qc}", [128, 2, 512], F32))

        d_s0 = dS.n
        wb13 = [st.spool[0].n, st.spool[1].n]
        am0, bm0, si0, hv0 = fAM.n, fBM.n, fSI.n, fHV.n

        @blk.sync
        def _(e):
            dS.inc(e.dma_start(
                out=xl2[:, :, :],
                in_=st.XL2T[:, qc * FQC:(qc + 1) * FQC].rearrange(
                    "(fb p) t -> p fb t", p=128)), 16)
            for hb in range(c.FHB):
                if hb >= 2:
                    e.wait_ge(fBM.h, bm0 + hb - 1)
                st.spool[hb % 2].inc(e.dma_start(
                    out=w13[:, hb % 2, 0, :, :],
                    in_=st.W1T[:, hb * 128:(hb + 1) * 128].rearrange(
                        "(fb p) h -> p fb h", p=128)), 16)
                st.spool[hb % 2].inc(e.dma_start(
                    out=w13[:, hb % 2, 1, :, :],
                    in_=st.W3T[:, hb * 128:(hb + 1) * 128].rearrange(
                        "(fb p) h -> p fb h", p=128)), 16)

        @blk.tensor
        def _(e):
            e.wait_ge(dS.h, d_s0 + 16)
            for hb in range(c.FHB):
                e.wait_ge(st.spool[hb % 2].h,
                          wb13[hb % 2] + 32 * (hb // 2 + 1))
                if hb >= 2:
                    e.wait_ge(fSI.h, si0 + hb - 1)
                    e.wait_ge(fHV.h, hv0 + hb - 1)
                for fb in range(c.FB):
                    ins = e.matmul(pA[:, hb % 2, 0:FQC], w13[:, hb % 2, 0, fb, :],
                                   xl2[:, fb, :], start=(fb == 0),
                                   stop=(fb == c.FB - 1))
                fAM.inc(ins)
                for fb in range(c.FB):
                    ins = e.matmul(pB[:, hb % 2, 0:FQC], w13[:, hb % 2, 1, fb, :],
                                   xl2[:, fb, :], start=(fb == 0),
                                   stop=(fb == c.FB - 1))
                fBM.inc(ins)

        @blk.scalar
        def _(e):
            for hb in range(c.FHB):
                e.wait_ge(fAM.h, am0 + hb + 1)
                if hb >= 1:
                    e.wait_ge(fHV.h, hv0 + hb)
                fSI.inc(e.activation(out=asil[:, :], in_=pA[:, hb % 2, 0:FQC],
                                     func=AF.Silu, bias=0.0, scale=1.0))

        @blk.vector
        def _(e):
            for hb in range(c.FHB):
                e.wait_ge(fSI.h, si0 + hb + 1)
                e.wait_ge(fBM.h, bm0 + hb + 1)
                fHV.inc(e.tensor_tensor(out=hsb[:, hb, :], in0=asil[:, :],
                                        in1=pB[:, hb % 2, 0:FQC], op=AL.mult))

        phase_barrier(st)


def phase_ffn2(st, qc):
    c, nc, blk = st.cfg, st.nc, st.blk
    dS, dA, dG = st.sems["dS"], st.sems["dA"], st.sems["dG"]
    (fOM, fOE, fTP, fFIv) = _mk_sems(st, ["ffom", "ffoe", "fftp", "fffiv"])
    FQC = c.FQC
    TPQ = FQC // 128
    JG = c.FB // 4
    hsb = st.hsb

    with contextlib.ExitStack() as ph:
        w2s = ph.enter_context(nc.sbuf_tensor(f"w2s_{qc}", [128, 2, c.FHB, 128], BF16))
        foT = ph.enter_context(nc.sbuf_tensor(f"foT_{qc}", [128, c.FB, FQC], BF16))
        xmi = ph.enter_context(nc.sbuf_tensor(f"xmi_{qc}", [128, c.D], F32))
        osb = ph.enter_context(nc.sbuf_tensor(f"osb_{qc}", [128, 2, 512], F32))
        pO = ph.enter_context(nc.psum_tensor(f"pO_{qc}", [128, 2, 512], F32))
        pTF = ph.enter_context(nc.psum_tensor(f"pTF_{qc}", [128, 2, 512], F32))

        d_a0, d_g0 = dA.n, dG.n
        w2b = [st.apool[0].n, st.apool[1].n]
        ob_out = [st.gpool[0].n, st.gpool[1].n]
        om0, oe0, tp0, fi0 = fOM.n, fOE.n, fTP.n, fFIv.n

        @blk.scalar
        def _(e):
            for i in range(c.FB + 2):
                if i >= 2:
                    e.wait_ge(fOM.h, om0 + i - 1)
                    fOE.inc(e.copy(out=foT[:, i - 2, :],
                                   in_=pO[:, (i - 2) % 2, 0:FQC]))
                if i < c.FB:
                    st.apool[i % 2].inc(e.dma_start(
                        out=w2s[:, i % 2, :, :],
                        in_=st.W2T[:, i * 128:(i + 1) * 128].rearrange(
                            "(hb p) d -> p hb d", p=128)), 16)
            for tq in range(TPQ):
                if tq >= 1:
                    e.wait_ge(fFIv.h, fi0 + tq * JG)
                dA.inc(e.dma_start(
                    out=xmi[:, :],
                    in_=st.XMID[(qc * FQC + tq * 128):(qc * FQC + (tq + 1) * 128),
                                :]), 16)

        @blk.tensor
        def _(e):
            for db in range(c.FB):
                e.wait_ge(st.apool[db % 2].h, w2b[db % 2] + 16 * (db // 2 + 1))
                if db >= 2:
                    e.wait_ge(fOE.h, oe0 + db - 1)
                for hb in range(c.FHB):
                    ins = e.matmul(pO[:, db % 2, 0:FQC], w2s[:, db % 2, hb, :],
                                   hsb[:, hb, :], start=(hb == 0),
                                   stop=(hb == c.FHB - 1))
                fOM.inc(ins)
            for tq in range(TPQ):
                for jg in range(JG):
                    gidx = tq * JG + jg
                    if gidx >= 2:
                        e.wait_ge(fFIv.h, fi0 + gidx - 1)
                    for q in range(4):
                        db = 4 * jg + q
                        e.wait_ge(fOE.h, oe0 + db + 1)
                        ins = e.matmul(pTF[:, gidx % 2, q * 128:(q + 1) * 128],
                                       foT[:, db, tq * 128:(tq + 1) * 128],
                                       st.identb[:, :], start=True, stop=True)
                    fTP.inc(ins)

        @blk.vector
        def _(e):
            for tq in range(TPQ):
                e.wait_ge(dA.h, d_a0 + 16 * (tq + 1))
                for jg in range(JG):
                    gidx = tq * JG + jg
                    e.wait_ge(fTP.h, tp0 + gidx + 1)
                    if gidx >= 2:
                        e.wait_ge(st.gpool[gidx % 2].h,
                                  ob_out[gidx % 2] + 16 * ((gidx - 2) // 2 + 1))
                    fFIv.inc(e.tensor_tensor(
                        out=osb[:, gidx % 2, :],
                        in0=pTF[:, gidx % 2, :],
                        in1=xmi[:, jg * 512:(jg + 1) * 512], op=AL.add))

        @blk.gpsimd
        def _(e):
            for tq in range(TPQ):
                for jg in range(JG):
                    gidx = tq * JG + jg
                    e.wait_ge(fFIv.h, fi0 + gidx + 1)
                    st.gpool[gidx % 2].inc(e.dma_start(
                        out=st.io["out"][(qc * FQC + tq * 128):
                                         (qc * FQC + (tq + 1) * 128),
                                         jg * 512:(jg + 1) * 512],
                        in_=osb[:, gidx % 2, :]), 16)

        phase_barrier(st)


def _alloc_globals(st):
    c, nc = st.cfg, st.nc
    sb = lambda n, s, d: st.es.enter_context(nc.sbuf_tensor(n, s, d))
    st.identf_sb = sb("identf_sb", [128, 128], F32)
    st.identb = sb("identb", [128, 128], BF16)
    st.ones1f = sb("ones1f", [1, 128], F32)
    st.ones32f = sb("ones32f", [32, 1], F32)
    st.ones128f = sb("ones128f", [128, 1], F32)
    st.ones1b = sb("ones1b", [1, 128], BF16)
    st.ones128b = sb("ones128b", [128, 1], BF16)
    st.ones33b = sb("ones33b", [33, 128], BF16)
    st.bjunk = sb("bjunk", [1, 8], F32)
    st.e34b = sb("e34b", [34, 128], BF16)
    st.bT_dq = sb("bT_dq", [128, c.C1B], F32)
    st.bT_dkv = sb("bT_dkv", [128, c.CB], F32)
    st.bT_uk = sb("bT_uk", [128, c.FB], F32)
    st.bT_uq = sb("bT_uq", [128, c.FB], F32)
    st.bT_rq = sb("bT_rq", [128, c.RB], F32)
    st.bT_rk = sb("bT_rk", [64, 1], F32)
    st.brow = sb("brow", [33, c.D], BF16)
    st.epsb = sb("epsb", [128, 1], F32)
    st.negB = sb("negB", [1, 1], BF16)


def build(cfg: Cfg):
    import ml_dtypes
    c = cfg
    nc = bass.Bass("TRN2", target_bir_lowering=False, debug=False)
    st = St()
    st.cfg, st.nc = c, nc
    st.es = contextlib.ExitStack()
    st.sems = {}

    io = {}
    io["x"] = nc.dram_tensor("x", [c.SK, c.D], F32, kind="ExternalInput").ap()
    for nm, shp in [("w_dq", [c.D, c.DC1]), ("w_uq", [c.DC1, c.D]),
                    ("w_dkv", [c.D, c.DC]), ("w_uk", [c.DC, c.D]),
                    ("w_uv", [c.DC, c.D]), ("w_rq", [c.DC1, c.H * c.DR]),
                    ("w_rk", [c.D, c.DR]), ("w_out", [c.D, c.D]),
                    ("b_dq", [c.DC1]), ("b_uq", [c.D]), ("b_dkv", [c.DC]),
                    ("b_uk", [c.D]), ("b_uv", [c.D]), ("b_rq", [c.H * c.DR]),
                    ("b_rk", [c.DR]), ("b_out", [c.D])]:
        io[nm] = nc.dram_tensor(nm, shp, F32, kind="ExternalInput").ap()
    io["w1p"] = nc.dram_tensor("w1p", [c.FB, 64, c.FH], I8, kind="ExternalInput").ap()
    io["w3p"] = nc.dram_tensor("w3p", [c.FB, 64, c.FH], I8, kind="ExternalInput").ap()
    io["w2p"] = nc.dram_tensor("w2p", [c.FHB, 64, c.D], I8, kind="ExternalInput").ap()
    io["s1r"] = nc.dram_tensor("s1r", [c.FB, 2, c.FH], F32, kind="ExternalInput").ap()
    io["s3r"] = nc.dram_tensor("s3r", [c.FB, 2, c.FH], F32, kind="ExternalInput").ap()
    io["s2r"] = nc.dram_tensor("s2r", [c.FHB, 2, c.D], F32, kind="ExternalInput").ap()
    io["out"] = nc.dram_tensor("out", [c.NQ, c.D], F32, kind="ExternalOutput").ap()
    st.io = io

    scratch_kind = "ExternalOutput" if c.debug else "Internal"
    st.XLT = nc.dram_tensor("XLTs", [c.D, c.NQ], BF16, kind=scratch_kind).ap()
    st.KT = nc.dram_tensor("KTs", [c.D, c.SK], BF16, kind=scratch_kind).ap()
    st.VS = nc.dram_tensor("VSs", [c.SK, c.D], BF16, kind=scratch_kind).ap()
    st.CQT = nc.dram_tensor("CQTs", [c.DC1, c.NQ], BF16, kind=scratch_kind).ap()
    st.QT = nc.dram_tensor("QTs", [c.D, c.NQ], BF16, kind=scratch_kind).ap()
    st.QRT = nc.dram_tensor("QRTs", [c.H * c.DR, c.NQ], BF16, kind=scratch_kind).ap()
    st.AOT = nc.dram_tensor("AOTs", [c.D, c.NQ], BF16, kind=scratch_kind).ap()
    st.XMID = nc.dram_tensor("XMIDs", [c.NQ, c.D], F32, kind=scratch_kind).ap()
    st.XL2T = nc.dram_tensor("XL2Ts", [c.D, c.NQ], BF16, kind=scratch_kind).ap()
    st.W1T = nc.dram_tensor("W1Ts", [c.D, c.FH], BF16, kind=scratch_kind).ap()
    st.W3T = nc.dram_tensor("W3Ts", [c.D, c.FH], BF16, kind=scratch_kind).ap()
    st.W2T = nc.dram_tensor("W2Ts", [c.FH, c.D], BF16, kind=scratch_kind).ap()

    bf = ml_dtypes.bfloat16
    st.ident_f = nc.inline_tensor(np.eye(128, dtype=np.float32), "identft")
    st.ident_b = nc.inline_tensor(np.eye(128, dtype=bf), "identbt")
    st.ones_col_f = nc.inline_tensor(np.ones((1, 128), np.float32), "ones1ft")
    st.ones_col_b = nc.inline_tensor(np.ones((1, 128), bf), "ones1bt")
    st.ones128_b = nc.inline_tensor(np.ones((128, 1), bf), "ones128bt")
    o33 = np.zeros((33, 128), dtype=np.float32)
    o33[0, :] = 1.0
    o33[32, :] = 1.0
    st.ones33_b = nc.inline_tensor(o33.astype(bf), "ones33bt")
    st.ones32_f = nc.inline_tensor(np.ones((32, 1), np.float32), "ones32ft")
    st.ones128_f = nc.inline_tensor(np.ones((128, 1), np.float32), "ones128ft")
    e34 = np.zeros((34, 128), dtype=np.float32)
    e34[0, 0:64] = 1.0 / 16.0
    e34[1, 64:128] = 1.0
    e34[32, 0:64] = 1.0 / 16.0
    e34[33, 64:128] = 1.0
    st.e34_t = nc.inline_tensor(e34.astype(bf), "e34t")

    _alloc_globals(st)
    _mk_sems(st, ["dS", "dA", "dG", "bar"])
    st.spool = _mk_sems(st, ["qs0", "qs1"])
    st.apool = _mk_sems(st, ["qa0", "qa1"])
    st.gpool = _mk_sems(st, ["qg0", "qg1", "qg2", "qg3"])
    st.blk = st.es.enter_context(nc.Block())

    phase_p0(st)
    st.krr_es = contextlib.ExitStack()
    st.KRR = st.krr_es.enter_context(nc.sbuf_tensor("KRR", [65, c.SK], BF16))
    st.ckvt_es = contextlib.ExitStack()
    st.CKVT = st.ckvt_es.enter_context(nc.sbuf_tensor("CKVT", [128, c.CB, c.SK], BF16))
    if c.n_phases >= 1:
        phase_a1(st)
    if c.n_phases >= 2:
        phase_a2(st)
    st.ckvt_es.close()
    if c.n_phases >= 3:
        phase_a2b(st)
    if c.n_phases >= 4:
        _phase_a3_generic(st, "a3q", "w_uq", c.FB, st.bT_uq, st.QT, c.SCALER)
        _phase_a3_generic(st, "a3r", "w_rq", c.RB, st.bT_rq, st.QRT, c.SCALER)
    if c.n_phases >= 5:
        phase_att(st)
    st.krr_es.close()
    if c.n_phases >= 6:
        phase_wo(st)
    if c.n_phases >= 7:
        phase_ln2(st)
    if c.n_phases >= 8:
        _phase_deq(st, "dq1", st.io["w1p"], st.io["s1r"], st.W1T, c.FB, c.FH)
    if c.n_phases >= 8.3:
        _phase_deq(st, "dq3", st.io["w3p"], st.io["s3r"], st.W3T, c.FB, c.FH)
    if c.n_phases >= 8.6:
        _phase_deq(st, "dq2", st.io["w2p"], st.io["s2r"], st.W2T, c.FHB, c.D)
    if c.n_phases >= 9:
        st.hsb_es = contextlib.ExitStack()
        st.hsb = st.hsb_es.enter_context(
            nc.sbuf_tensor("hsb", [128, c.FHB, c.FQC], BF16))
        for qc in range(c.NQ // c.FQC):
            phase_ffn1(st, qc)
            phase_ffn2(st, qc)
        st.hsb_es.close()

    st.es.close()
    return nc


def _prep_core_inputs(core, x_full, w):
    b, qb = core // 4, core % 4
    SK = x_full.shape[1]
    NQ = SK // 4
    xb = x_full[b]
    qs, qe = qb * NQ, (qb + 1) * NQ
    x_core = np.concatenate([xb[qs:qe], xb[:qs], xb[qe:]], axis=0)
    d = {"x": np.ascontiguousarray(x_core)}
    d.update(w)
    return d


def _pack_ffn(w_packed_i32, scales, out_feat, in_feat):
    R = w_packed_i32.shape[0]
    assert w_packed_i32.shape == (out_feat * in_feat // 128, 64)
    w8 = w_packed_i32.view(np.int8).reshape(R, 64, 4)[:, :, 0]
    IB = in_feat // 128
    wp = np.ascontiguousarray(
        w8.reshape(out_feat, IB, 64).transpose(1, 2, 0))
    sr = np.ascontiguousarray(
        scales.reshape(out_feat, IB, 2).transpose(1, 2, 0))
    return wp, sr


_FULL = Cfg()


def kernel(**inputs):
    c = _FULL
    x = inputs["L0_attn_sequence"]
    w = {}
    for nm in ["w_dq", "b_dq", "w_uq", "b_uq", "w_dkv", "b_dkv", "w_uk", "b_uk",
               "w_uv", "b_uv", "w_rq", "b_rq", "w_rk", "b_rk", "w_out", "b_out"]:
        w[nm] = np.ascontiguousarray(np.asarray(inputs["L0_attn_" + nm], np.float32))
    w["w1p"], w["s1r"] = _pack_ffn(np.asarray(inputs["L0_ffn_w1"]),
                                   np.asarray(inputs["L0_ffn_s1"], np.float32),
                                   c.FH, c.D)
    w["w3p"], w["s3r"] = _pack_ffn(np.asarray(inputs["L0_ffn_w3"]),
                                   np.asarray(inputs["L0_ffn_s3"], np.float32),
                                   c.FH, c.D)
    w["w2p"], w["s2r"] = _pack_ffn(np.asarray(inputs["L0_ffn_w2"]),
                                   np.asarray(inputs["L0_ffn_s2"], np.float32),
                                   c.D, c.FH)
    x = np.asarray(x, np.float32)

    nc = build(c)
    in_maps = [_prep_core_inputs(core, x, w) for core in range(8)]
    res = run_bass_kernel_spmd(nc, in_maps, core_ids=list(range(8)))

    out = np.empty((2, 8192, 4096), np.float32)
    for core in range(8):
        b, qb = core // 4, core % 4
        out[b, qb * 2048:(qb + 1) * 2048] = res.results[core]["out"]
    return out


# version 04
# speedup vs baseline: 1.3145x; 1.3145x over previous
import contextlib
import math

import numpy as np

import concourse.bass as bass
import concourse.mybir as mybir
from concourse.bass_utils import run_bass_kernel_spmd

F32 = mybir.dt.float32
BF16 = mybir.dt.bfloat16
I8 = mybir.dt.int8
AF = mybir.ActivationFunctionType
AL = mybir.AluOpType
AX = mybir.AxisListType.X

LN_EPS = 1e-5


class Cfg:
    def __init__(self, D=4096, DC=512, DC1=1536, DR=64, H=32, DH=128,
                 SK=8192, NQ=2048, FH=11008, CH=512, QC=512, FQC=512,
                 n_phases=99, debug=False):
        self.D, self.DC, self.DC1, self.DR = D, DC, DC1, DR
        self.H, self.DH, self.SK, self.NQ, self.FH = H, DH, SK, NQ, FH
        self.CH, self.QC, self.FQC = CH, QC, FQC
        self.n_phases = n_phases
        self.debug = debug
        assert D % 128 == 0 and DC % 128 == 0 and DC1 % 128 == 0
        assert DR == 64 and DH == 128 and H * DH == D
        assert SK % CH == 0 and NQ % CH == 0 and CH % 128 == 0
        assert NQ % QC == 0 and QC % 128 == 0 and QC <= 512
        assert FH % 128 == 0 and NQ % FQC == 0 and FQC <= 512
        self.FB = D // 128
        self.CB = DC // 128
        self.C1B = DC1 // 128
        self.RB = (H * DR) // 128
        self.FHB = FH // 128
        self.NCH = SK // CH
        self.NQCH = NQ // CH
        self.TPC = CH // 128
        self.SKB = SK // 128
        self.NQC = NQ // QC
        self.NQB = NQ // 128
        self.SCALER = 1.0 / math.sqrt(DH + DR)


class Sem:
    def __init__(self, nc, es, name):
        self.h = es.enter_context(nc.semaphore(name))
        self.n = 0

    def inc(self, ins, k=1):
        ins.then_inc(self.h, k)
        self.n += k
        return ins


class St:
    pass


def _mk_sems(st, names):
    for nm in names:
        if nm not in st.sems:
            st.sems[nm] = Sem(st.nc, st.es, nm)
    return [st.sems[nm] for nm in names]


def phase_barrier(st):
    blk, bar = st.blk, st.sems["bar"]
    dS, dA, dG = st.sems["dS"], st.sems["dA"], st.sems["dG"]
    target = bar.n + 5
    flushes = {"sync": [(dS.h, dS.n)] + [(s.h, s.n) for s in st.spool],
               "scalar": [(dA.h, dA.n)] + [(s.h, s.n) for s in st.apool],
               "gpsimd": [(dG.h, dG.n)] + [(s.h, s.n) for s in st.gpool]}

    @blk.sync
    def _(e):
        for h, n in flushes["sync"]:
            e.wait_ge(h, n)
        bar.inc(e.nop())
        e.wait_ge(bar.h, target)

    @blk.scalar
    def _(e):
        for h, n in flushes["scalar"]:
            e.wait_ge(h, n)
        bar.inc(e.memzero(st.bjunk[0:1, 1:2]))
        e.wait_ge(bar.h, target)

    @blk.gpsimd
    def _(e):
        for h, n in flushes["gpsimd"]:
            e.wait_ge(h, n)
        bar.inc(e.nop())
        e.wait_ge(bar.h, target)

    @blk.vector
    def _(e):
        bar.inc(e.memset(st.bjunk[0:1, 0:1], 0.0))
        e.wait_ge(bar.h, target)

    @blk.tensor
    def _(e):
        bar.inc(e.nop())
        e.wait_ge(bar.h, target)


def phase_p0(st):
    c, nc, blk = st.cfg, st.nc, st.blk
    dS, dG, tQ, sQ = _mk_sems(st, ["dS", "dG", "p0t", "p0s"])

    with contextlib.ExitStack() as ph:
        bstg = ph.enter_context(nc.sbuf_tensor("bstg", [max(c.FB, c.C1B, c.RB), 128], F32))
        pT0 = ph.enter_context(nc.psum_tensor("pT0", [128, 128], F32))

        @blk.sync
        def _(e):
            dS.inc(e.dma_start(out=st.identf_sb[:, :], in_=st.ident_f[:, :]), 16)
            dS.inc(e.dma_start(out=st.identb[:, :], in_=st.ident_b[:, :]), 16)
            dS.inc(e.dma_start(out=st.ones1f[:, :], in_=st.ones_col_f[:, :]), 16)
            dS.inc(e.dma_start(out=st.ones32f[:, :], in_=st.ones32_f[:, :]), 16)
            dS.inc(e.dma_start(out=st.ones128f[:, :], in_=st.ones128_f[:, :]), 16)
            dS.inc(e.dma_start(out=st.ones1b[:, :], in_=st.ones_col_b[:, :]), 16)
            dS.inc(e.dma_start(out=st.ones128b[:, :], in_=st.ones128_b[:, :]), 16)
            dS.inc(e.dma_start(out=st.ones33b[:, :], in_=st.ones33_b[:, :]), 16)
            dS.inc(e.dma_start(out=st.e34b[:, :], in_=st.e34_t[:, :]), 16)

        @blk.gpsimd
        def _(e):
            dG.inc(e.dma_start(out=st.brow[0:1, :], in_=st.io["b_uv"][None, :]), 16)
            dG.inc(e.dma_start(out=st.brow[32:33, :], in_=st.io["b_out"][None, :]), 16)

        @blk.vector
        def _(e):
            e.memset(st.epsb[:, :], LN_EPS)

        bias_jobs = [("b_dq", c.C1B, st.bT_dq), ("b_dkv", c.CB, st.bT_dkv),
                     ("b_uk", c.FB, st.bT_uk), ("b_uq", c.FB, st.bT_uq),
                     ("b_rq", c.RB, st.bT_rq)]
        for name, nb, bdst in bias_jobs:
            s_before = sQ.n

            @blk.sync
            def _(e, name=name, nb=nb, s_before=s_before):
                e.wait_ge(sQ.h, s_before)
                dS.inc(e.dma_start(
                    out=bstg[0:nb, :],
                    in_=st.io[name].rearrange("(nb p) -> nb p", p=128)), 16)

            d_need, s_need = dS.n, sQ.n

            @blk.tensor
            def _(e, nb=nb, d_need=d_need, s_need=s_need):
                e.wait_ge(dS.h, d_need)
                e.wait_ge(sQ.h, s_need)
                tQ.inc(e.matmul(pT0[:, 0:nb], bstg[0:nb, :],
                                st.identf_sb[0:nb, 0:nb], start=True, stop=True))

            t_need = tQ.n

            @blk.scalar
            def _(e, nb=nb, bdst=bdst, t_need=t_need):
                e.wait_ge(tQ.h, t_need)
                sQ.inc(e.copy(out=bdst[:, 0:nb], in_=pT0[:, 0:nb]))

        s_before = sQ.n

        @blk.sync
        def _(e, s_before=s_before):
            e.wait_ge(sQ.h, s_before)
            dS.inc(e.dma_start(out=bstg[0:1, 0:64], in_=st.io["b_rk"][None, :]), 16)

        d_need, s_need = dS.n, sQ.n

        @blk.tensor
        def _(e, d_need=d_need, s_need=s_need):
            e.wait_ge(dS.h, d_need)
            e.wait_ge(sQ.h, s_need)
            tQ.inc(e.matmul(pT0[0:64, 0:1], bstg[0:1, 0:64],
                            st.identf_sb[0:1, 0:1], start=True, stop=True))

        t_need = tQ.n

        @blk.scalar
        def _(e, t_need=t_need):
            e.wait_ge(tQ.h, t_need)
            sQ.inc(e.copy(out=st.bT_rk[:, :], in_=pT0[0:64, 0:1]))

        phase_barrier(st)


def phase_a1(st):
    c, nc, blk = st.cfg, st.nc, st.blk
    dS, dG = st.sems["dS"], st.sems["dG"]
    (aST, aSQ, aAP, aTP, aTE, aMM, aCE, aKE, vCH) = _mk_sems(
        st, ["a1stat", "a1sqrt", "a1apply", "a1tpose", "a1tpev",
             "a1mm", "a1cev", "a1krev", "a1vch"])
    NG = c.D // 512
    NT = c.NCH * c.TPC
    GPT = c.FB // 4

    with contextlib.ExitStack() as ph:
        w_dkv_b = ph.enter_context(nc.sbuf_tensor("w_dkv_b", [128, c.FB, c.DC], BF16))
        w_rk_b = ph.enter_context(nc.sbuf_tensor("w_rk_b", [128, c.FB, c.DR], BF16))
        xs = ph.enter_context(nc.sbuf_tensor("xs", [128, 2, c.D], F32))
        xlb = ph.enter_context(nc.sbuf_tensor("xlb", [128, 2, c.D], BF16))
        xlT = ph.enter_context(nc.sbuf_tensor("xlT", [128, c.FB, c.CH], BF16))
        stat = ph.enter_context(nc.sbuf_tensor("stat", [128, 8], F32))
        mv6 = ph.enter_context(nc.sbuf_tensor("mv6", [128, NG, 6], F32))
        pT1 = ph.enter_context(nc.psum_tensor("pT1", [128, 2, 512], F32))
        pM1 = ph.enter_context(nc.psum_tensor("pM1", [128, 2, 512], F32))
        pK1 = ph.enter_context(nc.psum_tensor("pK1", [64, 512], F32))

        d_w0 = dG.n

        @blk.gpsimd
        def _(e):
            dG.inc(e.dma_start(out=w_dkv_b[:, :, :],
                               in_=st.io["w_dkv"].rearrange("(fb p) k -> p fb k", p=128)), 16)
            dG.inc(e.dma_start(out=w_rk_b[:, :, :],
                               in_=st.io["w_rk"].rearrange("(fb p) k -> p fb k", p=128)), 16)
            if c.debug:
                pass

        d_wready = dG.n
        d_x0 = dS.n

        def spills_done(ci):
            return d_wready + 16 * (min(ci + 1, c.NQCH))

        xb0, xb1 = st.spool[0].n, st.spool[1].n

        @blk.sync
        def _(e):
            for g in range(NT):
                if g >= 2:
                    e.wait_ge(aAP.h, g - 1)
                st.spool[g % 2].inc(e.dma_start(
                    out=xs[:, g % 2, :],
                    in_=st.io["x"][g * 128:(g + 1) * 128, :]), 16)

        def xload_done(g):
            return (xb0 if g % 2 == 0 else xb1) + 16 * (g // 2 + 1)

        @blk.vector
        def _(e):
            for ci in range(c.NCH):
                for t in range(c.TPC):
                    g = ci * c.TPC + t
                    e.wait_ge(st.spool[g % 2].h, xload_done(g))
                    for ng in range(NG):
                        ins = e.bn_stats(out=mv6[:, ng, :],
                                         in_=xs[:, g % 2, ng * 512:(ng + 1) * 512])
                    vCH.inc(ins)
                    e.wait_ge(vCH.h, vCH.n)
                    aST.inc(e.bn_aggr(out=stat[:, 0:2], in_=mv6[:, :, :]))
                    e.wait_ge(aSQ.h, g + 1)
                    vCH.inc(e.reciprocal(out=stat[:, 3:4], in_=stat[:, 2:3]))
                    e.wait_ge(vCH.h, vCH.n)
                    vCH.inc(e.tensor_scalar(out=stat[:, 4:5], in0=stat[:, 0:1],
                                    scalar1=stat[:, 3:4], scalar2=-1.0,
                                    op0=AL.mult, op1=AL.mult))
                    e.wait_ge(vCH.h, vCH.n)
                    if g >= 2:
                        e.wait_ge(aTP.h, (g - 1) * GPT)
                    aAP.inc(e.tensor_scalar(out=xlb[:, g % 2, :], in0=xs[:, g % 2, :],
                                            scalar1=stat[:, 4:5], scalar2=stat[:, 3:4],
                                            op0=AL.add, op1=AL.mult))
                for cb in range(c.CB):
                    e.wait_ge(aMM.h, ci * (c.CB + 1) + cb + 1)
                    aCE.inc(e.tensor_scalar(
                        out=st.CKVT[:, cb, ci * c.CH:(ci + 1) * c.CH],
                        in0=pM1[:, cb % 2, :], scalar1=st.bT_dkv[:, cb:cb + 1],
                        scalar2=None, op0=AL.add))
                e.wait_ge(aMM.h, (ci + 1) * (c.CB + 1))
                aKE.inc(e.tensor_scalar(
                    out=st.KRR[0:64, ci * c.CH:(ci + 1) * c.CH],
                    in0=pK1[:, :], scalar1=st.bT_rk[:, 0:1],
                    scalar2=None, op0=AL.add))
                if ci == c.NCH - 1:
                    aKE.inc(e.memset(st.KRR[64:65, :], 1.0))

        @blk.scalar
        def _(e):
            for ci in range(c.NCH):
                for t in range(c.TPC):
                    g = ci * c.TPC + t
                    e.wait_ge(aST.h, g + 1)
                    aSQ.inc(e.activation(out=stat[:, 2:3], in_=stat[:, 1:2],
                                         func=AF.Sqrt, bias=st.epsb[:, 0:1], scale=1.0))
                    for j in range(GPT):
                        gidx = g * GPT + j
                        e.wait_ge(aTP.h, gidx + 1)
                        if t == 0 and j == 0 and ci >= 1:
                            e.wait_ge(aMM.h, ci * (c.CB + 1))
                            if ci - 1 < c.NQCH:
                                e.wait_ge(dG.h, spills_done(ci - 1))
                        aTE.inc(e.copy(
                            out=xlT[:, 4 * j:4 * j + 4, t * 128:(t + 1) * 128],
                            in_=pT1[:, gidx % 2, :].rearrange("p (b t) -> p b t", b=4)))

        @blk.tensor
        def _(e):
            e.wait_ge(dG.h, d_wready)
            for ci in range(c.NCH):
                for t in range(c.TPC):
                    g = ci * c.TPC + t
                    e.wait_ge(aAP.h, g + 1)
                    for j in range(GPT):
                        gidx = g * GPT + j
                        if gidx >= 2:
                            e.wait_ge(aTE.h, gidx - 1)
                        for q in range(4):
                            fb = 4 * j + q
                            tQ_ins = e.matmul(
                                pT1[:, gidx % 2, q * 128:(q + 1) * 128],
                                xlb[:, g % 2, fb * 128:(fb + 1) * 128],
                                st.identb[:, :], start=True, stop=True)
                            if q == 3:
                                aTP.inc(tQ_ins)
                e.wait_ge(aTE.h, (ci + 1) * c.TPC * GPT)
                for cb in range(c.CB):
                    mmi = ci * (c.CB + 1) + cb
                    if mmi >= 2:
                        e.wait_ge(aCE.h, ci * c.CB + cb - 1)
                    for fb in range(c.FB):
                        ins = e.matmul(pM1[:, cb % 2, :],
                                       w_dkv_b[:, fb, cb * 128:(cb + 1) * 128],
                                       xlT[:, fb, :],
                                       start=(fb == 0), stop=(fb == c.FB - 1))
                    aMM.inc(ins)
                if ci >= 1:
                    e.wait_ge(aKE.h, ci)
                for fb in range(c.FB):
                    ins = e.matmul(pK1[:, :], w_rk_b[:, fb, :], xlT[:, fb, :],
                                   start=(fb == 0), stop=(fb == c.FB - 1))
                aMM.inc(ins)

        import os as _osa

        @blk.gpsimd
        def _(e):
            for ci in range(c.NQCH):
                e.wait_ge(aTE.h, (ci + 1) * c.TPC * GPT)
                if not _osa.environ.get("K_A1_NOSPILL", ""):
                    dG.inc(e.dma_start(
                        out=st.XLT[:, ci * c.CH:(ci + 1) * c.CH].rearrange(
                            "(fb p) t -> p fb t", p=128),
                        in_=xlT[:, :, :]), 16)

        phase_barrier(st)

        if c.debug:
            _dump_debug(st, [("dbg_ckvt", st.CKVT, [128, c.CB, c.SK], BF16),
                             ("dbg_krr", st.KRR, [65, c.SK], BF16)])


def _dump_debug(st, items):
    c, nc, blk = st.cfg, st.nc, st.blk
    dG = st.sems["dG"]
    phase_barrier(st)

    @blk.gpsimd
    def _(e):
        for name, sbten, shape, dt in items:
            dten = nc.dram_tensor(name, shape, dt, kind="ExternalOutput").ap()
            dG.inc(e.dma_start(out=dten[tuple(slice(None) for _ in shape)],
                               in_=sbten[tuple(slice(None) for _ in shape)]), 16)

    phase_barrier(st)


def phase_a2(st):
    c, nc, blk = st.cfg, st.nc, st.blk
    dS, dG = st.sems["dS"], st.sems["dG"]
    (kMM, kEV, kSQ, rSQ, nMM, vMM, vEV, mxA, nK) = _mk_sems(
        st, ["a2kmm", "a2kev", "a2ksq", "a2krsq", "a2nmm",
             "a2vmm", "a2vev", "a2mx", "a2nk"])
    NDC = c.D // 512

    with contextlib.ExitStack() as ph:
        w_uk_b = ph.enter_context(nc.sbuf_tensor("w_uk_b", [128, c.CB, c.D], BF16))
        w_uv_b = ph.enter_context(nc.sbuf_tensor("w_uv_b", [128, c.CB, c.D], BF16))
        kbuf = ph.enter_context(nc.sbuf_tensor("kbuf", [128, 8, 512], BF16))
        ksq = ph.enter_context(nc.sbuf_tensor("ksq", [128, 4, 512], BF16))
        krsq = ph.enter_context(nc.sbuf_tensor("krsq", [64, 512], BF16))
        vbuf = ph.enter_context(nc.sbuf_tensor("vbuf", [128, 2, c.D], BF16))
        mxacc = ph.enter_context(nc.sbuf_tensor("mxacc", [1, 512], F32))
        mxs = ph.enter_context(nc.sbuf_tensor("mxs", [1, 2], F32))
        pK = ph.enter_context(nc.psum_tensor("pK", [128, 2, 512], F32))
        pV = ph.enter_context(nc.psum_tensor("pV", [128, 2, 512], F32))
        pN = ph.enter_context(nc.psum_tensor("pN", [1, 512], F32))

        d_w0 = dG.n

        @blk.gpsimd
        def _(e):
            dG.inc(e.dma_start(out=w_uk_b[:, :, :],
                               in_=st.io["w_uk"].rearrange("(cb p) k -> p cb k", p=128)), 16)
            dG.inc(e.dma_start(out=w_uv_b[:, :, :],
                               in_=st.io["w_uv"].rearrange("(cb p) k -> p cb k", p=128)), 16)

        d_wready = dG.n
        KGR = c.FB // 4
        DPC = KGR + c.TPC

        kb = [st.gpool[0].n, st.gpool[1].n]
        vb = [st.gpool[2].n, st.gpool[3].n]

        def kgidx(i_fb):
            ci_, gr_ = i_fb // c.FB, (i_fb % c.FB) // 4
            return ci_ * KGR + gr_

        def kdma_sem(i_fb):
            g = kgidx(i_fb)
            return st.gpool[g % 2], kb[g % 2] + 16 * (g // 2 + 1)

        def vdma_sem(jj):
            return st.gpool[2 + jj % 2], vb[jj % 2] + 16 * (jj // 2 + 1)

        @blk.vector
        def _(e):
            e.memset(mxacc[:, :], 0.0)
            for ci in range(c.NCH):
                sl = slice(ci * c.CH, (ci + 1) * c.CH)
                for fb in range(c.FB):
                    i = ci * c.FB + fb
                    e.wait_ge(kMM.h, i + 1)
                    if i >= 8:
                        s_, v_ = kdma_sem(i - 8)
                        e.wait_ge(s_.h, v_)
                    kEV.inc(e.tensor_scalar(out=kbuf[:, i % 8, :],
                                            in0=pK[:, i % 2, :],
                                            scalar1=st.bT_uk[:, fb:fb + 1],
                                            scalar2=None, op0=AL.add))
                    e.wait_ge(kEV.h, kEV.n)
                    if i >= 4:
                        e.wait_ge(nK.h, i - 3)
                    kSQ.inc(e.tensor_tensor(out=ksq[:, i % 4, :],
                                            in0=kbuf[:, i % 8, :],
                                            in1=kbuf[:, i % 8, :], op=AL.mult))
                if ci >= 1:
                    e.wait_ge(nMM.h, ci)
                rSQ.inc(e.tensor_tensor(out=krsq[:, :], in0=st.KRR[0:64, sl],
                                        in1=st.KRR[0:64, sl], op=AL.mult))
                for tt in range(c.TPC):
                    for dc in range(NDC):
                        i = (ci * c.TPC + tt) * NDC + dc
                        jj = ci * c.TPC + tt
                        e.wait_ge(vMM.h, i + 1)
                        if dc == 0 and jj >= 2:
                            s_, v_ = vdma_sem(jj - 2)
                            e.wait_ge(s_.h, v_)
                        vEV.inc(e.tensor_copy(out=vbuf[:, tt % 2, dc * 512:(dc + 1) * 512],
                                              in_=pV[:, i % 2, :]))
                e.wait_ge(nMM.h, ci + 1)
                mxA.inc(e.tensor_tensor(out=mxacc[:, :], in0=mxacc[:, :],
                                        in1=pN[:, :], op=AL.max))
            e.wait_ge(mxA.h, c.NCH)
            mxA.inc(e.reduce_max(out=mxs[:, 0:1], in_=mxacc[:, :], axis=AX))

        @blk.scalar
        def _(e):
            e.wait_ge(mxA.h, c.NCH + 1)
            mxA.inc(e.activation(out=mxs[:, 1:2], in_=mxs[:, 0:1], func=AF.Sqrt,
                                 bias=0.0, scale=1.0))

        @blk.vector
        def _(e):
            e.wait_ge(mxA.h, c.NCH + 2)
            mxA.inc(e.tensor_scalar(out=st.negB[:, :], in0=mxs[:, 1:2],
                                    scalar1=-1.05, scalar2=None, op0=AL.mult))

        @blk.tensor
        def _(e):
            e.wait_ge(dG.h, d_wready)
            for ci in range(c.NCH):
                sl = slice(ci * c.CH, (ci + 1) * c.CH)
                for fb in range(c.FB):
                    i = ci * c.FB + fb
                    if i >= 2:
                        e.wait_ge(kEV.h, i - 1)
                    for cb in range(c.CB):
                        ins = e.matmul(pK[:, i % 2, :],
                                       w_uk_b[:, cb, fb * 128:(fb + 1) * 128],
                                       st.CKVT[:, cb, sl],
                                       start=(cb == 0), stop=(cb == c.CB - 1))
                    kMM.inc(ins)
                    nfb = fb - 4
                    if nfb >= 0:
                        ni = ci * c.FB + nfb
                        e.wait_ge(kSQ.h, ni + 1)
                        if nfb == 0 and ci >= 1:
                            e.wait_ge(mxA.h, ci)
                        nK.inc(e.matmul(pN[:, :], st.ones128b[:, :], ksq[:, ni % 4, :],
                                        start=(nfb == 0), stop=False))
                for nfb in range(max(c.FB - 4, 0), c.FB):
                    ni = ci * c.FB + nfb
                    e.wait_ge(kSQ.h, ni + 1)
                    if nfb == 0 and ci >= 1:
                        e.wait_ge(mxA.h, ci)
                    nK.inc(e.matmul(pN[:, :], st.ones128b[:, :], ksq[:, ni % 4, :],
                                    start=(nfb == 0), stop=False))
                e.wait_ge(rSQ.h, ci + 1)
                nMM.inc(e.matmul(pN[:, :], st.ones128b[0:64, :], krsq[:, :],
                                 start=False, stop=True))
                for tt in range(c.TPC):
                    tsl = slice(ci * c.CH + tt * 128, ci * c.CH + (tt + 1) * 128)
                    for dc in range(NDC):
                        i = (ci * c.TPC + tt) * NDC + dc
                        if i >= 2:
                            e.wait_ge(vEV.h, i - 1)
                        for cb in range(c.CB):
                            e.matmul(pV[:, i % 2, :],
                                     st.CKVT[:, cb, tsl],
                                     w_uv_b[:, cb, dc * 512:(dc + 1) * 512],
                                     start=(cb == 0), stop=False)
                        vMM.inc(e.matmul(pV[:, i % 2, :], st.ones1b[:, :],
                                         st.brow[0:1, dc * 512:(dc + 1) * 512],
                                         start=False, stop=True))

        @blk.gpsimd
        def _(e):
            for ci in range(c.NCH):
                for gr in range(KGR):
                    g = ci * KGR + gr
                    e.wait_ge(kEV.h, ci * c.FB + 4 * gr + 4)
                    st.gpool[g % 2].inc(e.dma_start(
                        out=st.KT[4 * gr * 128:(4 * gr + 4) * 128,
                                  ci * c.CH:(ci + 1) * c.CH].rearrange(
                                      "(fb p) t -> p fb t", p=128),
                        in_=kbuf[:, (4 * g) % 8:(4 * g) % 8 + 4, :]), 16)
                for tt in range(c.TPC):
                    jj = ci * c.TPC + tt
                    e.wait_ge(vEV.h, (jj + 1) * NDC)
                    st.gpool[2 + jj % 2].inc(e.dma_start(
                        out=st.VS[ci * c.CH + tt * 128:ci * c.CH + (tt + 1) * 128, :],
                        in_=vbuf[:, tt % 2, :]), 16)

        phase_barrier(st)


def phase_a2b(st):
    c, nc, blk = st.cfg, st.nc, st.blk
    dS, dG = st.sems["dS"], st.sems["dG"]
    (bMM, bEV) = _mk_sems(st, ["a2bmm", "a2bev"])

    with contextlib.ExitStack() as ph:
        w_dq_b = ph.enter_context(nc.sbuf_tensor("w_dq_b", [128, c.FB, c.DC1], BF16))
        xq = ph.enter_context(nc.sbuf_tensor("xq", [128, 2, c.FB, c.CH], BF16))
        cqb = ph.enter_context(nc.sbuf_tensor("cqb", [128, 1, c.C1B, c.CH], BF16))
        pB = ph.enter_context(nc.psum_tensor("pB2", [128, 2, 512], F32))

        d_w0 = dG.n

        @blk.gpsimd
        def _(e):
            dG.inc(e.dma_start(out=w_dq_b[:, :, :],
                               in_=st.io["w_dq"].rearrange("(fb p) k -> p fb k", p=128)), 16)
            for qch in range(c.NQCH):
                e.wait_ge(bEV.h, (qch + 1) * c.C1B)
                dG.inc(e.dma_start(
                    out=st.CQT[:, qch * c.CH:(qch + 1) * c.CH].rearrange(
                        "(cb p) t -> p cb t", p=128),
                    in_=cqb[:, 0, :, :]), 16)

        d_wready = d_w0 + 16
        d_x0 = dS.n

        qb0, qb1 = st.spool[0].n, st.spool[1].n

        @blk.sync
        def _(e):
            for qch in range(c.NQCH):
                if qch >= 2:
                    e.wait_ge(bMM.h, (qch - 1) * c.C1B)
                st.spool[qch % 2].inc(e.dma_start(
                    out=xq[:, qch % 2, :, :],
                    in_=st.XLT[:, qch * c.CH:(qch + 1) * c.CH].rearrange(
                        "(fb p) t -> p fb t", p=128)), 16)

        @blk.tensor
        def _(e):
            e.wait_ge(dG.h, d_wready)
            for qch in range(c.NQCH):
                e.wait_ge(st.spool[qch % 2].h,
                          (qb0 if qch % 2 == 0 else qb1) + 16 * (qch // 2 + 1))
                for cb in range(c.C1B):
                    i = qch * c.C1B + cb
                    if i >= 2:
                        e.wait_ge(bEV.h, i - 1)
                    for fb in range(c.FB):
                        ins = e.matmul(pB[:, i % 2, 0:c.CH],
                                       w_dq_b[:, fb, cb * 128:(cb + 1) * 128],
                                       xq[:, qch % 2, fb, :],
                                       start=(fb == 0), stop=(fb == c.FB - 1))
                    bMM.inc(ins)

        @blk.vector
        def _(e):
            for qch in range(c.NQCH):
                for cb in range(c.C1B):
                    i = qch * c.C1B + cb
                    e.wait_ge(bMM.h, i + 1)
                    if qch >= 1 and cb == 0:
                        e.wait_ge(dG.h, d_wready + 16 * qch)
                    bEV.inc(e.tensor_scalar(out=cqb[:, 0, cb, :],
                                            in0=pB[:, i % 2, 0:c.CH],
                                            scalar1=st.bT_dq[:, cb:cb + 1],
                                            scalar2=None, op0=AL.add))

        phase_barrier(st)


def _phase_a3_generic(st, tag, wname, wblocks, bT, dst, scale):
    c, nc, blk = st.cfg, st.nc, st.blk
    dS, dG = st.sems["dS"], st.sems["dG"]
    (gMM, gEV) = _mk_sems(st, [tag + "mm", tag + "ev"])
    OB = wblocks

    with contextlib.ExitStack() as ph:
        w_b = ph.enter_context(nc.sbuf_tensor(tag + "w", [128, c.C1B, OB * 128], BF16))
        cq = ph.enter_context(nc.sbuf_tensor(tag + "cq", [128, 2, c.C1B, c.CH], BF16))
        g4pre = 4 if OB % 4 == 0 else (2 if OB % 2 == 0 else 1)
        ob_ring = ph.enter_context(nc.sbuf_tensor(tag + "ob", [128, 2 * g4pre, c.CH], BF16))
        pQ = ph.enter_context(nc.psum_tensor(tag + "pq", [128, 2, 512], F32))

        d_w0 = dG.n
        g4 = 4 if OB % 4 == 0 else (2 if OB % 2 == 0 else 1)
        NSLOT = 2 * g4

        @blk.gpsimd
        def _(e):
            dG.inc(e.dma_start(out=w_b[:, :, :],
                               in_=st.io[wname].rearrange("(cb p) k -> p cb k", p=128)), 16)
            for qch in range(c.NQCH):
                for gr in range(OB // g4):
                    e.wait_ge(gEV.h, qch * OB + g4 * gr + g4)
                    dG.inc(e.dma_start(
                        out=dst[g4 * gr * 128:(g4 * gr + g4) * 128,
                                qch * c.CH:(qch + 1) * c.CH].rearrange(
                                    "(ob p) t -> p ob t", p=128),
                        in_=ob_ring[:, g4 * (gr % 2):g4 * (gr % 2) + g4, :]), 16)

        d_wready = d_w0 + 16
        d_x0 = dS.n
        DPQ = OB // g4

        a3_dg_total = d_wready + 16 * c.NQCH * DPQ

        def odma_done(i_ob):
            q_, g_ = i_ob // OB, (i_ob % OB) // g4
            return min(d_wready + 16 * (q_ * DPQ + g_ + 2), a3_dg_total)

        cb0, cb1 = st.spool[0].n, st.spool[1].n

        @blk.sync
        def _(e):
            for qch in range(c.NQCH):
                if qch >= 2:
                    e.wait_ge(gMM.h, (qch - 1) * OB)
                st.spool[qch % 2].inc(e.dma_start(
                    out=cq[:, qch % 2, :, :],
                    in_=st.CQT[:, qch * c.CH:(qch + 1) * c.CH].rearrange(
                        "(cb p) t -> p cb t", p=128)), 16)

        @blk.tensor
        def _(e):
            e.wait_ge(dG.h, d_wready)
            for qch in range(c.NQCH):
                e.wait_ge(st.spool[qch % 2].h,
                          (cb0 if qch % 2 == 0 else cb1) + 16 * (qch // 2 + 1))
                for ob in range(OB):
                    i = qch * OB + ob
                    if i >= 2:
                        e.wait_ge(gEV.h, i - 1)
                    for cb in range(c.C1B):
                        ins = e.matmul(pQ[:, i % 2, 0:c.CH],
                                       w_b[:, cb, ob * 128:(ob + 1) * 128],
                                       cq[:, qch % 2, cb, :],
                                       start=(cb == 0), stop=(cb == c.C1B - 1))
                    gMM.inc(ins)

        @blk.vector
        def _(e):
            for qch in range(c.NQCH):
                for ob in range(OB):
                    i = qch * OB + ob
                    e.wait_ge(gMM.h, i + 1)
                    if i >= NSLOT:
                        e.wait_ge(dG.h, odma_done(i - NSLOT))
                    if scale is not None:
                        gEV.inc(e.tensor_scalar(out=ob_ring[:, i % NSLOT, :],
                                                in0=pQ[:, i % 2, 0:c.CH],
                                                scalar1=bT[:, ob:ob + 1],
                                                scalar2=scale, op0=AL.add,
                                                op1=AL.mult))
                    else:
                        gEV.inc(e.tensor_scalar(out=ob_ring[:, i % NSLOT, :],
                                                in0=pQ[:, i % 2, 0:c.CH],
                                                scalar1=bT[:, ob:ob + 1],
                                                scalar2=None, op0=AL.add))

        phase_barrier(st)


def phase_att(st):
    c, nc, blk = st.cfg, st.nc, st.blk
    dS, dG = st.sems["dS"], st.sems["dG"]
    (nqM, sqN, rk1, rowQ, sMM, eXP, aP4, rcV, bcM, sBC, vOU,
     hSQ) = _mk_sems(st, ["atnq", "atsq", "atr1", "atrow", "atsmm", "atexp",
                          "atap4", "atrcv", "atbcm",
                          "atsbc", "atvou", "athsq"])
    NQC, SKB, QC = c.NQC, c.SKB, c.QC
    e_base = eXP.n
    a_base = aP4.n
    s_base = sMM.n

    def pair_done(gkt):
        return e_base + gkt // 2 + 1

    with contextlib.ExitStack() as ph:
        ksb = ph.enter_context(nc.sbuf_tensor("ksb", [128, 2, c.SK], BF16))
        vsb = ph.enter_context(nc.sbuf_tensor("vsb", [128, 2, SKB, 128], BF16))
        qsb = ph.enter_context(nc.sbuf_tensor("qsb", [128, 2, c.NQ], BF16))
        qrr = ph.enter_context(nc.sbuf_tensor("qrr", [65, 2, c.NQ], BF16))
        qsq = ph.enter_context(nc.sbuf_tensor("qsq", [128, c.NQ], BF16))
        qrsq = ph.enter_context(nc.sbuf_tensor("qrsq", [64, c.NQ], BF16))
        PR = 16
        Pst = ph.enter_context(nc.sbuf_tensor("Pst", [128, PR, QC], BF16))
        sqn = ph.enter_context(nc.sbuf_tensor("sqn", [1, QC], BF16))
        rinv = ph.enter_context(nc.sbuf_tensor("rinv", [1, QC], F32))
        bcst = ph.enter_context(nc.sbuf_tensor("bcst", [128, QC], F32))
        ao = ph.enter_context(nc.sbuf_tensor("ao", [128, 2, c.NQ], BF16))
        pS = ph.enter_context(nc.psum_tensor("pS", [128, 4, 512], F32))
        pAV = ph.enter_context(nc.psum_tensor("pAV", [128, 512], F32))
        pBC = ph.enter_context(nc.psum_tensor("pBC", [128, 512], F32))
        pSm = ph.enter_context(nc.psum_tensor("pSm", [1, 2, 512], F32))

        hb0, hb1 = st.gpool[0].n, st.gpool[1].n
        ab0, ab1 = st.gpool[2].n, st.gpool[3].n

        def hload_done(h):
            return (hb0 if h % 2 == 0 else hb1) + 64 * (h // 2 + 1)

        def aodma_done(j):
            return (ab0 if j % 2 == 0 else ab1) + 16 * (j // 2 + 1)

        @blk.gpsimd
        def _(e):
            for h in range(c.H):
                if h >= 2:
                    e.wait_ge(vOU.h, (h - 2 + 1) * NQC)
                    st.gpool[2 + h % 2].inc(e.dma_start(
                        out=st.AOT[(h - 2) * 128:(h - 1) * 128, :],
                        in_=ao[:, h % 2, :]), 16)
                if h >= 2:
                    e.wait_ge(aP4.h, a_base + (SKB // 4) * (h - 1) * NQC)
                hp = st.gpool[h % 2]
                hp.inc(e.dma_start(out=ksb[:, h % 2, :],
                                   in_=st.KT[h * 128:(h + 1) * 128, :]), 16)
                hp.inc(e.dma_start(out=vsb[:, h % 2, :, :],
                                   in_=st.VS[:, h * 128:(h + 1) * 128].rearrange(
                                       "(kt p) d -> p kt d", p=128)), 16)
                hp.inc(e.dma_start(out=qsb[:, h % 2, :],
                                   in_=st.QT[h * 128:(h + 1) * 128, :]), 16)
                hp.inc(e.dma_start(out=qrr[0:64, h % 2, :],
                                   in_=st.QRT[h * c.DR:(h + 1) * c.DR, :]), 16)
            for h in range(max(c.H - 2, 0), c.H):
                e.wait_ge(vOU.h, (h + 1) * NQC)
                st.gpool[2 + h % 2].inc(e.dma_start(
                    out=st.AOT[h * 128:(h + 1) * 128, :],
                    in_=ao[:, h % 2, :]), 16)

        @blk.vector
        def _(e):
            for h in range(c.H):
                e.wait_ge(st.gpool[h % 2].h, hload_done(h))
                if h >= 1:
                    e.wait_ge(nqM.h, h * NQC)
                hSQ.inc(e.tensor_tensor(out=qsq[:, :], in0=qsb[:, h % 2, :],
                                        in1=qsb[:, h % 2, :], op=AL.mult))
                hSQ.inc(e.tensor_tensor(out=qrsq[:, :], in0=qrr[0:64, h % 2, :],
                                        in1=qrr[0:64, h % 2, :], op=AL.mult))
                for qc in range(NQC):
                    t = h * NQC + qc
                    qsl = slice(qc * QC, (qc + 1) * QC)
                    if qc == 0 and h >= 2:
                        e.wait_ge(st.gpool[2 + h % 2].h, aodma_done(h - 2))
                    e.wait_ge(aP4.h, a_base + (SKB // 4) * (t + 1))
                    rcV.inc(e.reciprocal(out=rinv[:, :], in_=pSm[:, 1, 0:QC]))
                    e.wait_ge(sBC.h, t + 1)
                    vOU.inc(e.tensor_tensor(out=ao[:, h % 2, qsl], in0=pAV[:, 0:QC],
                                            in1=bcst[:, 0:QC], op=AL.mult))

        @blk.scalar
        def _(e):
            for h in range(c.H):
                for qc in range(NQC):
                    t = h * NQC + qc
                    qsl = slice(qc * QC, (qc + 1) * QC)
                    e.wait_ge(nqM.h, t + 1)
                    sqN.inc(e.activation(out=sqn[:, :], in_=pSm[:, 0, 0:QC],
                                         func=AF.Sqrt, bias=0.0, scale=1.0))
                    e.wait_ge(rk1.h, t + 1)
                    rowQ.inc(e.copy(out=qrr[64:65, h % 2, qsl], in_=pSm[:, 0, 0:QC]))
                    for ep in range(SKB // 2):
                        g0 = t * SKB + 2 * ep
                        e.wait_ge(sMM.h, s_base + g0 + 2)
                        if g0 >= PR:
                            e.wait_ge(aP4.h, a_base + (g0 - 15) // 4 + 1)
                        eXP.inc(e.activation(
                            out=Pst[:, g0 % PR:g0 % PR + 2, :],
                            in_=pS[:, (2 * ep) % 4:(2 * ep) % 4 + 2, 0:QC],
                            func=AF.Exp, bias=0.0, scale=1.0))
                    e.wait_ge(bcM.h, t + 1)
                    sBC.inc(e.copy(out=bcst[:, 0:QC], in_=pBC[:, 0:QC]))

        @blk.tensor
        def _(e):
            for h in range(c.H):
                for qc in range(NQC):
                    t = h * NQC + qc
                    qsl = slice(qc * QC, (qc + 1) * QC)
                    e.wait_ge(hSQ.h, 2 * (h + 1))
                    if t >= 1:
                        e.wait_ge(rowQ.h, t)
                    e.matmul(pSm[:, 0, 0:QC], st.ones128b[:, :], qsq[:, qsl],
                             start=True, stop=False)
                    nqM.inc(e.matmul(pSm[:, 0, 0:QC], st.ones128b[0:64, :],
                                     qrsq[:, qsl], start=False, stop=True))
                    e.wait_ge(sqN.h, t + 1)
                    rk1.inc(e.matmul(pSm[:, 0, 0:QC], st.negB[:, :], sqn[:, :],
                                     start=True, stop=True))
                    e.wait_ge(rowQ.h, t + 1)
                    for kt in range(SKB):
                        gkt = t * SKB + kt
                        if gkt >= 4:
                            e.wait_ge(eXP.h, pair_done(gkt - 4))
                        e.matmul(pS[:, kt % 4, 0:QC],
                                 ksb[:, h % 2, kt * 128:(kt + 1) * 128],
                                 qsb[:, h % 2, qsl], start=True, stop=False)
                        sMM.inc(e.matmul(pS[:, kt % 4, 0:QC],
                                         st.KRR[:, kt * 128:(kt + 1) * 128],
                                         qrr[:, h % 2, qsl], start=False, stop=True))
                        if kt >= 3:
                            akt = kt - 3
                            gat = t * SKB + akt
                            e.wait_ge(eXP.h, pair_done(gat))
                            if akt == 0:
                                if t >= 1:
                                    e.wait_ge(vOU.h, t)
                                    e.wait_ge(rcV.h, t)
                            e.matmul(pAV[:, 0:QC], vsb[:, h % 2, akt, :],
                                     Pst[:, gat % PR, :], start=(akt == 0),
                                     stop=(akt == SKB - 1))
                            ins = e.matmul(pSm[:, 1, 0:QC], st.ones128b[:, :],
                                           Pst[:, gat % PR, :], start=(akt == 0),
                                           stop=(akt == SKB - 1))
                            if akt % 4 == 3:
                                aP4.inc(ins)
                    for akt in range(max(SKB - 3, 0), SKB):
                        gat = t * SKB + akt
                        e.wait_ge(eXP.h, pair_done(gat))
                        if akt == 0 and t >= 1:
                            e.wait_ge(vOU.h, t)
                            e.wait_ge(rcV.h, t)
                        e.matmul(pAV[:, 0:QC], vsb[:, h % 2, akt, :],
                                 Pst[:, gat % PR, :], start=(akt == 0),
                                 stop=(akt == SKB - 1))
                        ins = e.matmul(pSm[:, 1, 0:QC], st.ones128b[:, :],
                                       Pst[:, gat % PR, :], start=(akt == 0),
                                       stop=(akt == SKB - 1))
                        if akt % 4 == 3:
                            aP4.inc(ins)
                    e.wait_ge(rcV.h, t + 1)
                    if t >= 1:
                        e.wait_ge(sBC.h, t)
                    bcM.inc(e.matmul(pBC[:, 0:QC], st.ones1f[:, :], rinv[:, :],
                                     start=True, stop=True))

        phase_barrier(st)


def phase_wo(st):
    c, nc, blk = st.cfg, st.nc, st.blk
    dS, dA, dG = st.sems["dS"], st.sems["dA"], st.sems["dG"]
    (wMM, wEV) = _mk_sems(st, ["womm", "woev"])
    NDC = c.D // 512
    NQB = c.NQB

    with contextlib.ExitStack() as ph:
        wo_bf = ph.enter_context(nc.sbuf_tensor("wo_bf", [128, 2, c.FB, 512], BF16))
        aot = ph.enter_context(nc.sbuf_tensor("aot", [128, 2, c.FB, 128], BF16))
        xres = ph.enter_context(nc.sbuf_tensor("xres", [128, 2, 512], F32))
        xm = ph.enter_context(nc.sbuf_tensor("xm", [128, 2, 512], F32))
        pW = ph.enter_context(nc.psum_tensor("pW", [128, 2, 512], F32))

        d_g0, d_s0, d_a0 = dG.n, dS.n, dA.n
        atb = [st.gpool[0].n, st.gpool[1].n]
        xrb = [st.spool[0].n, st.spool[1].n]
        xmb = [st.apool[0].n, st.apool[1].n]

        def wo_done(fc):
            return d_g0 + 16 * (fc + 1)

        def aot_sem(i):
            return st.gpool[i % 2], atb[i % 2] + 16 * (i // 2 + 1)

        def xres_sem(i):
            return st.spool[i % 2], xrb[i % 2] + 16 * (i // 2 + 1)

        def xm_sem(i):
            return st.apool[i % 2], xmb[i % 2] + 16 * (i // 2 + 1)

        @blk.gpsimd
        def _(e):
            for fc in range(NDC):
                if fc >= 2:
                    e.wait_ge(wMM.h, (fc - 1) * NQB)
                dG.inc(e.dma_start(
                    out=wo_bf[:, fc % 2, :, :],
                    in_=st.io["w_out"][:, fc * 512:(fc + 1) * 512].rearrange(
                        "(hd p) f -> p hd f", p=128)), 16)
                for qt in range(NQB):
                    i = fc * NQB + qt
                    if i >= 2:
                        e.wait_ge(wMM.h, i - 1)
                    st.gpool[i % 2].inc(e.dma_start(
                        out=aot[:, qt % 2, :, :],
                        in_=st.AOT[:, qt * 128:(qt + 1) * 128].rearrange(
                            "(hd p) t -> p hd t", p=128)), 16)

        @blk.sync
        def _(e):
            for fc in range(NDC):
                for qt in range(NQB):
                    i = fc * NQB + qt
                    if i >= 2:
                        e.wait_ge(wEV.h, i - 1)
                    st.spool[i % 2].inc(e.dma_start(
                        out=xres[:, i % 2, :],
                        in_=st.io["x"][qt * 128:(qt + 1) * 128,
                                       fc * 512:(fc + 1) * 512]), 16)

        @blk.tensor
        def _(e):
            for fc in range(NDC):
                e.wait_ge(dG.h, wo_done(fc))
                for qt in range(NQB):
                    i = fc * NQB + qt
                    s_, v_ = aot_sem(i)
                    e.wait_ge(s_.h, v_)
                    if i >= 2:
                        e.wait_ge(wEV.h, i - 1)
                    for hd in range(c.FB):
                        e.matmul(pW[:, i % 2, :], aot[:, qt % 2, hd, :],
                                 wo_bf[:, fc % 2, hd, :],
                                 start=(hd == 0), stop=False)
                    wMM.inc(e.matmul(pW[:, i % 2, :], st.ones33b[32:33, :],
                                     st.brow[32:33, fc * 512:(fc + 1) * 512],
                                     start=False, stop=True))

        @blk.vector
        def _(e):
            for fc in range(NDC):
                for qt in range(NQB):
                    i = fc * NQB + qt
                    e.wait_ge(wMM.h, i + 1)
                    s_, v_ = xres_sem(i)
                    e.wait_ge(s_.h, v_)
                    if i >= 2:
                        s2_, v2_ = xm_sem(i - 2)
                        e.wait_ge(s2_.h, v2_)
                    wEV.inc(e.tensor_tensor(out=xm[:, i % 2, :], in0=pW[:, i % 2, :],
                                            in1=xres[:, i % 2, :], op=AL.add))

        @blk.scalar
        def _(e):
            for fc in range(NDC):
                for qt in range(NQB):
                    i = fc * NQB + qt
                    e.wait_ge(wEV.h, i + 1)
                    st.apool[i % 2].inc(e.dma_start(
                        out=st.XMID[qt * 128:(qt + 1) * 128,
                                    fc * 512:(fc + 1) * 512],
                        in_=xm[:, i % 2, :]), 16)

        phase_barrier(st)


def phase_ln2(st):
    c, nc, blk = st.cfg, st.nc, st.blk
    dS, dG = st.sems["dS"], st.sems["dG"]
    (lST, lSQ, lAP, lTP, lTE, vCH) = _mk_sems(
        st, ["l2stat", "l2sqrt", "l2apply", "l2tp", "l2tev", "l2vch"])
    NG = c.D // 512
    GPT = c.FB // 4

    with contextlib.ExitStack() as ph:
        xs = ph.enter_context(nc.sbuf_tensor("xs2", [128, 2, c.D], F32))
        xlb = ph.enter_context(nc.sbuf_tensor("xlb2", [128, 2, c.D], BF16))
        xlT = ph.enter_context(nc.sbuf_tensor("xlT2", [128, 2, c.FB, 128], BF16))
        stat = ph.enter_context(nc.sbuf_tensor("stat2", [128, 8], F32))
        mv6 = ph.enter_context(nc.sbuf_tensor("mv62", [128, NG, 6], F32))
        pT1 = ph.enter_context(nc.psum_tensor("pT2", [128, 2, 512], F32))

        d_s0, d_g0 = dS.n, dG.n

        lb = [st.spool[0].n, st.spool[1].n]
        gb = [st.gpool[0].n, st.gpool[1].n]

        @blk.sync
        def _(e):
            for g in range(c.NQB):
                if g >= 2:
                    e.wait_ge(lAP.h, g - 1)
                st.spool[g % 2].inc(e.dma_start(
                    out=xs[:, g % 2, :],
                    in_=st.XMID[g * 128:(g + 1) * 128, :]), 16)

        @blk.vector
        def _(e):
            for g in range(c.NQB):
                e.wait_ge(st.spool[g % 2].h,
                          lb[g % 2] + 16 * (g // 2 + 1))
                for ng in range(NG):
                    ins = e.bn_stats(out=mv6[:, ng, :],
                                     in_=xs[:, g % 2, ng * 512:(ng + 1) * 512])
                vCH.inc(ins)
                e.wait_ge(vCH.h, vCH.n)
                lST.inc(e.bn_aggr(out=stat[:, 0:2], in_=mv6[:, :, :]))
                e.wait_ge(lSQ.h, g + 1)
                vCH.inc(e.reciprocal(out=stat[:, 3:4], in_=stat[:, 2:3]))
                e.wait_ge(vCH.h, vCH.n)
                vCH.inc(e.tensor_scalar(out=stat[:, 4:5], in0=stat[:, 0:1],
                                        scalar1=stat[:, 3:4], scalar2=-1.0,
                                        op0=AL.mult, op1=AL.mult))
                e.wait_ge(vCH.h, vCH.n)
                if g >= 2:
                    e.wait_ge(lTP.h, (g - 1) * GPT)
                lAP.inc(e.tensor_scalar(out=xlb[:, g % 2, :], in0=xs[:, g % 2, :],
                                        scalar1=stat[:, 4:5], scalar2=stat[:, 3:4],
                                        op0=AL.add, op1=AL.mult))

        @blk.scalar
        def _(e):
            for g in range(c.NQB):
                e.wait_ge(lST.h, g + 1)
                lSQ.inc(e.activation(out=stat[:, 2:3], in_=stat[:, 1:2],
                                     func=AF.Sqrt, bias=st.epsb[:, 0:1], scale=1.0))
                for j in range(GPT):
                    gidx = g * GPT + j
                    e.wait_ge(lTP.h, gidx + 1)
                    if j == 0 and g >= 2:
                        e.wait_ge(st.gpool[g % 2].h,
                                  gb[g % 2] + 16 * ((g - 2) // 2 + 1))
                    lTE.inc(e.copy(
                        out=xlT[:, g % 2, 4 * j:4 * j + 4, :],
                        in_=pT1[:, gidx % 2, :].rearrange("p (b t) -> p b t", b=4)))

        @blk.tensor
        def _(e):
            for g in range(c.NQB):
                e.wait_ge(lAP.h, g + 1)
                for j in range(GPT):
                    gidx = g * GPT + j
                    if gidx >= 2:
                        e.wait_ge(lTE.h, gidx - 1)
                    for q in range(4):
                        fb = 4 * j + q
                        ins = e.matmul(pT1[:, gidx % 2, q * 128:(q + 1) * 128],
                                       xlb[:, g % 2, fb * 128:(fb + 1) * 128],
                                       st.identb[:, :], start=True, stop=True)
                        if q == 3:
                            lTP.inc(ins)

        @blk.gpsimd
        def _(e):
            for g in range(c.NQB):
                e.wait_ge(lTE.h, (g + 1) * GPT)
                st.gpool[g % 2].inc(e.dma_start(
                    out=st.XL2T[:, g * 128:(g + 1) * 128].rearrange(
                        "(fb p) t -> p fb t", p=128),
                    in_=xlT[:, g % 2, :, :]), 16)

        phase_barrier(st)


def _phase_deq(st, tag, wp, sr, WT, IBN, ON):
    c, nc, blk = st.cfg, st.nc, st.blk
    dS, dG = st.sems["dS"], st.sems["dG"]
    (dqB, dqW, dqV, dqM) = _mk_sems(st, [tag + "b", tag + "w", tag + "v",
                                          tag + "m"])
    chunks = []
    o = 0
    while o < ON:
        sz = min(512, ON - o)
        chunks.append((o, sz))
        o += sz
    NCHK = len(chunks)

    with contextlib.ExitStack() as ph:
        pk8 = ph.enter_context(nc.sbuf_tensor(tag + "pk", [64, 2, ON], I8))
        srf = ph.enter_context(nc.sbuf_tensor(tag + "sr", [2, 1, ON], F32))
        ssp = ph.enter_context(nc.sbuf_tensor(tag + "ss", [34, ON], BF16))
        v8 = ph.enter_context(nc.sbuf_tensor(tag + "v8", [64, ON], I8))
        tb8 = ph.enter_context(nc.sbuf_tensor(tag + "tb", [64, ON], I8))
        tt8 = ph.enter_context(nc.sbuf_tensor(tag + "tt", [64, ON], I8))
        wrow = ph.enter_context(nc.sbuf_tensor(tag + "wr", [128, 2, ON], BF16))
        pSC = ph.enter_context(nc.psum_tensor(tag + "ps", [128, 2, 512], F32))

        db_ = [st.spool[0].n, st.spool[1].n]
        gb_deq = [st.gpool[0].n, st.gpool[1].n]

        @blk.sync
        def _(e):
            for ib in range(IBN):
                if ib >= 2:
                    e.wait_ge(dqV.h, 4 * (ib - 1))
                if ib >= 1:
                    e.wait_ge(dqV.h, 4 * (ib - 1) + 2)
                st.spool[ib % 2].inc(e.dma_start(out=pk8[:, ib % 2, :],
                                                 in_=wp[ib, :, :]), 16)
                st.spool[ib % 2].inc(e.dma_start(out=srf[:, 0, :],
                                                 in_=sr[ib, :, :]), 16)

        @blk.vector
        def _(e):
            dqM.inc(e.memset(ssp[:, :], 0.0))
            e.wait_ge(dqM.h, dqM.n)
            for ib in range(IBN):
                e.wait_ge(st.spool[ib % 2].h, db_[ib % 2] + 32 * (ib // 2 + 1))
                if ib >= 1:
                    e.wait_ge(dqB.h, ib * NCHK)
                dqV.inc(e.tensor_copy(out=ssp[0:2, :], in_=srf[:, 0, :]))
                e.wait_ge(dqV.h, dqV.n)
                dqV.inc(e.tensor_tensor(out=ssp[32:34, :], in0=srf[:, 0, :],
                                        in1=ssp[0:2, :], op=AL.subtract))
                if ib >= 1:
                    e.wait_ge(dqW.h, ib * 2 * NCHK)
                dqV.inc(e.tensor_scalar(out=v8[:, :], in0=pk8[:, ib % 2, :],
                                        scalar1=15, scalar2=None, op0=AL.bitwise_and))
                e.wait_ge(dqV.h, dqV.n)
                e.tensor_scalar(out=tb8[:, :], in0=v8[:, :],
                                scalar1=8, scalar2=None, op0=AL.bitwise_xor)
                dqV.inc(e.tensor_tensor(out=tt8[:, :], in0=pk8[:, ib % 2, :],
                                        in1=v8[:, :], op=AL.subtract))
                e.wait_ge(dqV.h, dqV.n)
                for k, (o0, sz) in enumerate(chunks):
                    e.wait_ge(dqB.h, ib * NCHK + k + 1)
                    if ib >= 2 and k == 0:
                        e.wait_ge(st.gpool[ib % 2].h,
                                  gb_deq[ib % 2] + 16 * ((ib - 2) // 2 + 1))
                    dqW.inc(e.tensor_tensor(out=wrow[0:64, ib % 2, o0:o0 + sz],
                                            in0=tt8[:, o0:o0 + sz],
                                            in1=pSC[0:64, k % 2, 0:sz], op=AL.mult))
                    dqW.inc(e.scalar_tensor_tensor(
                        out=wrow[64:128, ib % 2, o0:o0 + sz],
                        in0=tb8[:, o0:o0 + sz], scalar=-8.0,
                        in1=pSC[64:128, k % 2, 0:sz],
                        op0=AL.add, op1=AL.mult))

        @blk.tensor
        def _(e):
            for ib in range(IBN):
                for k, (o0, sz) in enumerate(chunks):
                    i = ib * NCHK + k
                    e.wait_ge(dqV.h, 4 * ib + 2)
                    if i >= 2:
                        e.wait_ge(dqW.h, 2 * (i - 1))
                    dqB.inc(e.matmul(pSC[:, k % 2, 0:sz], st.e34b[:, :],
                                     ssp[:, o0:o0 + sz], start=True, stop=True))

        @blk.gpsimd
        def _(e):
            for ib in range(IBN):
                e.wait_ge(dqW.h, (ib + 1) * 2 * NCHK)
                st.gpool[ib % 2].inc(e.dma_start(
                    out=WT[ib * 128:(ib + 1) * 128, :],
                    in_=wrow[:, ib % 2, :]), 16)

        phase_barrier(st)


def phase_ffn1(st, qc):
    c, nc, blk = st.cfg, st.nc, st.blk
    dS = st.sems["dS"]
    (fAM, fBM, fSI, fHV) = _mk_sems(st, ["ffam", "ffbm", "ffsi", "ffhv"])
    FQC = c.FQC
    hsb = st.hsb

    with contextlib.ExitStack() as ph:
        xl2 = ph.enter_context(nc.sbuf_tensor(f"xl2_{qc}", [128, c.FB, FQC], BF16))
        w13 = ph.enter_context(nc.sbuf_tensor(f"w13_{qc}", [128, 2, 2, c.FB, 128], BF16))
        asil = ph.enter_context(nc.sbuf_tensor(f"asil_{qc}", [128, FQC], BF16))
        pA = ph.enter_context(nc.psum_tensor(f"pA_{qc}", [128, 2, 512], F32))
        pB = ph.enter_context(nc.psum_tensor(f"pBf_{qc}", [128, 2, 512], F32))

        d_s0 = dS.n
        wb13 = [st.spool[0].n, st.spool[1].n]
        am0, bm0, si0, hv0 = fAM.n, fBM.n, fSI.n, fHV.n

        @blk.sync
        def _(e):
            dS.inc(e.dma_start(
                out=xl2[:, :, :],
                in_=st.XL2T[:, qc * FQC:(qc + 1) * FQC].rearrange(
                    "(fb p) t -> p fb t", p=128)), 16)
            for hb in range(c.FHB):
                if hb >= 2:
                    e.wait_ge(fBM.h, bm0 + hb - 1)
                st.spool[hb % 2].inc(e.dma_start(
                    out=w13[:, hb % 2, 0, :, :],
                    in_=st.W1T[:, hb * 128:(hb + 1) * 128].rearrange(
                        "(fb p) h -> p fb h", p=128)), 16)
                st.spool[hb % 2].inc(e.dma_start(
                    out=w13[:, hb % 2, 1, :, :],
                    in_=st.W3T[:, hb * 128:(hb + 1) * 128].rearrange(
                        "(fb p) h -> p fb h", p=128)), 16)

        @blk.tensor
        def _(e):
            e.wait_ge(dS.h, d_s0 + 16)
            for hb in range(c.FHB):
                e.wait_ge(st.spool[hb % 2].h,
                          wb13[hb % 2] + 32 * (hb // 2 + 1))
                if hb >= 2:
                    e.wait_ge(fSI.h, si0 + hb - 1)
                    e.wait_ge(fHV.h, hv0 + hb - 1)
                for fb in range(c.FB):
                    ins = e.matmul(pA[:, hb % 2, 0:FQC], w13[:, hb % 2, 0, fb, :],
                                   xl2[:, fb, :], start=(fb == 0),
                                   stop=(fb == c.FB - 1))
                fAM.inc(ins)
                for fb in range(c.FB):
                    ins = e.matmul(pB[:, hb % 2, 0:FQC], w13[:, hb % 2, 1, fb, :],
                                   xl2[:, fb, :], start=(fb == 0),
                                   stop=(fb == c.FB - 1))
                fBM.inc(ins)

        @blk.scalar
        def _(e):
            for hb in range(c.FHB):
                e.wait_ge(fAM.h, am0 + hb + 1)
                if hb >= 1:
                    e.wait_ge(fHV.h, hv0 + hb)
                fSI.inc(e.activation(out=asil[:, :], in_=pA[:, hb % 2, 0:FQC],
                                     func=AF.Silu, bias=0.0, scale=1.0))

        @blk.vector
        def _(e):
            for hb in range(c.FHB):
                e.wait_ge(fSI.h, si0 + hb + 1)
                e.wait_ge(fBM.h, bm0 + hb + 1)
                fHV.inc(e.tensor_tensor(out=hsb[:, hb, :], in0=asil[:, :],
                                        in1=pB[:, hb % 2, 0:FQC], op=AL.mult))

        phase_barrier(st)


def phase_ffn2(st, qc):
    c, nc, blk = st.cfg, st.nc, st.blk
    dS, dA, dG = st.sems["dS"], st.sems["dA"], st.sems["dG"]
    (fOM, fOE, fTP, fFIv) = _mk_sems(st, ["ffom", "ffoe", "fftp", "fffiv"])
    FQC = c.FQC
    TPQ = FQC // 128
    JG = c.FB // 4
    hsb = st.hsb

    with contextlib.ExitStack() as ph:
        w2s = ph.enter_context(nc.sbuf_tensor(f"w2s_{qc}", [128, 2, c.FHB, 128], BF16))
        foT = ph.enter_context(nc.sbuf_tensor(f"foT_{qc}", [128, c.FB, FQC], BF16))
        xmi = ph.enter_context(nc.sbuf_tensor(f"xmi_{qc}", [128, c.D], F32))
        osb = ph.enter_context(nc.sbuf_tensor(f"osb_{qc}", [128, 2, 512], F32))
        pO = ph.enter_context(nc.psum_tensor(f"pO_{qc}", [128, 2, 512], F32))
        pTF = ph.enter_context(nc.psum_tensor(f"pTF_{qc}", [128, 2, 512], F32))

        d_a0, d_g0 = dA.n, dG.n
        w2b = [st.apool[0].n, st.apool[1].n]
        ob_out = [st.gpool[0].n, st.gpool[1].n]
        om0, oe0, tp0, fi0 = fOM.n, fOE.n, fTP.n, fFIv.n

        @blk.scalar
        def _(e):
            for i in range(c.FB + 2):
                if i >= 2:
                    e.wait_ge(fOM.h, om0 + i - 1)
                    fOE.inc(e.copy(out=foT[:, i - 2, :],
                                   in_=pO[:, (i - 2) % 2, 0:FQC]))
                if i < c.FB:
                    st.apool[i % 2].inc(e.dma_start(
                        out=w2s[:, i % 2, :, :],
                        in_=st.W2T[:, i * 128:(i + 1) * 128].rearrange(
                            "(hb p) d -> p hb d", p=128)), 16)
            for tq in range(TPQ):
                if tq >= 1:
                    e.wait_ge(fFIv.h, fi0 + tq * JG)
                dA.inc(e.dma_start(
                    out=xmi[:, :],
                    in_=st.XMID[(qc * FQC + tq * 128):(qc * FQC + (tq + 1) * 128),
                                :]), 16)

        @blk.tensor
        def _(e):
            for db in range(c.FB):
                e.wait_ge(st.apool[db % 2].h, w2b[db % 2] + 16 * (db // 2 + 1))
                if db >= 2:
                    e.wait_ge(fOE.h, oe0 + db - 1)
                for hb in range(c.FHB):
                    ins = e.matmul(pO[:, db % 2, 0:FQC], w2s[:, db % 2, hb, :],
                                   hsb[:, hb, :], start=(hb == 0),
                                   stop=(hb == c.FHB - 1))
                fOM.inc(ins)
            for tq in range(TPQ):
                for jg in range(JG):
                    gidx = tq * JG + jg
                    if gidx >= 2:
                        e.wait_ge(fFIv.h, fi0 + gidx - 1)
                    for q in range(4):
                        db = 4 * jg + q
                        e.wait_ge(fOE.h, oe0 + db + 1)
                        ins = e.matmul(pTF[:, gidx % 2, q * 128:(q + 1) * 128],
                                       foT[:, db, tq * 128:(tq + 1) * 128],
                                       st.identb[:, :], start=True, stop=True)
                    fTP.inc(ins)

        @blk.vector
        def _(e):
            for tq in range(TPQ):
                e.wait_ge(dA.h, d_a0 + 16 * (tq + 1))
                for jg in range(JG):
                    gidx = tq * JG + jg
                    e.wait_ge(fTP.h, tp0 + gidx + 1)
                    if gidx >= 2:
                        e.wait_ge(st.gpool[gidx % 2].h,
                                  ob_out[gidx % 2] + 16 * ((gidx - 2) // 2 + 1))
                    fFIv.inc(e.tensor_tensor(
                        out=osb[:, gidx % 2, :],
                        in0=pTF[:, gidx % 2, :],
                        in1=xmi[:, jg * 512:(jg + 1) * 512], op=AL.add))

        @blk.gpsimd
        def _(e):
            for tq in range(TPQ):
                for jg in range(JG):
                    gidx = tq * JG + jg
                    e.wait_ge(fFIv.h, fi0 + gidx + 1)
                    st.gpool[gidx % 2].inc(e.dma_start(
                        out=st.io["out"][(qc * FQC + tq * 128):
                                         (qc * FQC + (tq + 1) * 128),
                                         jg * 512:(jg + 1) * 512],
                        in_=osb[:, gidx % 2, :]), 16)

        phase_barrier(st)


def _alloc_globals(st):
    c, nc = st.cfg, st.nc
    sb = lambda n, s, d: st.es.enter_context(nc.sbuf_tensor(n, s, d))
    st.identf_sb = sb("identf_sb", [128, 128], F32)
    st.identb = sb("identb", [128, 128], BF16)
    st.ones1f = sb("ones1f", [1, 128], F32)
    st.ones32f = sb("ones32f", [32, 1], F32)
    st.ones128f = sb("ones128f", [128, 1], F32)
    st.ones1b = sb("ones1b", [1, 128], BF16)
    st.ones128b = sb("ones128b", [128, 1], BF16)
    st.ones33b = sb("ones33b", [33, 128], BF16)
    st.bjunk = sb("bjunk", [1, 8], F32)
    st.e34b = sb("e34b", [34, 128], BF16)
    st.bT_dq = sb("bT_dq", [128, c.C1B], F32)
    st.bT_dkv = sb("bT_dkv", [128, c.CB], F32)
    st.bT_uk = sb("bT_uk", [128, c.FB], F32)
    st.bT_uq = sb("bT_uq", [128, c.FB], F32)
    st.bT_rq = sb("bT_rq", [128, c.RB], F32)
    st.bT_rk = sb("bT_rk", [64, 1], F32)
    st.brow = sb("brow", [33, c.D], BF16)
    st.epsb = sb("epsb", [128, 1], F32)
    st.negB = sb("negB", [1, 1], BF16)


def build(cfg: Cfg):
    import ml_dtypes
    c = cfg
    nc = bass.Bass("TRN2", target_bir_lowering=False, debug=False)
    st = St()
    st.cfg, st.nc = c, nc
    st.es = contextlib.ExitStack()
    st.sems = {}

    io = {}
    io["x"] = nc.dram_tensor("x", [c.SK, c.D], F32, kind="ExternalInput").ap()
    for nm, shp in [("w_dq", [c.D, c.DC1]), ("w_uq", [c.DC1, c.D]),
                    ("w_dkv", [c.D, c.DC]), ("w_uk", [c.DC, c.D]),
                    ("w_uv", [c.DC, c.D]), ("w_rq", [c.DC1, c.H * c.DR]),
                    ("w_rk", [c.D, c.DR]), ("w_out", [c.D, c.D]),
                    ("b_dq", [c.DC1]), ("b_uq", [c.D]), ("b_dkv", [c.DC]),
                    ("b_uk", [c.D]), ("b_uv", [c.D]), ("b_rq", [c.H * c.DR]),
                    ("b_rk", [c.DR]), ("b_out", [c.D])]:
        io[nm] = nc.dram_tensor(nm, shp, F32, kind="ExternalInput").ap()
    io["w1p"] = nc.dram_tensor("w1p", [c.FB, 64, c.FH], I8, kind="ExternalInput").ap()
    io["w3p"] = nc.dram_tensor("w3p", [c.FB, 64, c.FH], I8, kind="ExternalInput").ap()
    io["w2p"] = nc.dram_tensor("w2p", [c.FHB, 64, c.D], I8, kind="ExternalInput").ap()
    io["s1r"] = nc.dram_tensor("s1r", [c.FB, 2, c.FH], F32, kind="ExternalInput").ap()
    io["s3r"] = nc.dram_tensor("s3r", [c.FB, 2, c.FH], F32, kind="ExternalInput").ap()
    io["s2r"] = nc.dram_tensor("s2r", [c.FHB, 2, c.D], F32, kind="ExternalInput").ap()
    io["out"] = nc.dram_tensor("out", [c.NQ, c.D], F32, kind="ExternalOutput").ap()
    st.io = io

    scratch_kind = "ExternalOutput" if c.debug else "Internal"
    st.XLT = nc.dram_tensor("XLTs", [c.D, c.NQ], BF16, kind=scratch_kind).ap()
    st.KT = nc.dram_tensor("KTs", [c.D, c.SK], BF16, kind=scratch_kind).ap()
    st.VS = nc.dram_tensor("VSs", [c.SK, c.D], BF16, kind=scratch_kind).ap()
    st.CQT = nc.dram_tensor("CQTs", [c.DC1, c.NQ], BF16, kind=scratch_kind).ap()
    st.QT = nc.dram_tensor("QTs", [c.D, c.NQ], BF16, kind=scratch_kind).ap()
    st.QRT = nc.dram_tensor("QRTs", [c.H * c.DR, c.NQ], BF16, kind=scratch_kind).ap()
    st.AOT = nc.dram_tensor("AOTs", [c.D, c.NQ], BF16, kind=scratch_kind).ap()
    st.XMID = nc.dram_tensor("XMIDs", [c.NQ, c.D], F32, kind=scratch_kind).ap()
    st.XL2T = nc.dram_tensor("XL2Ts", [c.D, c.NQ], BF16, kind=scratch_kind).ap()
    st.W1T = nc.dram_tensor("W1Ts", [c.D, c.FH], BF16, kind=scratch_kind).ap()
    st.W3T = nc.dram_tensor("W3Ts", [c.D, c.FH], BF16, kind=scratch_kind).ap()
    st.W2T = nc.dram_tensor("W2Ts", [c.FH, c.D], BF16, kind=scratch_kind).ap()

    bf = ml_dtypes.bfloat16
    st.ident_f = nc.inline_tensor(np.eye(128, dtype=np.float32), "identft")
    st.ident_b = nc.inline_tensor(np.eye(128, dtype=bf), "identbt")
    st.ones_col_f = nc.inline_tensor(np.ones((1, 128), np.float32), "ones1ft")
    st.ones_col_b = nc.inline_tensor(np.ones((1, 128), bf), "ones1bt")
    st.ones128_b = nc.inline_tensor(np.ones((128, 1), bf), "ones128bt")
    o33 = np.zeros((33, 128), dtype=np.float32)
    o33[0, :] = 1.0
    o33[32, :] = 1.0
    st.ones33_b = nc.inline_tensor(o33.astype(bf), "ones33bt")
    st.ones32_f = nc.inline_tensor(np.ones((32, 1), np.float32), "ones32ft")
    st.ones128_f = nc.inline_tensor(np.ones((128, 1), np.float32), "ones128ft")
    e34 = np.zeros((34, 128), dtype=np.float32)
    e34[0, 0:64] = 1.0 / 16.0
    e34[1, 64:128] = 1.0
    e34[32, 0:64] = 1.0 / 16.0
    e34[33, 64:128] = 1.0
    st.e34_t = nc.inline_tensor(e34.astype(bf), "e34t")

    _alloc_globals(st)
    _mk_sems(st, ["dS", "dA", "dG", "bar"])
    st.spool = _mk_sems(st, ["qs0", "qs1"])
    st.apool = _mk_sems(st, ["qa0", "qa1"])
    st.gpool = _mk_sems(st, ["qg0", "qg1", "qg2", "qg3"])
    st.blk = st.es.enter_context(nc.Block())

    phase_p0(st)
    st.krr_es = contextlib.ExitStack()
    st.KRR = st.krr_es.enter_context(nc.sbuf_tensor("KRR", [65, c.SK], BF16))
    st.ckvt_es = contextlib.ExitStack()
    st.CKVT = st.ckvt_es.enter_context(nc.sbuf_tensor("CKVT", [128, c.CB, c.SK], BF16))
    if c.n_phases >= 1:
        phase_a1(st)
    if c.n_phases >= 2:
        phase_a2(st)
    st.ckvt_es.close()
    if c.n_phases >= 3:
        phase_a2b(st)
    if c.n_phases >= 4:
        _phase_a3_generic(st, "a3q", "w_uq", c.FB, st.bT_uq, st.QT, c.SCALER)
        _phase_a3_generic(st, "a3r", "w_rq", c.RB, st.bT_rq, st.QRT, c.SCALER)
    if c.n_phases >= 5:
        phase_att(st)
    st.krr_es.close()
    if c.n_phases >= 6:
        phase_wo(st)
    if c.n_phases >= 7:
        phase_ln2(st)
    if c.n_phases >= 8:
        _phase_deq(st, "dq1", st.io["w1p"], st.io["s1r"], st.W1T, c.FB, c.FH)
    if c.n_phases >= 8.3:
        _phase_deq(st, "dq3", st.io["w3p"], st.io["s3r"], st.W3T, c.FB, c.FH)
    if c.n_phases >= 8.6:
        _phase_deq(st, "dq2", st.io["w2p"], st.io["s2r"], st.W2T, c.FHB, c.D)
    if c.n_phases >= 9:
        st.hsb_es = contextlib.ExitStack()
        st.hsb = st.hsb_es.enter_context(
            nc.sbuf_tensor("hsb", [128, c.FHB, c.FQC], BF16))
        for qc in range(c.NQ // c.FQC):
            phase_ffn1(st, qc)
            phase_ffn2(st, qc)
        st.hsb_es.close()

    st.es.close()
    return nc


def _prep_core_inputs(core, x_full, w):
    b, qb = core // 4, core % 4
    SK = x_full.shape[1]
    NQ = SK // 4
    xb = x_full[b]
    qs, qe = qb * NQ, (qb + 1) * NQ
    x_core = np.concatenate([xb[qs:qe], xb[:qs], xb[qe:]], axis=0)
    d = {"x": np.ascontiguousarray(x_core)}
    d.update(w)
    return d


def _pack_ffn(w_packed_i32, scales, out_feat, in_feat):
    R = w_packed_i32.shape[0]
    assert w_packed_i32.shape == (out_feat * in_feat // 128, 64)
    w8 = w_packed_i32.view(np.int8).reshape(R, 64, 4)[:, :, 0]
    IB = in_feat // 128
    wp = np.ascontiguousarray(
        w8.reshape(out_feat, IB, 64).transpose(1, 2, 0))
    sr = np.ascontiguousarray(
        scales.reshape(out_feat, IB, 2).transpose(1, 2, 0))
    return wp, sr


_FULL = Cfg()


def kernel(**inputs):
    c = _FULL
    x = inputs["L0_attn_sequence"]
    w = {}
    for nm in ["w_dq", "b_dq", "w_uq", "b_uq", "w_dkv", "b_dkv", "w_uk", "b_uk",
               "w_uv", "b_uv", "w_rq", "b_rq", "w_rk", "b_rk", "w_out", "b_out"]:
        w[nm] = np.ascontiguousarray(np.asarray(inputs["L0_attn_" + nm], np.float32))
    w["w1p"], w["s1r"] = _pack_ffn(np.asarray(inputs["L0_ffn_w1"]),
                                   np.asarray(inputs["L0_ffn_s1"], np.float32),
                                   c.FH, c.D)
    w["w3p"], w["s3r"] = _pack_ffn(np.asarray(inputs["L0_ffn_w3"]),
                                   np.asarray(inputs["L0_ffn_s3"], np.float32),
                                   c.FH, c.D)
    w["w2p"], w["s2r"] = _pack_ffn(np.asarray(inputs["L0_ffn_w2"]),
                                   np.asarray(inputs["L0_ffn_s2"], np.float32),
                                   c.D, c.FH)
    x = np.asarray(x, np.float32)

    nc = build(c)
    in_maps = [_prep_core_inputs(core, x, w) for core in range(8)]
    res = run_bass_kernel_spmd(nc, in_maps, core_ids=list(range(8)))

    out = np.empty((2, 8192, 4096), np.float32)
    for core in range(8):
        b, qb = core // 4, core % 4
        out[b, qb * 2048:(qb + 1) * 2048] = res.results[core]["out"]
    return out


# version 05 (final)
# speedup vs baseline: 1.5349x; 1.1677x over previous
import contextlib
import math

import numpy as np

import concourse.bass as bass
import concourse.mybir as mybir
from concourse.bass_utils import run_bass_kernel_spmd

F32 = mybir.dt.float32
BF16 = mybir.dt.bfloat16
I8 = mybir.dt.int8
AF = mybir.ActivationFunctionType
AL = mybir.AluOpType
AX = mybir.AxisListType.X

LN_EPS = 1e-5


class Cfg:
    def __init__(self, D=4096, DC=512, DC1=1536, DR=64, H=32, DH=128,
                 SK=8192, NQ=2048, FH=11008, CH=512, QC=512, FQC=512,
                 n_phases=99, debug=False):
        self.D, self.DC, self.DC1, self.DR = D, DC, DC1, DR
        self.H, self.DH, self.SK, self.NQ, self.FH = H, DH, SK, NQ, FH
        self.CH, self.QC, self.FQC = CH, QC, FQC
        self.n_phases = n_phases
        self.debug = debug
        self.merge_deq = True
        assert D % 128 == 0 and DC % 128 == 0 and DC1 % 128 == 0
        assert DR == 64 and DH == 128 and H * DH == D
        assert SK % CH == 0 and NQ % CH == 0 and CH % 128 == 0
        assert NQ % QC == 0 and QC % 128 == 0 and QC <= 512
        assert FH % 128 == 0 and NQ % FQC == 0 and FQC <= 512
        self.FB = D // 128
        self.CB = DC // 128
        self.C1B = DC1 // 128
        self.RB = (H * DR) // 128
        self.FHB = FH // 128
        self.NCH = SK // CH
        self.NQCH = NQ // CH
        self.TPC = CH // 128
        self.SKB = SK // 128
        self.NQC = NQ // QC
        self.NQB = NQ // 128
        self.SCALER = 1.0 / math.sqrt(DH + DR)


class Sem:
    def __init__(self, nc, es, name):
        self.h = es.enter_context(nc.semaphore(name))
        self.n = 0

    def inc(self, ins, k=1):
        ins.then_inc(self.h, k)
        self.n += k
        return ins


class St:
    pass


def _mk_sems(st, names):
    for nm in names:
        if nm not in st.sems:
            st.sems[nm] = Sem(st.nc, st.es, nm)
    return [st.sems[nm] for nm in names]


def phase_barrier(st):
    blk, bar = st.blk, st.sems["bar"]
    dS, dA, dG = st.sems["dS"], st.sems["dA"], st.sems["dG"]
    target = bar.n + 5
    flushes = {"sync": [(dS.h, dS.n)] + [(s.h, s.n) for s in st.spool],
               "scalar": [(dA.h, dA.n)] + [(s.h, s.n) for s in st.apool],
               "gpsimd": [(dG.h, dG.n)] + [(s.h, s.n) for s in st.gpool]}

    @blk.sync
    def _(e):
        for h, n in flushes["sync"]:
            e.wait_ge(h, n)
        bar.inc(e.nop())
        e.wait_ge(bar.h, target)

    @blk.scalar
    def _(e):
        for h, n in flushes["scalar"]:
            e.wait_ge(h, n)
        bar.inc(e.memzero(st.bjunk[0:1, 1:2]))
        e.wait_ge(bar.h, target)

    @blk.gpsimd
    def _(e):
        for h, n in flushes["gpsimd"]:
            e.wait_ge(h, n)
        bar.inc(e.nop())
        e.wait_ge(bar.h, target)

    @blk.vector
    def _(e):
        bar.inc(e.memset(st.bjunk[0:1, 0:1], 0.0))
        e.wait_ge(bar.h, target)

    @blk.tensor
    def _(e):
        bar.inc(e.nop())
        e.wait_ge(bar.h, target)


def phase_p0(st):
    c, nc, blk = st.cfg, st.nc, st.blk
    dS, dG, tQ, sQ = _mk_sems(st, ["dS", "dG", "p0t", "p0s"])

    with contextlib.ExitStack() as ph:
        bstg = ph.enter_context(nc.sbuf_tensor("bstg", [max(c.FB, c.C1B, c.RB), 128], F32))
        pT0 = ph.enter_context(nc.psum_tensor("pT0", [128, 128], F32))

        @blk.sync
        def _(e):
            dS.inc(e.dma_start(out=st.identf_sb[:, :], in_=st.ident_f[:, :]), 16)
            dS.inc(e.dma_start(out=st.identb[:, :], in_=st.ident_b[:, :]), 16)
            dS.inc(e.dma_start(out=st.ones1f[:, :], in_=st.ones_col_f[:, :]), 16)
            dS.inc(e.dma_start(out=st.ones32f[:, :], in_=st.ones32_f[:, :]), 16)
            dS.inc(e.dma_start(out=st.ones128f[:, :], in_=st.ones128_f[:, :]), 16)
            dS.inc(e.dma_start(out=st.ones1b[:, :], in_=st.ones_col_b[:, :]), 16)
            dS.inc(e.dma_start(out=st.ones128b[:, :], in_=st.ones128_b[:, :]), 16)
            dS.inc(e.dma_start(out=st.ones33b[:, :], in_=st.ones33_b[:, :]), 16)
            dS.inc(e.dma_start(out=st.e34b[:, :], in_=st.e34_t[:, :]), 16)

        @blk.gpsimd
        def _(e):
            dG.inc(e.dma_start(out=st.brow[0:1, :], in_=st.io["b_uv"][None, :]), 16)
            dG.inc(e.dma_start(out=st.brow[32:33, :], in_=st.io["b_out"][None, :]), 16)

        @blk.vector
        def _(e):
            e.memset(st.epsb[:, :], LN_EPS)

        bias_jobs = [("b_dq", c.C1B, st.bT_dq), ("b_dkv", c.CB, st.bT_dkv),
                     ("b_uk", c.FB, st.bT_uk), ("b_uq", c.FB, st.bT_uq),
                     ("b_rq", c.RB, st.bT_rq)]
        for name, nb, bdst in bias_jobs:
            s_before = sQ.n

            @blk.sync
            def _(e, name=name, nb=nb, s_before=s_before):
                e.wait_ge(sQ.h, s_before)
                dS.inc(e.dma_start(
                    out=bstg[0:nb, :],
                    in_=st.io[name].rearrange("(nb p) -> nb p", p=128)), 16)

            d_need, s_need = dS.n, sQ.n

            @blk.tensor
            def _(e, nb=nb, d_need=d_need, s_need=s_need):
                e.wait_ge(dS.h, d_need)
                e.wait_ge(sQ.h, s_need)
                tQ.inc(e.matmul(pT0[:, 0:nb], bstg[0:nb, :],
                                st.identf_sb[0:nb, 0:nb], start=True, stop=True))

            t_need = tQ.n

            @blk.scalar
            def _(e, nb=nb, bdst=bdst, t_need=t_need):
                e.wait_ge(tQ.h, t_need)
                sQ.inc(e.copy(out=bdst[:, 0:nb], in_=pT0[:, 0:nb]))

        s_before = sQ.n

        @blk.sync
        def _(e, s_before=s_before):
            e.wait_ge(sQ.h, s_before)
            dS.inc(e.dma_start(out=bstg[0:1, 0:64], in_=st.io["b_rk"][None, :]), 16)

        d_need, s_need = dS.n, sQ.n

        @blk.tensor
        def _(e, d_need=d_need, s_need=s_need):
            e.wait_ge(dS.h, d_need)
            e.wait_ge(sQ.h, s_need)
            tQ.inc(e.matmul(pT0[0:64, 0:1], bstg[0:1, 0:64],
                            st.identf_sb[0:1, 0:1], start=True, stop=True))

        t_need = tQ.n

        @blk.scalar
        def _(e, t_need=t_need):
            e.wait_ge(tQ.h, t_need)
            sQ.inc(e.copy(out=st.bT_rk[:, :], in_=pT0[0:64, 0:1]))

        phase_barrier(st)


def phase_a1(st):
    c, nc, blk = st.cfg, st.nc, st.blk
    dS, dG = st.sems["dS"], st.sems["dG"]
    (aST, aSQ, aAP, aTP, aTE, aMM, aCE, aKE, vCH) = _mk_sems(
        st, ["a1stat", "a1sqrt", "a1apply", "a1tpose", "a1tpev",
             "a1mm", "a1cev", "a1krev", "a1vch"])
    NG = c.D // 512
    NT = c.NCH * c.TPC
    GPT = c.FB // 4

    with contextlib.ExitStack() as ph:
        w_dkv_b = ph.enter_context(nc.sbuf_tensor("w_dkv_b", [128, c.FB, c.DC], BF16))
        w_rk_b = ph.enter_context(nc.sbuf_tensor("w_rk_b", [128, c.FB, c.DR], BF16))
        xs = ph.enter_context(nc.sbuf_tensor("xs", [128, 2, c.D], F32))
        xlb = ph.enter_context(nc.sbuf_tensor("xlb", [128, 2, c.D], BF16))
        xlT = ph.enter_context(nc.sbuf_tensor("xlT", [128, c.FB, c.CH], BF16))
        stat = ph.enter_context(nc.sbuf_tensor("stat", [128, 8], F32))
        mv6 = ph.enter_context(nc.sbuf_tensor("mv6", [128, NG, 6], F32))
        pT1 = ph.enter_context(nc.psum_tensor("pT1", [128, 2, 512], F32))
        pM1 = ph.enter_context(nc.psum_tensor("pM1", [128, 2, 512], F32))
        pK1 = ph.enter_context(nc.psum_tensor("pK1", [64, 512], F32))

        d_w0 = dG.n

        @blk.gpsimd
        def _(e):
            dG.inc(e.dma_start(out=w_dkv_b[:, :, :],
                               in_=st.io["w_dkv"].rearrange("(fb p) k -> p fb k", p=128)), 16)
            dG.inc(e.dma_start(out=w_rk_b[:, :, :],
                               in_=st.io["w_rk"].rearrange("(fb p) k -> p fb k", p=128)), 16)
            if c.debug:
                pass

        d_wready = dG.n
        d_x0 = dS.n

        def spills_done(ci):
            return d_wready + 16 * (min(ci + 1, c.NQCH))

        xb0, xb1 = st.spool[0].n, st.spool[1].n

        @blk.sync
        def _(e):
            for g in range(NT):
                if g >= 2:
                    e.wait_ge(aAP.h, g - 1)
                st.spool[g % 2].inc(e.dma_start(
                    out=xs[:, g % 2, :],
                    in_=st.io["x"][g * 128:(g + 1) * 128, :]), 16)

        def xload_done(g):
            return (xb0 if g % 2 == 0 else xb1) + 16 * (g // 2 + 1)

        @blk.vector
        def _(e):
            for ci in range(c.NCH):
                for t in range(c.TPC):
                    g = ci * c.TPC + t
                    e.wait_ge(st.spool[g % 2].h, xload_done(g))
                    for ng in range(NG):
                        ins = e.bn_stats(out=mv6[:, ng, :],
                                         in_=xs[:, g % 2, ng * 512:(ng + 1) * 512])
                    vCH.inc(ins)
                    e.wait_ge(vCH.h, vCH.n)
                    aST.inc(e.bn_aggr(out=stat[:, 0:2], in_=mv6[:, :, :]))
                    e.wait_ge(aSQ.h, g + 1)
                    vCH.inc(e.reciprocal(out=stat[:, 3:4], in_=stat[:, 2:3]))
                    e.wait_ge(vCH.h, vCH.n)
                    vCH.inc(e.tensor_scalar(out=stat[:, 4:5], in0=stat[:, 0:1],
                                    scalar1=stat[:, 3:4], scalar2=-1.0,
                                    op0=AL.mult, op1=AL.mult))
                    e.wait_ge(vCH.h, vCH.n)
                    if g >= 2:
                        e.wait_ge(aTP.h, (g - 1) * GPT)
                    aAP.inc(e.tensor_scalar(out=xlb[:, g % 2, :], in0=xs[:, g % 2, :],
                                            scalar1=stat[:, 4:5], scalar2=stat[:, 3:4],
                                            op0=AL.add, op1=AL.mult))
                for cb in range(c.CB):
                    e.wait_ge(aMM.h, ci * (c.CB + 1) + cb + 1)
                    aCE.inc(e.tensor_scalar(
                        out=st.CKVT[:, cb, ci * c.CH:(ci + 1) * c.CH],
                        in0=pM1[:, cb % 2, :], scalar1=st.bT_dkv[:, cb:cb + 1],
                        scalar2=None, op0=AL.add))
                e.wait_ge(aMM.h, (ci + 1) * (c.CB + 1))
                aKE.inc(e.tensor_scalar(
                    out=st.KRR[0:64, ci * c.CH:(ci + 1) * c.CH],
                    in0=pK1[:, :], scalar1=st.bT_rk[:, 0:1],
                    scalar2=None, op0=AL.add))
                if ci == c.NCH - 1:
                    aKE.inc(e.memset(st.KRR[64:65, :], 1.0))

        @blk.scalar
        def _(e):
            for ci in range(c.NCH):
                for t in range(c.TPC):
                    g = ci * c.TPC + t
                    e.wait_ge(aST.h, g + 1)
                    aSQ.inc(e.activation(out=stat[:, 2:3], in_=stat[:, 1:2],
                                         func=AF.Sqrt, bias=st.epsb[:, 0:1], scale=1.0))
                    for j in range(GPT):
                        gidx = g * GPT + j
                        e.wait_ge(aTP.h, gidx + 1)
                        if t == 0 and j == 0 and ci >= 1:
                            e.wait_ge(aMM.h, ci * (c.CB + 1))
                            if ci - 1 < c.NQCH:
                                e.wait_ge(dG.h, spills_done(ci - 1))
                        aTE.inc(e.copy(
                            out=xlT[:, 4 * j:4 * j + 4, t * 128:(t + 1) * 128],
                            in_=pT1[:, gidx % 2, :].rearrange("p (b t) -> p b t", b=4)))

        @blk.tensor
        def _(e):
            e.wait_ge(dG.h, d_wready)
            for ci in range(c.NCH):
                for t in range(c.TPC):
                    g = ci * c.TPC + t
                    e.wait_ge(aAP.h, g + 1)
                    for j in range(GPT):
                        gidx = g * GPT + j
                        if gidx >= 2:
                            e.wait_ge(aTE.h, gidx - 1)
                        for q in range(4):
                            fb = 4 * j + q
                            tQ_ins = e.matmul(
                                pT1[:, gidx % 2, q * 128:(q + 1) * 128],
                                xlb[:, g % 2, fb * 128:(fb + 1) * 128],
                                st.identb[:, :], start=True, stop=True)
                            if q == 3:
                                aTP.inc(tQ_ins)
                e.wait_ge(aTE.h, (ci + 1) * c.TPC * GPT)
                for cb in range(c.CB):
                    mmi = ci * (c.CB + 1) + cb
                    if mmi >= 2:
                        e.wait_ge(aCE.h, ci * c.CB + cb - 1)
                    for fb in range(c.FB):
                        ins = e.matmul(pM1[:, cb % 2, :],
                                       w_dkv_b[:, fb, cb * 128:(cb + 1) * 128],
                                       xlT[:, fb, :],
                                       start=(fb == 0), stop=(fb == c.FB - 1))
                    aMM.inc(ins)
                if ci >= 1:
                    e.wait_ge(aKE.h, ci)
                for fb in range(c.FB):
                    ins = e.matmul(pK1[:, :], w_rk_b[:, fb, :], xlT[:, fb, :],
                                   start=(fb == 0), stop=(fb == c.FB - 1))
                aMM.inc(ins)

        import os as _osa

        @blk.gpsimd
        def _(e):
            for ci in range(c.NQCH):
                e.wait_ge(aTE.h, (ci + 1) * c.TPC * GPT)
                if not _osa.environ.get("K_A1_NOSPILL", ""):
                    dG.inc(e.dma_start(
                        out=st.XLT[:, ci * c.CH:(ci + 1) * c.CH].rearrange(
                            "(fb p) t -> p fb t", p=128),
                        in_=xlT[:, :, :]), 16)

        phase_barrier(st)

        if c.debug:
            _dump_debug(st, [("dbg_ckvt", st.CKVT, [128, c.CB, c.SK], BF16),
                             ("dbg_krr", st.KRR, [65, c.SK], BF16)])


def _dump_debug(st, items):
    c, nc, blk = st.cfg, st.nc, st.blk
    dG = st.sems["dG"]
    phase_barrier(st)

    @blk.gpsimd
    def _(e):
        for name, sbten, shape, dt in items:
            dten = nc.dram_tensor(name, shape, dt, kind="ExternalOutput").ap()
            dG.inc(e.dma_start(out=dten[tuple(slice(None) for _ in shape)],
                               in_=sbten[tuple(slice(None) for _ in shape)]), 16)

    phase_barrier(st)


def phase_a2(st):
    c, nc, blk = st.cfg, st.nc, st.blk
    dS, dG = st.sems["dS"], st.sems["dG"]
    (kMM, kEV, kSQ, rSQ, nMM, vMM, vEV, mxA, nK) = _mk_sems(
        st, ["a2kmm", "a2kev", "a2ksq", "a2krsq", "a2nmm",
             "a2vmm", "a2vev", "a2mx", "a2nk"])
    NDC = c.D // 512

    with contextlib.ExitStack() as ph:
        w_uk_b = ph.enter_context(nc.sbuf_tensor("w_uk_b", [128, c.CB, c.D], BF16))
        w_uv_b = ph.enter_context(nc.sbuf_tensor("w_uv_b", [128, c.CB, c.D], BF16))
        kbuf = ph.enter_context(nc.sbuf_tensor("kbuf", [128, 8, 512], BF16))
        ksq = ph.enter_context(nc.sbuf_tensor("ksq", [128, 4, 512], BF16))
        krsq = ph.enter_context(nc.sbuf_tensor("krsq", [64, 512], BF16))
        vbuf = ph.enter_context(nc.sbuf_tensor("vbuf", [128, 2, c.D], BF16))
        mxacc = ph.enter_context(nc.sbuf_tensor("mxacc", [1, 512], F32))
        mxs = ph.enter_context(nc.sbuf_tensor("mxs", [1, 2], F32))
        pK = ph.enter_context(nc.psum_tensor("pK", [128, 2, 512], F32))
        pV = ph.enter_context(nc.psum_tensor("pV", [128, 2, 512], F32))
        pN = ph.enter_context(nc.psum_tensor("pN", [1, 512], F32))

        d_w0 = dG.n

        @blk.gpsimd
        def _(e):
            dG.inc(e.dma_start(out=w_uk_b[:, :, :],
                               in_=st.io["w_uk"].rearrange("(cb p) k -> p cb k", p=128)), 16)
            dG.inc(e.dma_start(out=w_uv_b[:, :, :],
                               in_=st.io["w_uv"].rearrange("(cb p) k -> p cb k", p=128)), 16)

        d_wready = dG.n
        KGR = c.FB // 4
        DPC = KGR + c.TPC

        kb = [st.gpool[0].n, st.gpool[1].n]
        vb = [st.gpool[2].n, st.gpool[3].n]

        def kgidx(i_fb):
            ci_, gr_ = i_fb // c.FB, (i_fb % c.FB) // 4
            return ci_ * KGR + gr_

        def kdma_sem(i_fb):
            g = kgidx(i_fb)
            return st.gpool[g % 2], kb[g % 2] + 16 * (g // 2 + 1)

        def vdma_sem(jj):
            return st.gpool[2 + jj % 2], vb[jj % 2] + 16 * (jj // 2 + 1)

        @blk.vector
        def _(e):
            e.memset(mxacc[:, :], 0.0)
            for ci in range(c.NCH):
                sl = slice(ci * c.CH, (ci + 1) * c.CH)
                for fb in range(c.FB):
                    i = ci * c.FB + fb
                    e.wait_ge(kMM.h, i + 1)
                    if i >= 8:
                        s_, v_ = kdma_sem(i - 8)
                        e.wait_ge(s_.h, v_)
                    kEV.inc(e.tensor_scalar(out=kbuf[:, i % 8, :],
                                            in0=pK[:, i % 2, :],
                                            scalar1=st.bT_uk[:, fb:fb + 1],
                                            scalar2=None, op0=AL.add))
                    e.wait_ge(kEV.h, kEV.n)
                    if i >= 4:
                        e.wait_ge(nK.h, i - 3)
                    kSQ.inc(e.tensor_tensor(out=ksq[:, i % 4, :],
                                            in0=kbuf[:, i % 8, :],
                                            in1=kbuf[:, i % 8, :], op=AL.mult))
                if ci >= 1:
                    e.wait_ge(nMM.h, ci)
                rSQ.inc(e.tensor_tensor(out=krsq[:, :], in0=st.KRR[0:64, sl],
                                        in1=st.KRR[0:64, sl], op=AL.mult))
                for tt in range(c.TPC):
                    for dc in range(NDC):
                        i = (ci * c.TPC + tt) * NDC + dc
                        jj = ci * c.TPC + tt
                        e.wait_ge(vMM.h, i + 1)
                        if dc == 0 and jj >= 2:
                            s_, v_ = vdma_sem(jj - 2)
                            e.wait_ge(s_.h, v_)
                        vEV.inc(e.tensor_copy(out=vbuf[:, tt % 2, dc * 512:(dc + 1) * 512],
                                              in_=pV[:, i % 2, :]))
                e.wait_ge(nMM.h, ci + 1)
                mxA.inc(e.tensor_tensor(out=mxacc[:, :], in0=mxacc[:, :],
                                        in1=pN[:, :], op=AL.max))
            e.wait_ge(mxA.h, c.NCH)
            mxA.inc(e.reduce_max(out=mxs[:, 0:1], in_=mxacc[:, :], axis=AX))

        @blk.scalar
        def _(e):
            e.wait_ge(mxA.h, c.NCH + 1)
            mxA.inc(e.activation(out=mxs[:, 1:2], in_=mxs[:, 0:1], func=AF.Sqrt,
                                 bias=0.0, scale=1.0))

        @blk.vector
        def _(e):
            e.wait_ge(mxA.h, c.NCH + 2)
            mxA.inc(e.tensor_scalar(out=st.negB[:, :], in0=mxs[:, 1:2],
                                    scalar1=-1.05, scalar2=None, op0=AL.mult))

        @blk.tensor
        def _(e):
            e.wait_ge(dG.h, d_wready)
            for ci in range(c.NCH):
                sl = slice(ci * c.CH, (ci + 1) * c.CH)
                for fb in range(c.FB):
                    i = ci * c.FB + fb
                    if i >= 2:
                        e.wait_ge(kEV.h, i - 1)
                    for cb in range(c.CB):
                        ins = e.matmul(pK[:, i % 2, :],
                                       w_uk_b[:, cb, fb * 128:(fb + 1) * 128],
                                       st.CKVT[:, cb, sl],
                                       start=(cb == 0), stop=(cb == c.CB - 1))
                    kMM.inc(ins)
                    nfb = fb - 4
                    if nfb >= 0:
                        ni = ci * c.FB + nfb
                        e.wait_ge(kSQ.h, ni + 1)
                        if nfb == 0 and ci >= 1:
                            e.wait_ge(mxA.h, ci)
                        nK.inc(e.matmul(pN[:, :], st.ones128b[:, :], ksq[:, ni % 4, :],
                                        start=(nfb == 0), stop=False))
                for nfb in range(max(c.FB - 4, 0), c.FB):
                    ni = ci * c.FB + nfb
                    e.wait_ge(kSQ.h, ni + 1)
                    if nfb == 0 and ci >= 1:
                        e.wait_ge(mxA.h, ci)
                    nK.inc(e.matmul(pN[:, :], st.ones128b[:, :], ksq[:, ni % 4, :],
                                    start=(nfb == 0), stop=False))
                e.wait_ge(rSQ.h, ci + 1)
                nMM.inc(e.matmul(pN[:, :], st.ones128b[0:64, :], krsq[:, :],
                                 start=False, stop=True))
                for tt in range(c.TPC):
                    tsl = slice(ci * c.CH + tt * 128, ci * c.CH + (tt + 1) * 128)
                    for dc in range(NDC):
                        i = (ci * c.TPC + tt) * NDC + dc
                        if i >= 2:
                            e.wait_ge(vEV.h, i - 1)
                        for cb in range(c.CB):
                            e.matmul(pV[:, i % 2, :],
                                     st.CKVT[:, cb, tsl],
                                     w_uv_b[:, cb, dc * 512:(dc + 1) * 512],
                                     start=(cb == 0), stop=False)
                        vMM.inc(e.matmul(pV[:, i % 2, :], st.ones1b[:, :],
                                         st.brow[0:1, dc * 512:(dc + 1) * 512],
                                         start=False, stop=True))

        @blk.gpsimd
        def _(e):
            for ci in range(c.NCH):
                for gr in range(KGR):
                    g = ci * KGR + gr
                    e.wait_ge(kEV.h, ci * c.FB + 4 * gr + 4)
                    st.gpool[g % 2].inc(e.dma_start(
                        out=st.KT[4 * gr * 128:(4 * gr + 4) * 128,
                                  ci * c.CH:(ci + 1) * c.CH].rearrange(
                                      "(fb p) t -> p fb t", p=128),
                        in_=kbuf[:, (4 * g) % 8:(4 * g) % 8 + 4, :]), 16)
                for tt in range(c.TPC):
                    jj = ci * c.TPC + tt
                    e.wait_ge(vEV.h, (jj + 1) * NDC)
                    st.gpool[2 + jj % 2].inc(e.dma_start(
                        out=st.VS[ci * c.CH + tt * 128:ci * c.CH + (tt + 1) * 128, :],
                        in_=vbuf[:, tt % 2, :]), 16)

        phase_barrier(st)


def phase_a2b(st):
    c, nc, blk = st.cfg, st.nc, st.blk
    dS, dG = st.sems["dS"], st.sems["dG"]
    (bMM, bEV) = _mk_sems(st, ["a2bmm", "a2bev"])

    with contextlib.ExitStack() as ph:
        w_dq_b = ph.enter_context(nc.sbuf_tensor("w_dq_b", [128, c.FB, c.DC1], BF16))
        xq = ph.enter_context(nc.sbuf_tensor("xq", [128, 2, c.FB, c.CH], BF16))
        cqb = ph.enter_context(nc.sbuf_tensor("cqb", [128, 1, c.C1B, c.CH], BF16))
        pB = ph.enter_context(nc.psum_tensor("pB2", [128, 2, 512], F32))

        d_w0 = dG.n

        @blk.gpsimd
        def _(e):
            dG.inc(e.dma_start(out=w_dq_b[:, :, :],
                               in_=st.io["w_dq"].rearrange("(fb p) k -> p fb k", p=128)), 16)
            for qch in range(c.NQCH):
                e.wait_ge(bEV.h, (qch + 1) * c.C1B)
                dG.inc(e.dma_start(
                    out=st.CQT[:, qch * c.CH:(qch + 1) * c.CH].rearrange(
                        "(cb p) t -> p cb t", p=128),
                    in_=cqb[:, 0, :, :]), 16)

        d_wready = d_w0 + 16
        d_x0 = dS.n

        qb0, qb1 = st.spool[0].n, st.spool[1].n

        @blk.sync
        def _(e):
            for qch in range(c.NQCH):
                if qch >= 2:
                    e.wait_ge(bMM.h, (qch - 1) * c.C1B)
                st.spool[qch % 2].inc(e.dma_start(
                    out=xq[:, qch % 2, :, :],
                    in_=st.XLT[:, qch * c.CH:(qch + 1) * c.CH].rearrange(
                        "(fb p) t -> p fb t", p=128)), 16)

        @blk.tensor
        def _(e):
            e.wait_ge(dG.h, d_wready)
            for qch in range(c.NQCH):
                e.wait_ge(st.spool[qch % 2].h,
                          (qb0 if qch % 2 == 0 else qb1) + 16 * (qch // 2 + 1))
                for cb in range(c.C1B):
                    i = qch * c.C1B + cb
                    if i >= 2:
                        e.wait_ge(bEV.h, i - 1)
                    for fb in range(c.FB):
                        ins = e.matmul(pB[:, i % 2, 0:c.CH],
                                       w_dq_b[:, fb, cb * 128:(cb + 1) * 128],
                                       xq[:, qch % 2, fb, :],
                                       start=(fb == 0), stop=(fb == c.FB - 1))
                    bMM.inc(ins)

        @blk.vector
        def _(e):
            for qch in range(c.NQCH):
                for cb in range(c.C1B):
                    i = qch * c.C1B + cb
                    e.wait_ge(bMM.h, i + 1)
                    if qch >= 1 and cb == 0:
                        e.wait_ge(dG.h, d_wready + 16 * qch)
                    bEV.inc(e.tensor_scalar(out=cqb[:, 0, cb, :],
                                            in0=pB[:, i % 2, 0:c.CH],
                                            scalar1=st.bT_dq[:, cb:cb + 1],
                                            scalar2=None, op0=AL.add))

        phase_barrier(st)


def _phase_a3_generic(st, tag, wname, wblocks, bT, dst, scale):
    c, nc, blk = st.cfg, st.nc, st.blk
    dS, dG = st.sems["dS"], st.sems["dG"]
    (gMM, gEV) = _mk_sems(st, [tag + "mm", tag + "ev"])
    OB = wblocks

    with contextlib.ExitStack() as ph:
        w_b = ph.enter_context(nc.sbuf_tensor(tag + "w", [128, c.C1B, OB * 128], BF16))
        cq = ph.enter_context(nc.sbuf_tensor(tag + "cq", [128, 2, c.C1B, c.CH], BF16))
        g4pre = 4 if OB % 4 == 0 else (2 if OB % 2 == 0 else 1)
        ob_ring = ph.enter_context(nc.sbuf_tensor(tag + "ob", [128, 2 * g4pre, c.CH], BF16))
        pQ = ph.enter_context(nc.psum_tensor(tag + "pq", [128, 2, 512], F32))

        d_w0 = dG.n
        g4 = 4 if OB % 4 == 0 else (2 if OB % 2 == 0 else 1)
        NSLOT = 2 * g4

        @blk.gpsimd
        def _(e):
            dG.inc(e.dma_start(out=w_b[:, :, :],
                               in_=st.io[wname].rearrange("(cb p) k -> p cb k", p=128)), 16)
            for qch in range(c.NQCH):
                for gr in range(OB // g4):
                    e.wait_ge(gEV.h, qch * OB + g4 * gr + g4)
                    dG.inc(e.dma_start(
                        out=dst[g4 * gr * 128:(g4 * gr + g4) * 128,
                                qch * c.CH:(qch + 1) * c.CH].rearrange(
                                    "(ob p) t -> p ob t", p=128),
                        in_=ob_ring[:, g4 * (gr % 2):g4 * (gr % 2) + g4, :]), 16)

        d_wready = d_w0 + 16
        d_x0 = dS.n
        DPQ = OB // g4

        a3_dg_total = d_wready + 16 * c.NQCH * DPQ

        def odma_done(i_ob):
            q_, g_ = i_ob // OB, (i_ob % OB) // g4
            return min(d_wready + 16 * (q_ * DPQ + g_ + 2), a3_dg_total)

        cb0, cb1 = st.spool[0].n, st.spool[1].n

        @blk.sync
        def _(e):
            for qch in range(c.NQCH):
                if qch >= 2:
                    e.wait_ge(gMM.h, (qch - 1) * OB)
                st.spool[qch % 2].inc(e.dma_start(
                    out=cq[:, qch % 2, :, :],
                    in_=st.CQT[:, qch * c.CH:(qch + 1) * c.CH].rearrange(
                        "(cb p) t -> p cb t", p=128)), 16)

        @blk.tensor
        def _(e):
            e.wait_ge(dG.h, d_wready)
            for qch in range(c.NQCH):
                e.wait_ge(st.spool[qch % 2].h,
                          (cb0 if qch % 2 == 0 else cb1) + 16 * (qch // 2 + 1))
                for ob in range(OB):
                    i = qch * OB + ob
                    if i >= 2:
                        e.wait_ge(gEV.h, i - 1)
                    for cb in range(c.C1B):
                        ins = e.matmul(pQ[:, i % 2, 0:c.CH],
                                       w_b[:, cb, ob * 128:(ob + 1) * 128],
                                       cq[:, qch % 2, cb, :],
                                       start=(cb == 0), stop=(cb == c.C1B - 1))
                    gMM.inc(ins)

        @blk.vector
        def _(e):
            for qch in range(c.NQCH):
                for ob in range(OB):
                    i = qch * OB + ob
                    e.wait_ge(gMM.h, i + 1)
                    if i >= NSLOT:
                        e.wait_ge(dG.h, odma_done(i - NSLOT))
                    if scale is not None:
                        gEV.inc(e.tensor_scalar(out=ob_ring[:, i % NSLOT, :],
                                                in0=pQ[:, i % 2, 0:c.CH],
                                                scalar1=bT[:, ob:ob + 1],
                                                scalar2=scale, op0=AL.add,
                                                op1=AL.mult))
                    else:
                        gEV.inc(e.tensor_scalar(out=ob_ring[:, i % NSLOT, :],
                                                in0=pQ[:, i % 2, 0:c.CH],
                                                scalar1=bT[:, ob:ob + 1],
                                                scalar2=None, op0=AL.add))

        phase_barrier(st)


class DeqJobs:

    SEC = 1024
    LA = 4
    WR = 8

    def __init__(self, st, ph):
        c, nc = st.cfg, st.nc
        self.st = st
        (self.dqB, self.dqW, self.dqV) = _mk_sems(st, ["mdqb", "mdqw", "mdqv"])
        self.b0 = self.dqB.n
        self.w0 = self.dqW.n
        self.v0 = self.dqV.n + 4
        self.ls = [st.spool[2].n, st.spool[3].n]
        self.ap = [st.apool[i].n for i in range(self.WR)]
        mats = [(st.io["w1p"], st.io["s1r"], st.W1T, c.FB, c.FH),
                (st.io["w3p"], st.io["s3r"], st.W3T, c.FB, c.FH),
                (st.io["w2p"], st.io["s2r"], st.W2T, c.FHB, c.D)]
        self.jobs = []
        for (wp, sr, WT, IBN, ON) in mats:
            for ib in range(IBN):
                o = 0
                while o < ON:
                    sz = min(self.SEC, ON - o)
                    self.jobs.append((wp, sr, WT, ib, o, sz))
                    o += sz
        self.NJ = len(self.jobs)
        self.jchunks = []
        self.cpre = [0]
        for (_, _, _, _, o, sz) in self.jobs:
            ch = []
            x = 0
            while x < sz:
                s2 = min(512, sz - x)
                ch.append((x, s2))
                x += s2
            self.jchunks.append(ch)
            self.cpre.append(self.cpre[-1] + len(ch))
        SEC = self.SEC
        self.pk8 = ph.enter_context(nc.sbuf_tensor("mpk", [64, 2, SEC], I8))
        self.srf = ph.enter_context(nc.sbuf_tensor("msr", [2, 2, SEC], F32))
        self.ssp = ph.enter_context(nc.sbuf_tensor("mss", [34, self.LA, SEC], BF16))
        self.v8 = ph.enter_context(nc.sbuf_tensor("mv8", [64, self.LA, SEC], I8))
        self.tb8 = ph.enter_context(nc.sbuf_tensor("mtb", [64, self.LA, SEC], I8))
        self.tt8 = ph.enter_context(nc.sbuf_tensor("mtt", [64, self.LA, SEC], I8))
        self.wrow = ph.enter_context(nc.sbuf_tensor("mwr", [128, self.WR, 512], BF16))
        self.pSC = ph.enter_context(nc.psum_tensor("mps", [128, 512], F32))
        NS = c.H * c.NQC
        self.slot_jobs = [list(range((self.NJ * t) // NS, (self.NJ * (t + 1)) // NS))
                          for t in range(NS)]
        self.slot_chunks = []
        for t in range(NS):
            lst = []
            for j in self.slot_jobs[t]:
                for k, (x, s2) in enumerate(self.jchunks[j]):
                    lst.append((j, k, self.cpre[j] + k, x, s2))
            self.slot_chunks.append(lst)

    def maps_for_slot(self, t, SKB):
        ents = self.slot_chunks[t]
        kt0 = (SKB * 3) // 8
        pemap, actmap = {}, {}
        n = max(len(ents), 1)
        for i, ent in enumerate(ents):
            pos = min(SKB - 1, kt0 + (i * (SKB - kt0)) // n)
            pemap.setdefault(pos, []).append(ent)
            ep = min(SKB // 2 - 1, (pos + 1) // 2 + 1)
            actmap.setdefault(ep, []).append(ent)
        return pemap, actmap

    def emit_ve_memsets(self, e):
        for r in range(self.LA):
            self.dqV.inc(e.memset(self.ssp[:, r, :], 0.0))
        e.wait_ge(self.dqV.h, self.dqV.n)

    def emit_ve_prep(self, e, j):
        st, dqV, dqB, dqW = self.st, self.dqV, self.dqB, self.dqW
        (wp, sr, WT, ib, o, sz) = self.jobs[j]
        e.wait_ge(st.spool[2 + j % 2].h, self.ls[j % 2] + 32 * (j // 2 + 1))
        if j >= self.LA:
            e.wait_ge(dqB.h, self.b0 + self.cpre[j - self.LA + 1])
            e.wait_ge(dqW.h, self.w0 + 2 * self.cpre[j - self.LA + 1])
        r = j % self.LA
        dqV.inc(e.tensor_copy(out=self.ssp[0:2, r, 0:sz], in_=self.srf[:, j % 2, 0:sz]))
        e.wait_ge(dqV.h, dqV.n)
        dqV.inc(e.tensor_tensor(out=self.ssp[32:34, r, 0:sz],
                                in0=self.srf[:, j % 2, 0:sz],
                                in1=self.ssp[0:2, r, 0:sz], op=AL.subtract))
        dqV.inc(e.tensor_scalar(out=self.v8[:, r, 0:sz], in0=self.pk8[:, j % 2, 0:sz],
                                scalar1=15, scalar2=None, op0=AL.bitwise_and))
        e.wait_ge(dqV.h, dqV.n)
        e.tensor_scalar(out=self.tb8[:, r, 0:sz], in0=self.v8[:, r, 0:sz],
                        scalar1=8, scalar2=None, op0=AL.bitwise_xor)
        dqV.inc(e.tensor_tensor(out=self.tt8[:, r, 0:sz], in0=self.pk8[:, j % 2, 0:sz],
                                in1=self.v8[:, r, 0:sz], op=AL.subtract))

    def emit_ve_slot(self, e, t):
        st, dqV, dqB, dqW = self.st, self.dqV, self.dqB, self.dqW
        for j in self.slot_jobs[t]:
            r = j % self.LA
            for k, (x, s2) in enumerate(self.jchunks[j]):
                cg = self.cpre[j] + k
                if k == 0:
                    e.wait_ge(dqV.h, self.v0 + 4 * (j + 1))
                e.wait_ge(dqB.h, self.b0 + cg + 1)
                if cg >= self.WR:
                    e.wait_ge(st.apool[cg % self.WR].h,
                              self.ap[cg % self.WR]
                              + 16 * ((cg - self.WR) // self.WR + 1))
                dqW.inc(e.tensor_tensor(out=self.wrow[0:64, cg % self.WR, 0:s2],
                                        in0=self.tt8[:, r, x:x + s2],
                                        in1=self.pSC[0:64, 0:s2], op=AL.mult))
                dqW.inc(e.scalar_tensor_tensor(
                    out=self.wrow[64:128, cg % self.WR, 0:s2],
                    in0=self.tb8[:, r, x:x + s2], scalar=-8.0,
                    in1=self.pSC[64:128, 0:s2],
                    op0=AL.add, op1=AL.mult))
            if j + self.LA < self.NJ:
                self.emit_ve_prep(e, j + self.LA)

    def emit_pe_chunk(self, e, ent):
        (j, k, cg, x, s2) = ent
        e.wait_ge(self.dqV.h, self.v0 + 4 * (j + 1))
        if cg >= 1:
            e.wait_ge(self.dqW.h, self.w0 + 2 * cg)
        self.dqB.inc(e.matmul(self.pSC[:, 0:s2], self.st.e34b[:, :],
                              self.ssp[:, j % self.LA, x:x + s2],
                              start=True, stop=True))

    def emit_act_chunk(self, e, ent):
        (j, k, cg, x, s2) = ent
        (wp, sr, WT, ib, o, sz) = self.jobs[j]
        e.wait_ge(self.dqW.h, self.w0 + 2 * (cg + 1))
        self.st.apool[cg % self.WR].inc(e.dma_start(
            out=WT[ib * 128:(ib + 1) * 128, o + x:o + x + s2],
            in_=self.wrow[:, cg % self.WR, 0:s2]), 16)

    def emit_sync_all(self, e):
        st = self.st
        for m, (wp, sr, WT, ib, o, sz) in enumerate(self.jobs):
            if m >= 2:
                e.wait_ge(self.dqV.h, self.v0 + 4 * (m - 1))
            st.spool[2 + m % 2].inc(e.dma_start(
                out=self.pk8[:, m % 2, 0:sz], in_=wp[ib, :, o:o + sz]), 16)
            st.spool[2 + m % 2].inc(e.dma_start(
                out=self.srf[:, m % 2, 0:sz], in_=sr[ib, :, o:o + sz]), 16)


def phase_att(st, with_deq=False):
    c, nc, blk = st.cfg, st.nc, st.blk
    dS, dG = st.sems["dS"], st.sems["dG"]
    (nqM, sqN, rk1, rowQ, sMM, eXP, aP4, rcV, bcM, sBC, vOU,
     hSQ) = _mk_sems(st, ["atnq", "atsq", "atr1", "atrow", "atsmm", "atexp",
                          "atap4", "atrcv", "atbcm",
                          "atsbc", "atvou", "athsq"])
    NQC, SKB, QC = c.NQC, c.SKB, c.QC
    e_base = eXP.n
    a_base = aP4.n
    s_base = sMM.n

    def pair_done(gkt):
        return e_base + gkt // 2 + 1

    with contextlib.ExitStack() as ph:
        ksb = ph.enter_context(nc.sbuf_tensor("ksb", [128, 2, c.SK], BF16))
        vsb = ph.enter_context(nc.sbuf_tensor("vsb", [128, 2, SKB, 128], BF16))
        qsb = ph.enter_context(nc.sbuf_tensor("qsb", [128, 2, c.NQ], BF16))
        qrr = ph.enter_context(nc.sbuf_tensor("qrr", [65, 2, c.NQ], BF16))
        qsq = ph.enter_context(nc.sbuf_tensor("qsq", [128, c.NQ], BF16))
        qrsq = ph.enter_context(nc.sbuf_tensor("qrsq", [64, c.NQ], BF16))
        PR = 16
        Pst = ph.enter_context(nc.sbuf_tensor("Pst", [128, PR, QC], BF16))
        sqn = ph.enter_context(nc.sbuf_tensor("sqn", [1, QC], BF16))
        rinv = ph.enter_context(nc.sbuf_tensor("rinv", [1, QC], F32))
        bcst = ph.enter_context(nc.sbuf_tensor("bcst", [128, QC], F32))
        ao = ph.enter_context(nc.sbuf_tensor("ao", [128, 2, c.NQ], BF16))
        pS = ph.enter_context(nc.psum_tensor("pS", [128, 4, 512], F32))
        pAV = ph.enter_context(nc.psum_tensor("pAV", [128, 512], F32))
        pNB = ph.enter_context(nc.psum_tensor("pNB", [128, 512], F32))
        pSm = ph.enter_context(nc.psum_tensor("pSm", [1, 512], F32))
        deq = DeqJobs(st, ph) if with_deq else None

        hb0, hb1 = st.gpool[0].n, st.gpool[1].n
        ab0, ab1 = st.gpool[2].n, st.gpool[3].n

        def hload_done(h):
            return (hb0 if h % 2 == 0 else hb1) + 64 * (h // 2 + 1)

        def aodma_done(j):
            return (ab0 if j % 2 == 0 else ab1) + 16 * (j // 2 + 1)

        @blk.gpsimd
        def _(e):
            for h in range(c.H):
                if h >= 2:
                    e.wait_ge(vOU.h, (h - 2 + 1) * NQC)
                    st.gpool[2 + h % 2].inc(e.dma_start(
                        out=st.AOT[(h - 2) * 128:(h - 1) * 128, :],
                        in_=ao[:, h % 2, :]), 16)
                if h >= 2:
                    e.wait_ge(aP4.h, a_base + (SKB // 4) * (h - 1) * NQC)
                hp = st.gpool[h % 2]
                hp.inc(e.dma_start(out=ksb[:, h % 2, :],
                                   in_=st.KT[h * 128:(h + 1) * 128, :]), 16)
                hp.inc(e.dma_start(out=vsb[:, h % 2, :, :],
                                   in_=st.VS[:, h * 128:(h + 1) * 128].rearrange(
                                       "(kt p) d -> p kt d", p=128)), 16)
                hp.inc(e.dma_start(out=qsb[:, h % 2, :],
                                   in_=st.QT[h * 128:(h + 1) * 128, :]), 16)
                hp.inc(e.dma_start(out=qrr[0:64, h % 2, :],
                                   in_=st.QRT[h * c.DR:(h + 1) * c.DR, :]), 16)
            for h in range(max(c.H - 2, 0), c.H):
                e.wait_ge(vOU.h, (h + 1) * NQC)
                st.gpool[2 + h % 2].inc(e.dma_start(
                    out=st.AOT[h * 128:(h + 1) * 128, :],
                    in_=ao[:, h % 2, :]), 16)

        @blk.vector
        def _(e):
            for h in range(c.H):
                e.wait_ge(st.gpool[h % 2].h, hload_done(h))
                if h >= 1:
                    e.wait_ge(nqM.h, h * NQC)
                hSQ.inc(e.tensor_tensor(out=qsq[:, :], in0=qsb[:, h % 2, :],
                                        in1=qsb[:, h % 2, :], op=AL.mult))
                hSQ.inc(e.tensor_tensor(out=qrsq[:, :], in0=qrr[0:64, h % 2, :],
                                        in1=qrr[0:64, h % 2, :], op=AL.mult))
                if h == 0 and deq is not None:
                    deq.emit_ve_memsets(e)
                    for j0 in range(min(deq.LA, deq.NJ)):
                        deq.emit_ve_prep(e, j0)
                for qc in range(NQC):
                    t = h * NQC + qc
                    qsl = slice(qc * QC, (qc + 1) * QC)
                    if qc == 0 and h >= 2:
                        e.wait_ge(st.gpool[2 + h % 2].h, aodma_done(h - 2))
                    if deq is not None:
                        deq.emit_ve_slot(e, t)
                    e.wait_ge(aP4.h, a_base + (SKB // 4) * (t + 1))
                    rcV.inc(e.reciprocal(out=rinv[:, :], in_=pSm[:, 0:QC]))
                    e.wait_ge(sBC.h, t + 1)
                    vOU.inc(e.tensor_tensor(out=ao[:, h % 2, qsl], in0=pAV[:, 0:QC],
                                            in1=bcst[:, 0:QC], op=AL.mult))

        @blk.scalar
        def _(e):
            for h in range(c.H):
                for qc in range(NQC):
                    t = h * NQC + qc
                    qsl = slice(qc * QC, (qc + 1) * QC)
                    e.wait_ge(nqM.h, t + 1)
                    sqN.inc(e.activation(out=sqn[:, :], in_=pNB[0:1, 0:QC],
                                         func=AF.Sqrt, bias=0.0, scale=1.0))
                    e.wait_ge(rk1.h, t + 1)
                    rowQ.inc(e.copy(out=qrr[64:65, h % 2, qsl], in_=pNB[0:1, 0:QC]))
                    actmap = deq.maps_for_slot(t, SKB)[1] if deq is not None else {}
                    for ep in range(SKB // 2):
                        g0 = t * SKB + 2 * ep
                        e.wait_ge(sMM.h, s_base + g0 + 2)
                        if g0 >= PR:
                            e.wait_ge(aP4.h, a_base + (g0 - 15) // 4 + 1)
                        eXP.inc(e.activation(
                            out=Pst[:, g0 % PR:g0 % PR + 2, :],
                            in_=pS[:, (2 * ep) % 4:(2 * ep) % 4 + 2, 0:QC],
                            func=AF.Exp, bias=0.0, scale=1.0))
                        for ent in actmap.get(ep, ()):
                            deq.emit_act_chunk(e, ent)
                    e.wait_ge(bcM.h, t + 1)
                    sBC.inc(e.copy(out=bcst[:, 0:QC], in_=pNB[:, 0:QC]))

        @blk.tensor
        def _(e):
            for h in range(c.H):
                for qc in range(NQC):
                    t = h * NQC + qc
                    qsl = slice(qc * QC, (qc + 1) * QC)
                    e.wait_ge(hSQ.h, 2 * (h + 1))
                    if t >= 1:
                        e.wait_ge(sBC.h, t)
                    e.matmul(pNB[0:1, 0:QC], st.ones128b[:, :], qsq[:, qsl],
                             start=True, stop=False)
                    nqM.inc(e.matmul(pNB[0:1, 0:QC], st.ones128b[0:64, :],
                                     qrsq[:, qsl], start=False, stop=True))
                    e.wait_ge(sqN.h, t + 1)
                    rk1.inc(e.matmul(pNB[0:1, 0:QC], st.negB[:, :], sqn[:, :],
                                     start=True, stop=True))
                    pemap = deq.maps_for_slot(t, SKB)[0] if deq is not None else {}
                    e.wait_ge(rowQ.h, t + 1)
                    for kt in range(SKB):
                        gkt = t * SKB + kt
                        if gkt >= 4:
                            e.wait_ge(eXP.h, pair_done(gkt - 4))
                        e.matmul(pS[:, kt % 4, 0:QC],
                                 ksb[:, h % 2, kt * 128:(kt + 1) * 128],
                                 qsb[:, h % 2, qsl], start=True, stop=False)
                        sMM.inc(e.matmul(pS[:, kt % 4, 0:QC],
                                         st.KRR[:, kt * 128:(kt + 1) * 128],
                                         qrr[:, h % 2, qsl], start=False, stop=True))
                        if kt >= 3:
                            akt = kt - 3
                            gat = t * SKB + akt
                            e.wait_ge(eXP.h, pair_done(gat))
                            if akt == 0:
                                if t >= 1:
                                    e.wait_ge(vOU.h, t)
                                    e.wait_ge(rcV.h, t)
                            e.matmul(pAV[:, 0:QC], vsb[:, h % 2, akt, :],
                                     Pst[:, gat % PR, :], start=(akt == 0),
                                     stop=(akt == SKB - 1))
                            ins = e.matmul(pSm[:, 0:QC], st.ones128b[:, :],
                                           Pst[:, gat % PR, :], start=(akt == 0),
                                           stop=(akt == SKB - 1))
                            if akt % 4 == 3:
                                aP4.inc(ins)
                        for ent in pemap.get(kt, ()):
                            deq.emit_pe_chunk(e, ent)
                    for akt in range(max(SKB - 3, 0), SKB):
                        gat = t * SKB + akt
                        e.wait_ge(eXP.h, pair_done(gat))
                        if akt == 0 and t >= 1:
                            e.wait_ge(vOU.h, t)
                            e.wait_ge(rcV.h, t)
                        e.matmul(pAV[:, 0:QC], vsb[:, h % 2, akt, :],
                                 Pst[:, gat % PR, :], start=(akt == 0),
                                 stop=(akt == SKB - 1))
                        ins = e.matmul(pSm[:, 0:QC], st.ones128b[:, :],
                                       Pst[:, gat % PR, :], start=(akt == 0),
                                       stop=(akt == SKB - 1))
                        if akt % 4 == 3:
                            aP4.inc(ins)
                    e.wait_ge(rcV.h, t + 1)
                    if t >= 1:
                        e.wait_ge(sBC.h, t)
                    bcM.inc(e.matmul(pNB[:, 0:QC], st.ones1f[:, :], rinv[:, :],
                                     start=True, stop=True))

        if deq is not None:
            @blk.sync
            def _(e):
                deq.emit_sync_all(e)

        phase_barrier(st)


def phase_wo(st):
    c, nc, blk = st.cfg, st.nc, st.blk
    dS, dA, dG = st.sems["dS"], st.sems["dA"], st.sems["dG"]
    (wMM, wEV) = _mk_sems(st, ["womm", "woev"])
    NDC = c.D // 512
    NQB = c.NQB

    with contextlib.ExitStack() as ph:
        wo_bf = ph.enter_context(nc.sbuf_tensor("wo_bf", [128, 2, c.FB, 512], BF16))
        aot = ph.enter_context(nc.sbuf_tensor("aot", [128, 2, c.FB, 128], BF16))
        xres = ph.enter_context(nc.sbuf_tensor("xres", [128, 2, 512], F32))
        xm = ph.enter_context(nc.sbuf_tensor("xm", [128, 2, 512], F32))
        pW = ph.enter_context(nc.psum_tensor("pW", [128, 2, 512], F32))

        d_g0, d_s0, d_a0 = dG.n, dS.n, dA.n
        atb = [st.gpool[0].n, st.gpool[1].n]
        xrb = [st.spool[0].n, st.spool[1].n]
        xmb = [st.apool[0].n, st.apool[1].n]

        def wo_done(fc):
            return d_g0 + 16 * (fc + 1)

        def aot_sem(i):
            return st.gpool[i % 2], atb[i % 2] + 16 * (i // 2 + 1)

        def xres_sem(i):
            return st.spool[i % 2], xrb[i % 2] + 16 * (i // 2 + 1)

        def xm_sem(i):
            return st.apool[i % 2], xmb[i % 2] + 16 * (i // 2 + 1)

        @blk.gpsimd
        def _(e):
            for fc in range(NDC):
                if fc >= 2:
                    e.wait_ge(wMM.h, (fc - 1) * NQB)
                dG.inc(e.dma_start(
                    out=wo_bf[:, fc % 2, :, :],
                    in_=st.io["w_out"][:, fc * 512:(fc + 1) * 512].rearrange(
                        "(hd p) f -> p hd f", p=128)), 16)
                for qt in range(NQB):
                    i = fc * NQB + qt
                    if i >= 2:
                        e.wait_ge(wMM.h, i - 1)
                    st.gpool[i % 2].inc(e.dma_start(
                        out=aot[:, qt % 2, :, :],
                        in_=st.AOT[:, qt * 128:(qt + 1) * 128].rearrange(
                            "(hd p) t -> p hd t", p=128)), 16)

        @blk.sync
        def _(e):
            for fc in range(NDC):
                for qt in range(NQB):
                    i = fc * NQB + qt
                    if i >= 2:
                        e.wait_ge(wEV.h, i - 1)
                    st.spool[i % 2].inc(e.dma_start(
                        out=xres[:, i % 2, :],
                        in_=st.io["x"][qt * 128:(qt + 1) * 128,
                                       fc * 512:(fc + 1) * 512]), 16)

        @blk.tensor
        def _(e):
            for fc in range(NDC):
                e.wait_ge(dG.h, wo_done(fc))
                for qt in range(NQB):
                    i = fc * NQB + qt
                    s_, v_ = aot_sem(i)
                    e.wait_ge(s_.h, v_)
                    if i >= 2:
                        e.wait_ge(wEV.h, i - 1)
                    for hd in range(c.FB):
                        e.matmul(pW[:, i % 2, :], aot[:, qt % 2, hd, :],
                                 wo_bf[:, fc % 2, hd, :],
                                 start=(hd == 0), stop=False)
                    wMM.inc(e.matmul(pW[:, i % 2, :], st.ones33b[32:33, :],
                                     st.brow[32:33, fc * 512:(fc + 1) * 512],
                                     start=False, stop=True))

        @blk.vector
        def _(e):
            for fc in range(NDC):
                for qt in range(NQB):
                    i = fc * NQB + qt
                    e.wait_ge(wMM.h, i + 1)
                    s_, v_ = xres_sem(i)
                    e.wait_ge(s_.h, v_)
                    if i >= 2:
                        s2_, v2_ = xm_sem(i - 2)
                        e.wait_ge(s2_.h, v2_)
                    wEV.inc(e.tensor_tensor(out=xm[:, i % 2, :], in0=pW[:, i % 2, :],
                                            in1=xres[:, i % 2, :], op=AL.add))

        @blk.scalar
        def _(e):
            for fc in range(NDC):
                for qt in range(NQB):
                    i = fc * NQB + qt
                    e.wait_ge(wEV.h, i + 1)
                    st.apool[i % 2].inc(e.dma_start(
                        out=st.XMID[qt * 128:(qt + 1) * 128,
                                    fc * 512:(fc + 1) * 512],
                        in_=xm[:, i % 2, :]), 16)

        phase_barrier(st)


def phase_ln2(st):
    c, nc, blk = st.cfg, st.nc, st.blk
    dS, dG = st.sems["dS"], st.sems["dG"]
    (lST, lSQ, lAP, lTP, lTE, vCH) = _mk_sems(
        st, ["l2stat", "l2sqrt", "l2apply", "l2tp", "l2tev", "l2vch"])
    NG = c.D // 512
    GPT = c.FB // 4

    with contextlib.ExitStack() as ph:
        xs = ph.enter_context(nc.sbuf_tensor("xs2", [128, 2, c.D], F32))
        xlb = ph.enter_context(nc.sbuf_tensor("xlb2", [128, 2, c.D], BF16))
        xlT = ph.enter_context(nc.sbuf_tensor("xlT2", [128, 2, c.FB, 128], BF16))
        stat = ph.enter_context(nc.sbuf_tensor("stat2", [128, 8], F32))
        mv6 = ph.enter_context(nc.sbuf_tensor("mv62", [128, NG, 6], F32))
        pT1 = ph.enter_context(nc.psum_tensor("pT2", [128, 2, 512], F32))

        d_s0, d_g0 = dS.n, dG.n

        lb = [st.spool[0].n, st.spool[1].n]
        gb = [st.gpool[0].n, st.gpool[1].n]

        @blk.sync
        def _(e):
            for g in range(c.NQB):
                if g >= 2:
                    e.wait_ge(lAP.h, g - 1)
                st.spool[g % 2].inc(e.dma_start(
                    out=xs[:, g % 2, :],
                    in_=st.XMID[g * 128:(g + 1) * 128, :]), 16)

        @blk.vector
        def _(e):
            for g in range(c.NQB):
                e.wait_ge(st.spool[g % 2].h,
                          lb[g % 2] + 16 * (g // 2 + 1))
                for ng in range(NG):
                    ins = e.bn_stats(out=mv6[:, ng, :],
                                     in_=xs[:, g % 2, ng * 512:(ng + 1) * 512])
                vCH.inc(ins)
                e.wait_ge(vCH.h, vCH.n)
                lST.inc(e.bn_aggr(out=stat[:, 0:2], in_=mv6[:, :, :]))
                e.wait_ge(lSQ.h, g + 1)
                vCH.inc(e.reciprocal(out=stat[:, 3:4], in_=stat[:, 2:3]))
                e.wait_ge(vCH.h, vCH.n)
                vCH.inc(e.tensor_scalar(out=stat[:, 4:5], in0=stat[:, 0:1],
                                        scalar1=stat[:, 3:4], scalar2=-1.0,
                                        op0=AL.mult, op1=AL.mult))
                e.wait_ge(vCH.h, vCH.n)
                if g >= 2:
                    e.wait_ge(lTP.h, (g - 1) * GPT)
                lAP.inc(e.tensor_scalar(out=xlb[:, g % 2, :], in0=xs[:, g % 2, :],
                                        scalar1=stat[:, 4:5], scalar2=stat[:, 3:4],
                                        op0=AL.add, op1=AL.mult))

        @blk.scalar
        def _(e):
            for g in range(c.NQB):
                e.wait_ge(lST.h, g + 1)
                lSQ.inc(e.activation(out=stat[:, 2:3], in_=stat[:, 1:2],
                                     func=AF.Sqrt, bias=st.epsb[:, 0:1], scale=1.0))
                for j in range(GPT):
                    gidx = g * GPT + j
                    e.wait_ge(lTP.h, gidx + 1)
                    if j == 0 and g >= 2:
                        e.wait_ge(st.gpool[g % 2].h,
                                  gb[g % 2] + 16 * ((g - 2) // 2 + 1))
                    lTE.inc(e.copy(
                        out=xlT[:, g % 2, 4 * j:4 * j + 4, :],
                        in_=pT1[:, gidx % 2, :].rearrange("p (b t) -> p b t", b=4)))

        @blk.tensor
        def _(e):
            for g in range(c.NQB):
                e.wait_ge(lAP.h, g + 1)
                for j in range(GPT):
                    gidx = g * GPT + j
                    if gidx >= 2:
                        e.wait_ge(lTE.h, gidx - 1)
                    for q in range(4):
                        fb = 4 * j + q
                        ins = e.matmul(pT1[:, gidx % 2, q * 128:(q + 1) * 128],
                                       xlb[:, g % 2, fb * 128:(fb + 1) * 128],
                                       st.identb[:, :], start=True, stop=True)
                        if q == 3:
                            lTP.inc(ins)

        @blk.gpsimd
        def _(e):
            for g in range(c.NQB):
                e.wait_ge(lTE.h, (g + 1) * GPT)
                st.gpool[g % 2].inc(e.dma_start(
                    out=st.XL2T[:, g * 128:(g + 1) * 128].rearrange(
                        "(fb p) t -> p fb t", p=128),
                    in_=xlT[:, g % 2, :, :]), 16)

        phase_barrier(st)


def _phase_deq(st, tag, wp, sr, WT, IBN, ON):
    c, nc, blk = st.cfg, st.nc, st.blk
    dS, dG = st.sems["dS"], st.sems["dG"]
    (dqB, dqW, dqV, dqM) = _mk_sems(st, [tag + "b", tag + "w", tag + "v",
                                          tag + "m"])
    chunks = []
    o = 0
    while o < ON:
        sz = min(512, ON - o)
        chunks.append((o, sz))
        o += sz
    NCHK = len(chunks)

    with contextlib.ExitStack() as ph:
        pk8 = ph.enter_context(nc.sbuf_tensor(tag + "pk", [64, 2, ON], I8))
        srf = ph.enter_context(nc.sbuf_tensor(tag + "sr", [2, 1, ON], F32))
        ssp = ph.enter_context(nc.sbuf_tensor(tag + "ss", [34, ON], BF16))
        v8 = ph.enter_context(nc.sbuf_tensor(tag + "v8", [64, ON], I8))
        tb8 = ph.enter_context(nc.sbuf_tensor(tag + "tb", [64, ON], I8))
        tt8 = ph.enter_context(nc.sbuf_tensor(tag + "tt", [64, ON], I8))
        wrow = ph.enter_context(nc.sbuf_tensor(tag + "wr", [128, 2, ON], BF16))
        pSC = ph.enter_context(nc.psum_tensor(tag + "ps", [128, 2, 512], F32))

        db_ = [st.spool[0].n, st.spool[1].n]
        gb_deq = [st.gpool[0].n, st.gpool[1].n]

        @blk.sync
        def _(e):
            for ib in range(IBN):
                if ib >= 2:
                    e.wait_ge(dqV.h, 4 * (ib - 1))
                if ib >= 1:
                    e.wait_ge(dqV.h, 4 * (ib - 1) + 2)
                st.spool[ib % 2].inc(e.dma_start(out=pk8[:, ib % 2, :],
                                                 in_=wp[ib, :, :]), 16)
                st.spool[ib % 2].inc(e.dma_start(out=srf[:, 0, :],
                                                 in_=sr[ib, :, :]), 16)

        @blk.vector
        def _(e):
            dqM.inc(e.memset(ssp[:, :], 0.0))
            e.wait_ge(dqM.h, dqM.n)
            for ib in range(IBN):
                e.wait_ge(st.spool[ib % 2].h, db_[ib % 2] + 32 * (ib // 2 + 1))
                if ib >= 1:
                    e.wait_ge(dqB.h, ib * NCHK)
                dqV.inc(e.tensor_copy(out=ssp[0:2, :], in_=srf[:, 0, :]))
                e.wait_ge(dqV.h, dqV.n)
                dqV.inc(e.tensor_tensor(out=ssp[32:34, :], in0=srf[:, 0, :],
                                        in1=ssp[0:2, :], op=AL.subtract))
                if ib >= 1:
                    e.wait_ge(dqW.h, ib * 2 * NCHK)
                dqV.inc(e.tensor_scalar(out=v8[:, :], in0=pk8[:, ib % 2, :],
                                        scalar1=15, scalar2=None, op0=AL.bitwise_and))
                e.wait_ge(dqV.h, dqV.n)
                e.tensor_scalar(out=tb8[:, :], in0=v8[:, :],
                                scalar1=8, scalar2=None, op0=AL.bitwise_xor)
                dqV.inc(e.tensor_tensor(out=tt8[:, :], in0=pk8[:, ib % 2, :],
                                        in1=v8[:, :], op=AL.subtract))
                e.wait_ge(dqV.h, dqV.n)
                for k, (o0, sz) in enumerate(chunks):
                    e.wait_ge(dqB.h, ib * NCHK + k + 1)
                    if ib >= 2 and k == 0:
                        e.wait_ge(st.gpool[ib % 2].h,
                                  gb_deq[ib % 2] + 16 * ((ib - 2) // 2 + 1))
                    dqW.inc(e.tensor_tensor(out=wrow[0:64, ib % 2, o0:o0 + sz],
                                            in0=tt8[:, o0:o0 + sz],
                                            in1=pSC[0:64, k % 2, 0:sz], op=AL.mult))
                    dqW.inc(e.scalar_tensor_tensor(
                        out=wrow[64:128, ib % 2, o0:o0 + sz],
                        in0=tb8[:, o0:o0 + sz], scalar=-8.0,
                        in1=pSC[64:128, k % 2, 0:sz],
                        op0=AL.add, op1=AL.mult))

        @blk.tensor
        def _(e):
            for ib in range(IBN):
                for k, (o0, sz) in enumerate(chunks):
                    i = ib * NCHK + k
                    e.wait_ge(dqV.h, 4 * ib + 2)
                    if i >= 2:
                        e.wait_ge(dqW.h, 2 * (i - 1))
                    dqB.inc(e.matmul(pSC[:, k % 2, 0:sz], st.e34b[:, :],
                                     ssp[:, o0:o0 + sz], start=True, stop=True))

        @blk.gpsimd
        def _(e):
            for ib in range(IBN):
                e.wait_ge(dqW.h, (ib + 1) * 2 * NCHK)
                st.gpool[ib % 2].inc(e.dma_start(
                    out=WT[ib * 128:(ib + 1) * 128, :],
                    in_=wrow[:, ib % 2, :]), 16)

        phase_barrier(st)


def phase_ffn1(st, qc):
    c, nc, blk = st.cfg, st.nc, st.blk
    dS = st.sems["dS"]
    (fAM, fBM, fSI, fHV) = _mk_sems(st, ["ffam", "ffbm", "ffsi", "ffhv"])
    FQC = c.FQC
    hsb = st.hsb

    with contextlib.ExitStack() as ph:
        xl2 = ph.enter_context(nc.sbuf_tensor(f"xl2_{qc}", [128, c.FB, FQC], BF16))
        w13 = ph.enter_context(nc.sbuf_tensor(f"w13_{qc}", [128, 2, 2, c.FB, 128], BF16))
        asil = ph.enter_context(nc.sbuf_tensor(f"asil_{qc}", [128, FQC], BF16))
        pA = ph.enter_context(nc.psum_tensor(f"pA_{qc}", [128, 2, 512], F32))
        pB = ph.enter_context(nc.psum_tensor(f"pBf_{qc}", [128, 2, 512], F32))

        d_s0 = dS.n
        wb13 = [st.spool[0].n, st.spool[1].n]
        am0, bm0, si0, hv0 = fAM.n, fBM.n, fSI.n, fHV.n

        @blk.sync
        def _(e):
            dS.inc(e.dma_start(
                out=xl2[:, :, :],
                in_=st.XL2T[:, qc * FQC:(qc + 1) * FQC].rearrange(
                    "(fb p) t -> p fb t", p=128)), 16)
            for hb in range(c.FHB):
                if hb >= 2:
                    e.wait_ge(fBM.h, bm0 + hb - 1)
                st.spool[hb % 2].inc(e.dma_start(
                    out=w13[:, hb % 2, 0, :, :],
                    in_=st.W1T[:, hb * 128:(hb + 1) * 128].rearrange(
                        "(fb p) h -> p fb h", p=128)), 16)
                st.spool[hb % 2].inc(e.dma_start(
                    out=w13[:, hb % 2, 1, :, :],
                    in_=st.W3T[:, hb * 128:(hb + 1) * 128].rearrange(
                        "(fb p) h -> p fb h", p=128)), 16)

        @blk.tensor
        def _(e):
            e.wait_ge(dS.h, d_s0 + 16)
            for hb in range(c.FHB):
                e.wait_ge(st.spool[hb % 2].h,
                          wb13[hb % 2] + 32 * (hb // 2 + 1))
                if hb >= 2:
                    e.wait_ge(fSI.h, si0 + hb - 1)
                    e.wait_ge(fHV.h, hv0 + hb - 1)
                for fb in range(c.FB):
                    ins = e.matmul(pA[:, hb % 2, 0:FQC], w13[:, hb % 2, 0, fb, :],
                                   xl2[:, fb, :], start=(fb == 0),
                                   stop=(fb == c.FB - 1))
                fAM.inc(ins)
                for fb in range(c.FB):
                    ins = e.matmul(pB[:, hb % 2, 0:FQC], w13[:, hb % 2, 1, fb, :],
                                   xl2[:, fb, :], start=(fb == 0),
                                   stop=(fb == c.FB - 1))
                fBM.inc(ins)

        @blk.scalar
        def _(e):
            for hb in range(c.FHB):
                e.wait_ge(fAM.h, am0 + hb + 1)
                if hb >= 1:
                    e.wait_ge(fHV.h, hv0 + hb)
                fSI.inc(e.activation(out=asil[:, :], in_=pA[:, hb % 2, 0:FQC],
                                     func=AF.Silu, bias=0.0, scale=1.0))

        @blk.vector
        def _(e):
            for hb in range(c.FHB):
                e.wait_ge(fSI.h, si0 + hb + 1)
                e.wait_ge(fBM.h, bm0 + hb + 1)
                fHV.inc(e.tensor_tensor(out=hsb[:, hb, :], in0=asil[:, :],
                                        in1=pB[:, hb % 2, 0:FQC], op=AL.mult))

        phase_barrier(st)


def phase_ffn2(st, qc):
    c, nc, blk = st.cfg, st.nc, st.blk
    dS, dA, dG = st.sems["dS"], st.sems["dA"], st.sems["dG"]
    (fOM, fOE, fTP, fFIv) = _mk_sems(st, ["ffom", "ffoe", "fftp", "fffiv"])
    FQC = c.FQC
    TPQ = FQC // 128
    JG = c.FB // 4
    hsb = st.hsb

    with contextlib.ExitStack() as ph:
        w2s = ph.enter_context(nc.sbuf_tensor(f"w2s_{qc}", [128, 2, c.FHB, 128], BF16))
        foT = ph.enter_context(nc.sbuf_tensor(f"foT_{qc}", [128, c.FB, FQC], BF16))
        xmi = ph.enter_context(nc.sbuf_tensor(f"xmi_{qc}", [128, c.D], F32))
        osb = ph.enter_context(nc.sbuf_tensor(f"osb_{qc}", [128, 2, 512], F32))
        pO = ph.enter_context(nc.psum_tensor(f"pO_{qc}", [128, 2, 512], F32))
        pTF = ph.enter_context(nc.psum_tensor(f"pTF_{qc}", [128, 2, 512], F32))

        d_a0, d_g0 = dA.n, dG.n
        w2b = [st.apool[0].n, st.apool[1].n]
        ob_out = [st.gpool[0].n, st.gpool[1].n]
        om0, oe0, tp0, fi0 = fOM.n, fOE.n, fTP.n, fFIv.n

        @blk.scalar
        def _(e):
            for i in range(c.FB + 2):
                if i >= 2:
                    e.wait_ge(fOM.h, om0 + i - 1)
                    fOE.inc(e.copy(out=foT[:, i - 2, :],
                                   in_=pO[:, (i - 2) % 2, 0:FQC]))
                if i < c.FB:
                    st.apool[i % 2].inc(e.dma_start(
                        out=w2s[:, i % 2, :, :],
                        in_=st.W2T[:, i * 128:(i + 1) * 128].rearrange(
                            "(hb p) d -> p hb d", p=128)), 16)
            for tq in range(TPQ):
                if tq >= 1:
                    e.wait_ge(fFIv.h, fi0 + tq * JG)
                dA.inc(e.dma_start(
                    out=xmi[:, :],
                    in_=st.XMID[(qc * FQC + tq * 128):(qc * FQC + (tq + 1) * 128),
                                :]), 16)

        @blk.tensor
        def _(e):
            for db in range(c.FB):
                e.wait_ge(st.apool[db % 2].h, w2b[db % 2] + 16 * (db // 2 + 1))
                if db >= 2:
                    e.wait_ge(fOE.h, oe0 + db - 1)
                for hb in range(c.FHB):
                    ins = e.matmul(pO[:, db % 2, 0:FQC], w2s[:, db % 2, hb, :],
                                   hsb[:, hb, :], start=(hb == 0),
                                   stop=(hb == c.FHB - 1))
                fOM.inc(ins)
            for tq in range(TPQ):
                for jg in range(JG):
                    gidx = tq * JG + jg
                    if gidx >= 2:
                        e.wait_ge(fFIv.h, fi0 + gidx - 1)
                    for q in range(4):
                        db = 4 * jg + q
                        e.wait_ge(fOE.h, oe0 + db + 1)
                        ins = e.matmul(pTF[:, gidx % 2, q * 128:(q + 1) * 128],
                                       foT[:, db, tq * 128:(tq + 1) * 128],
                                       st.identb[:, :], start=True, stop=True)
                    fTP.inc(ins)

        @blk.vector
        def _(e):
            for tq in range(TPQ):
                e.wait_ge(dA.h, d_a0 + 16 * (tq + 1))
                for jg in range(JG):
                    gidx = tq * JG + jg
                    e.wait_ge(fTP.h, tp0 + gidx + 1)
                    if gidx >= 2:
                        e.wait_ge(st.gpool[gidx % 2].h,
                                  ob_out[gidx % 2] + 16 * ((gidx - 2) // 2 + 1))
                    fFIv.inc(e.tensor_tensor(
                        out=osb[:, gidx % 2, :],
                        in0=pTF[:, gidx % 2, :],
                        in1=xmi[:, jg * 512:(jg + 1) * 512], op=AL.add))

        @blk.gpsimd
        def _(e):
            for tq in range(TPQ):
                for jg in range(JG):
                    gidx = tq * JG + jg
                    e.wait_ge(fFIv.h, fi0 + gidx + 1)
                    st.gpool[gidx % 2].inc(e.dma_start(
                        out=st.io["out"][(qc * FQC + tq * 128):
                                         (qc * FQC + (tq + 1) * 128),
                                         jg * 512:(jg + 1) * 512],
                        in_=osb[:, gidx % 2, :]), 16)

        phase_barrier(st)


def _alloc_globals(st):
    c, nc = st.cfg, st.nc
    sb = lambda n, s, d: st.es.enter_context(nc.sbuf_tensor(n, s, d))
    st.identf_sb = sb("identf_sb", [128, 128], F32)
    st.identb = sb("identb", [128, 128], BF16)
    st.ones1f = sb("ones1f", [1, 128], F32)
    st.ones32f = sb("ones32f", [32, 1], F32)
    st.ones128f = sb("ones128f", [128, 1], F32)
    st.ones1b = sb("ones1b", [1, 128], BF16)
    st.ones128b = sb("ones128b", [128, 1], BF16)
    st.ones33b = sb("ones33b", [33, 128], BF16)
    st.bjunk = sb("bjunk", [1, 8], F32)
    st.e34b = sb("e34b", [34, 128], BF16)
    st.bT_dq = sb("bT_dq", [128, c.C1B], F32)
    st.bT_dkv = sb("bT_dkv", [128, c.CB], F32)
    st.bT_uk = sb("bT_uk", [128, c.FB], F32)
    st.bT_uq = sb("bT_uq", [128, c.FB], F32)
    st.bT_rq = sb("bT_rq", [128, c.RB], F32)
    st.bT_rk = sb("bT_rk", [64, 1], F32)
    st.brow = sb("brow", [33, c.D], BF16)
    st.epsb = sb("epsb", [128, 1], F32)
    st.negB = sb("negB", [1, 1], BF16)


def build(cfg: Cfg):
    import ml_dtypes
    c = cfg
    nc = bass.Bass("TRN2", target_bir_lowering=False, debug=False)
    st = St()
    st.cfg, st.nc = c, nc
    st.es = contextlib.ExitStack()
    st.sems = {}

    io = {}
    io["x"] = nc.dram_tensor("x", [c.SK, c.D], F32, kind="ExternalInput").ap()
    for nm, shp in [("w_dq", [c.D, c.DC1]), ("w_uq", [c.DC1, c.D]),
                    ("w_dkv", [c.D, c.DC]), ("w_uk", [c.DC, c.D]),
                    ("w_uv", [c.DC, c.D]), ("w_rq", [c.DC1, c.H * c.DR]),
                    ("w_rk", [c.D, c.DR]), ("w_out", [c.D, c.D]),
                    ("b_dq", [c.DC1]), ("b_uq", [c.D]), ("b_dkv", [c.DC]),
                    ("b_uk", [c.D]), ("b_uv", [c.D]), ("b_rq", [c.H * c.DR]),
                    ("b_rk", [c.DR]), ("b_out", [c.D])]:
        io[nm] = nc.dram_tensor(nm, shp, F32, kind="ExternalInput").ap()
    io["w1p"] = nc.dram_tensor("w1p", [c.FB, 64, c.FH], I8, kind="ExternalInput").ap()
    io["w3p"] = nc.dram_tensor("w3p", [c.FB, 64, c.FH], I8, kind="ExternalInput").ap()
    io["w2p"] = nc.dram_tensor("w2p", [c.FHB, 64, c.D], I8, kind="ExternalInput").ap()
    io["s1r"] = nc.dram_tensor("s1r", [c.FB, 2, c.FH], F32, kind="ExternalInput").ap()
    io["s3r"] = nc.dram_tensor("s3r", [c.FB, 2, c.FH], F32, kind="ExternalInput").ap()
    io["s2r"] = nc.dram_tensor("s2r", [c.FHB, 2, c.D], F32, kind="ExternalInput").ap()
    io["out"] = nc.dram_tensor("out", [c.NQ, c.D], F32, kind="ExternalOutput").ap()
    st.io = io

    scratch_kind = "ExternalOutput" if c.debug else "Internal"
    st.XLT = nc.dram_tensor("XLTs", [c.D, c.NQ], BF16, kind=scratch_kind).ap()
    st.KT = nc.dram_tensor("KTs", [c.D, c.SK], BF16, kind=scratch_kind).ap()
    st.VS = nc.dram_tensor("VSs", [c.SK, c.D], BF16, kind=scratch_kind).ap()
    st.CQT = nc.dram_tensor("CQTs", [c.DC1, c.NQ], BF16, kind=scratch_kind).ap()
    st.QT = nc.dram_tensor("QTs", [c.D, c.NQ], BF16, kind=scratch_kind).ap()
    st.QRT = nc.dram_tensor("QRTs", [c.H * c.DR, c.NQ], BF16, kind=scratch_kind).ap()
    st.AOT = nc.dram_tensor("AOTs", [c.D, c.NQ], BF16, kind=scratch_kind).ap()
    st.XMID = nc.dram_tensor("XMIDs", [c.NQ, c.D], F32, kind=scratch_kind).ap()
    st.XL2T = nc.dram_tensor("XL2Ts", [c.D, c.NQ], BF16, kind=scratch_kind).ap()
    st.W1T = nc.dram_tensor("W1Ts", [c.D, c.FH], BF16, kind=scratch_kind).ap()
    st.W3T = nc.dram_tensor("W3Ts", [c.D, c.FH], BF16, kind=scratch_kind).ap()
    st.W2T = nc.dram_tensor("W2Ts", [c.FH, c.D], BF16, kind=scratch_kind).ap()

    bf = ml_dtypes.bfloat16
    st.ident_f = nc.inline_tensor(np.eye(128, dtype=np.float32), "identft")
    st.ident_b = nc.inline_tensor(np.eye(128, dtype=bf), "identbt")
    st.ones_col_f = nc.inline_tensor(np.ones((1, 128), np.float32), "ones1ft")
    st.ones_col_b = nc.inline_tensor(np.ones((1, 128), bf), "ones1bt")
    st.ones128_b = nc.inline_tensor(np.ones((128, 1), bf), "ones128bt")
    o33 = np.zeros((33, 128), dtype=np.float32)
    o33[0, :] = 1.0
    o33[32, :] = 1.0
    st.ones33_b = nc.inline_tensor(o33.astype(bf), "ones33bt")
    st.ones32_f = nc.inline_tensor(np.ones((32, 1), np.float32), "ones32ft")
    st.ones128_f = nc.inline_tensor(np.ones((128, 1), np.float32), "ones128ft")
    e34 = np.zeros((34, 128), dtype=np.float32)
    e34[0, 0:64] = 1.0 / 16.0
    e34[1, 64:128] = 1.0
    e34[32, 0:64] = 1.0 / 16.0
    e34[33, 64:128] = 1.0
    st.e34_t = nc.inline_tensor(e34.astype(bf), "e34t")

    _alloc_globals(st)
    _mk_sems(st, ["dS", "dA", "dG", "bar"])
    st.spool = _mk_sems(st, ["qs0", "qs1", "qs2", "qs3"])
    st.apool = _mk_sems(st, ["qa0", "qa1", "qa2", "qa3", "qa4", "qa5", "qa6", "qa7"])
    st.gpool = _mk_sems(st, ["qg0", "qg1", "qg2", "qg3", "qg4", "qg5"])
    st.blk = st.es.enter_context(nc.Block())

    phase_p0(st)
    st.krr_es = contextlib.ExitStack()
    st.KRR = st.krr_es.enter_context(nc.sbuf_tensor("KRR", [65, c.SK], BF16))
    st.ckvt_es = contextlib.ExitStack()
    st.CKVT = st.ckvt_es.enter_context(nc.sbuf_tensor("CKVT", [128, c.CB, c.SK], BF16))
    if c.n_phases >= 1:
        phase_a1(st)
    if c.n_phases >= 2:
        phase_a2(st)
    st.ckvt_es.close()
    if c.n_phases >= 3:
        phase_a2b(st)
    if c.n_phases >= 4:
        _phase_a3_generic(st, "a3q", "w_uq", c.FB, st.bT_uq, st.QT, c.SCALER)
        _phase_a3_generic(st, "a3r", "w_rq", c.RB, st.bT_rq, st.QRT, c.SCALER)
    merged_deq = c.merge_deq and c.n_phases >= 9
    if c.n_phases >= 5:
        phase_att(st, with_deq=merged_deq)
    st.krr_es.close()
    if c.n_phases >= 6:
        phase_wo(st)
    if c.n_phases >= 7:
        phase_ln2(st)
    if not merged_deq:
        if c.n_phases >= 8:
            _phase_deq(st, "dq1", st.io["w1p"], st.io["s1r"], st.W1T, c.FB, c.FH)
        if c.n_phases >= 8.3:
            _phase_deq(st, "dq3", st.io["w3p"], st.io["s3r"], st.W3T, c.FB, c.FH)
        if c.n_phases >= 8.6:
            _phase_deq(st, "dq2", st.io["w2p"], st.io["s2r"], st.W2T, c.FHB, c.D)
    if c.n_phases >= 9:
        st.hsb_es = contextlib.ExitStack()
        st.hsb = st.hsb_es.enter_context(
            nc.sbuf_tensor("hsb", [128, c.FHB, c.FQC], BF16))
        for qc in range(c.NQ // c.FQC):
            phase_ffn1(st, qc)
            phase_ffn2(st, qc)
        st.hsb_es.close()

    st.es.close()
    return nc


def _prep_core_inputs(core, x_full, w):
    b, qb = core // 4, core % 4
    SK = x_full.shape[1]
    NQ = SK // 4
    xb = x_full[b]
    qs, qe = qb * NQ, (qb + 1) * NQ
    x_core = np.concatenate([xb[qs:qe], xb[:qs], xb[qe:]], axis=0)
    d = {"x": np.ascontiguousarray(x_core)}
    d.update(w)
    return d


def _pack_ffn(w_packed_i32, scales, out_feat, in_feat):
    R = w_packed_i32.shape[0]
    assert w_packed_i32.shape == (out_feat * in_feat // 128, 64)
    w8 = w_packed_i32.view(np.int8).reshape(R, 64, 4)[:, :, 0]
    IB = in_feat // 128
    wp = np.ascontiguousarray(
        w8.reshape(out_feat, IB, 64).transpose(1, 2, 0))
    sr = np.ascontiguousarray(
        scales.reshape(out_feat, IB, 2).transpose(1, 2, 0))
    return wp, sr


_FULL = Cfg()


def kernel(**inputs):
    c = _FULL
    x = inputs["L0_attn_sequence"]
    w = {}
    for nm in ["w_dq", "b_dq", "w_uq", "b_uq", "w_dkv", "b_dkv", "w_uk", "b_uk",
               "w_uv", "b_uv", "w_rq", "b_rq", "w_rk", "b_rk", "w_out", "b_out"]:
        w[nm] = np.ascontiguousarray(np.asarray(inputs["L0_attn_" + nm], np.float32))
    w["w1p"], w["s1r"] = _pack_ffn(np.asarray(inputs["L0_ffn_w1"]),
                                   np.asarray(inputs["L0_ffn_s1"], np.float32),
                                   c.FH, c.D)
    w["w3p"], w["s3r"] = _pack_ffn(np.asarray(inputs["L0_ffn_w3"]),
                                   np.asarray(inputs["L0_ffn_s3"], np.float32),
                                   c.FH, c.D)
    w["w2p"], w["s2r"] = _pack_ffn(np.asarray(inputs["L0_ffn_w2"]),
                                   np.asarray(inputs["L0_ffn_s2"], np.float32),
                                   c.D, c.FH)
    x = np.asarray(x, np.float32)

    nc = build(c)
    in_maps = [_prep_core_inputs(core, x, w) for core in range(8)]
    res = run_bass_kernel_spmd(nc, in_maps, core_ids=list(range(8)))

    out = np.empty((2, 8192, 4096), np.float32)
    for core in range(8):
        b, qb = core // 4, core % 4
        out[b, qb * 2048:(qb + 1) * 2048] = res.results[core]["out"]
    return out
